# version 06
# baseline (speedup 1.0000x reference)
import math
from collections import defaultdict
from contextlib import ExitStack

import numpy as np
import concourse.bass as bass
import concourse.mybir as mybir
from concourse.bass_utils import run_bass_kernel_spmd

F32 = mybir.dt.float32
BF16 = mybir.dt.bfloat16
AF = mybir.ActivationFunctionType
ALU = mybir.AluOpType
AX = mybir.AxisListType

T = 4096
DM = 2048
H = 16
D = 128
FC = DM // 128
E = 16
I = 768
IC = I // 128
TOPK = 4
SCALE = D ** -0.5
EPS_RMS = 1e-6
EPS_LN = 1e-5
SEQ_SHIFTS = [0, 1, -1, 3, -3, 7, -7, 20, -20, 53, -53, 141, -141, 380, -380, 1024]
CH_SHIFTS = [0, 1, 2, 4, 8]
SEGS = [-3, -2, -1, 0, 1, 2, 3, 8]
NSEG = len(SEGS)
SEG_DIAGS = []
for _d in SEGS:
    SEG_DIAGS.append(sorted(s - _d * 128 for s in SEQ_SHIFTS
                            if -128 < s - _d * 128 < 128))
NT = T // 128


class Buf:
    __slots__ = ("name", "writer", "readers")

    def __init__(self, name):
        self.name = name
        self.writer = None
        self.readers = []


class Prog:
    ENGINES = ("pe", "act", "dve", "pool", "sp")

    def __init__(self, nc, es):
        self.nc = nc
        self.es = es
        self.sem = {}
        self.count = defaultdict(int)
        self.hwm = defaultdict(int)
        self.queue = {e: [] for e in self.ENGINES}
        self.nops = 0
        for e in self.ENGINES:
            self.sem[e] = es.enter_context(nc.semaphore(f"sem_{e}"))

    def buf(self, name):
        return Buf(name)

    def _dma_sem(self, key):
        skey = "dma_" + key
        if skey not in self.sem:
            self.sem[skey] = self.es.enter_context(
                self.nc.semaphore(f"sem_dma_{key}"))
        return skey

    def _wait(self, waiter_eng, src_stream, val):
        if val <= 0:
            return
        if src_stream.startswith("dma_"):
            val = self.count[src_stream]
        k = (waiter_eng, src_stream)
        if self.hwm[k] >= val:
            return
        self.hwm[k] = val
        sem = self.sem[src_stream]
        tgt = val * (16 if src_stream.startswith("dma_") else 1)
        self.queue[waiter_eng].append(lambda e: e.wait_ge(sem, tgt))

    def _pre(self, eng, reads, writes):
        need = {}
        bar = getattr(self, "barrier_counts", None)
        if bar:
            for s, c in bar.items():
                if c > need.get(s, 0):
                    need[s] = c
        for b in reads:
            if b.writer is not None:
                s, c = b.writer
                if c > need.get(s, 0):
                    need[s] = c
        for b in writes:
            if b.writer is not None:
                s, c = b.writer
                if c > need.get(s, 0):
                    need[s] = c
            for (s, c) in b.readers:
                if c > need.get(s, 0):
                    need[s] = c
        for s, c in need.items():
            self._wait(eng, s, c)

    def barrier(self):
        self.barrier_counts = dict(self.count)

    def _post(self, stream, reads, writes):
        self.count[stream] += 1
        c = self.count[stream]
        for b in writes:
            b.writer = (stream, c)
            b.readers = []
        for b in reads:
            b.readers.append((stream, c))
        return c

    def op(self, eng, fn, reads=(), writes=()):
        self._pre(eng, reads, writes)
        sem = self.sem[eng]
        self.queue[eng].append(lambda e: fn(e).then_inc(sem, 1))
        self.nops += 1
        return self._post(eng, reads, writes)

    def dma(self, queue, out=None, in_=None, reads=(), writes=(), key=None, **kw):
        if key is None:
            key = writes[0].name if writes else (reads[0].name if reads else queue)
        stream = self._dma_sem(key)
        self._pre(queue, reads, writes)
        sem = self.sem[stream]
        o, i = out, in_
        self.queue[queue].append(
            lambda e: e.dma_start(out=o, in_=i, **kw).then_inc(sem, 16))
        self.nops += 1
        return self._post(stream, reads, writes)

    def dma_transpose(self, queue, out, in_, reads=(), writes=(), key=None):
        if key is None:
            key = writes[0].name
        stream = self._dma_sem(key)
        self._pre(queue, reads, writes)
        sem = self.sem[stream]
        self.queue[queue].append(
            lambda e: e.dma_start(out=out, in_=in_, transpose=True).then_inc(sem, 16))
        self.nops += 1
        return self._post(stream, reads, writes)

    def finish(self, eng="pool"):
        for s, c in list(self.count.items()):
            if s != eng and c > 0:
                self._wait(eng, s, c)

    def emit(self):
        with self.nc.Block() as block:
            for eng_name, starter in (("pe", block.tensor), ("act", block.scalar),
                                      ("dve", block.vector), ("pool", block.gpsimd),
                                      ("sp", block.sync)):
                q = self.queue[eng_name]
                if q:
                    def body(e, q=q):
                        for fn in q:
                            fn(e)
                    starter(body)


def circ_runs(a, ln, period=T):
    a %= period
    out = []
    while ln > 0:
        take = min(ln, period - a)
        out.append((a, take))
        ln -= take
        a = 0
    return out


def build_program(dev_taps=()):
    es = ExitStack()
    nc = bass.Bass("TRN2", target_bir_lowering=False, debug=False)
    p = Prog(nc, es)

    kind_scr = "ExternalOutput" if "scratch" in dev_taps else "Internal"

    DT = nc.dram_tensor
    xT = DT("xT", [DM, T], F32, kind="ExternalInput").ap()
    wqm = DT("wqm", [H, FC, 128, 128], F32, kind="ExternalInput").ap()
    wkm = DT("wkm", [H, FC, 128, 128], F32, kind="ExternalInput").ap()
    wvdc = DT("wvdc", [4, FC, 128, 512], F32, kind="ExternalInput").ap()
    wom = DT("wom", [H, FC, 128, 128], F32, kind="ExternalInput").ap()
    bqr = DT("bqr", [1, DM], F32, kind="ExternalInput").ap()
    bkr = DT("bkr", [1, DM], F32, kind="ExternalInput").ap()
    boc = DT("boc", [128, FC], F32, kind="ExternalInput").ap()
    gqc = DT("gqc", [128, FC], F32, kind="ExternalInput").ap()
    gkc = DT("gkc", [128, FC], F32, kind="ExternalInput").ap()
    bvr = DT("bvr", [1, DM], F32, kind="ExternalInput").ap()
    wmix = DT("wmix", [1, 5], F32, kind="ExternalInput").ap()
    rwT = DT("rwT", [DM, E], F32, kind="ExternalInput").ap()
    ebr = DT("ebr", [1, E], F32, kind="ExternalInput").ap()
    gup = DT("gup", [E, FC, 128, 2 * I], F32, kind="ExternalInput").ap()
    sgu = DT("sgu", [FC, 128, 2 * I], F32, kind="ExternalInput").ap()
    dallm = DT("dallm", [H, 128, 102, 128], F32, kind="ExternalInput").ap()

    out = DT("out", [DM, T], F32, kind="ExternalOutput").ap()

    qpb_s = DT("qpb_s", [DM, T], BF16, kind=kind_scr).ap()
    kpb_s = DT("kpb_s", [DM, T], BF16, kind=kind_scr).ap()
    vtm_s = DT("vtm_s", [H, T, 128], BF16, kind=kind_scr).ap()
    avtm_s = DT("avtm_s", [T, DM], BF16, kind=kind_scr).ap()
    r1_s = DT("r1_s", [DM, T], F32, kind="Internal").ap()
    r1b_s = DT("r1b_s", [DM, T], BF16, kind="Internal").ap()
    x1_s = DT("x1_s", [DM, T], F32, kind=kind_scr).ap()
    x1b_s = DT("x1b_s", [DM, T], BF16, kind="Internal").ap()
    mid_s = DT("mid_s", [17, I, T], BF16, kind=kind_scr).ap()
    dbf_s = DT("dbf_s", [H, 128, 102 * 128], BF16, kind="Internal").ap()
    r2_s = DT("r2_s", [DM, T], F32, kind="Internal").ap()
    r2b_s = DT("r2b_s", [DM, T], BF16, kind="Internal").ap()

    sb = lambda n, s, d=F32: es.enter_context(nc.sbuf_tensor(n, s, d))
    pb = [es.enter_context(nc.psum_tensor(f"pb{i}", [128, 512], F32))
          for i in range(8)]
    b_pb = [p.buf(f"pb{i}") for i in range(8)]

    B = p.buf
    ones_row = sb("ones_row", [1, 128], BF16)
    ones_r512 = sb("ones_r512", [1, 512], BF16)
    ones_col = sb("ones_col", [128, 1], BF16)
    ident = sb("ident", [128, 128], BF16)
    identf = sb("identf", [128, 128], F32)
    stencil = sb("stencil", [128, 128], BF16)
    amask = sb("amask", [128, NSEG * 128], BF16)
    bq_r = sb("bq_r", [1, DM], BF16)
    bk_r = sb("bk_r", [1, DM], BF16)
    bo_c = sb("bo_c", [128, FC], F32)
    gq_c = sb("gq_c", [128, FC], F32)
    gk_c = sb("gk_c", [128, FC], F32)
    wmix_r = sb("wmix_r", [1, 8], F32)
    wmix_c = sb("wmix_c", [128, 8], F32)
    eb_r = sb("eb_r", [1, E], BF16)
    eb_b = sb("eb_b", [128, E], F32)
    b_ones_row, b_ones_col, b_ident = B("ones_row"), B("ones_col"), B("ident")
    b_identf = B("identf")
    b_ones_r512 = B("ones_r512")
    b_stencil = B("stencil")
    b_amask = B("amask")
    b_bq_r, b_bk_r, b_bo_c, b_gq_c, b_gk_c = B("bq_r"), B("bk_r"), B("bo_c"), B("gq_c"), B("gk_c")
    b_wmix_r, b_wmix_c, b_eb_r, b_eb_b = B("wmix_r"), B("wmix_c"), B("eb_r"), B("eb_b")

    b_qpb = B("qpb_s"); b_kpb = B("kpb_s"); b_vtm = B("vtm_s")
    b_avtm = B("avtm_s"); b_r1 = B("r1_s"); b_r1b = B("r1b_s")
    b_x1 = B("x1_s"); b_x1b = B("x1b_s"); b_mid = B("mid_s")
    b_dbf = B("dbf_s"); b_r2 = B("r2_s"); b_r2b = B("r2b_s"); b_out = B("out")

    constcols = sb("constcols", [128, 8], F32)
    b_constcols = B("constcols")
    _ccn = [0]
    def register_const(v):
        i = _ccn[0]
        _ccn[0] += 1
        p.op("dve", lambda e, i=i, v=v: e.memset(constcols[:, i:i + 1], float(v)),
             writes=[b_constcols])
        nc.const_aps.register(F32, float(v), constcols[:, i:i + 1])
    for _v in (EPS_RMS, EPS_LN, 1e-20):
        register_const(_v)

    rstd_rb = sb("rstd_rb", [1, 2 * T], BF16)
    b_rstd_rb = B("rstd_rb")
    ROW_Q, ROW_K = 0, 1
    es0 = ExitStack()
    sb0 = lambda n, s, d=F32: es0.enter_context(nc.sbuf_tensor(n, s, d))
    iotaD = sb0("iotaD", [128, 128], F32)
    stenf = sb0("stenf", [128, 128], F32)
    amaskf = sb0("amaskf", [128, NSEG * 128], F32)
    b_iotaD, b_stenf, b_amaskf = B("iotaD"), B("stenf"), B("amaskf")

    p.op("dve", lambda e: e.memset(ones_row[:], 1.0), writes=[b_ones_row])
    p.op("dve", lambda e: e.memset(ones_r512[:], 1.0), writes=[b_ones_r512])
    p.op("dve", lambda e: e.memset(ones_col[:], 1.0), writes=[b_ones_col])
    p.op("pool", lambda e: e.iota(iotaD[:], pattern=[[1, 128]], base=0,
                                  channel_multiplier=-1,
                                  allow_small_or_imprecise_dtypes=True),
         writes=[b_iotaD])
    p.op("dve", lambda e: e.tensor_single_scalar(identf[:], iotaD[:], 0.0, ALU.is_equal),
         reads=[b_iotaD], writes=[b_identf])
    p.op("dve", lambda e: e.tensor_copy(ident[:], identf[:]),
         reads=[b_identf], writes=[b_ident])
    for name, srcap, dst, bdst in (("boc", boc, bo_c, b_bo_c), ("gqc", gqc, gq_c, b_gq_c),
                                   ("gkc", gkc, gk_c, b_gk_c)):
        p.dma("sp", out=dst[:], in_=srcap[:, :], writes=[bdst])
    p.dma("pool", out=bq_r[:], in_=bqr[:, :], writes=[b_bq_r])
    p.dma("pool", out=bk_r[:], in_=bkr[:, :], writes=[b_bk_r])

    p.dma("sp", out=wmix_r[0:1, 0:5], in_=wmix[:, :], writes=[b_wmix_r])
    p.dma("pool", out=eb_r[:], in_=ebr[:, :], writes=[b_eb_r])

    wmix_rb = sb0("wmix_rb", [1, 8], BF16)
    b_wmix_rb = B("wmix_rb")
    p.op("dve", lambda e: e.memset(wmix_rb[:], 0.0), writes=[b_wmix_rb])
    p.op("dve", lambda e: e.tensor_copy(wmix_rb[0:1, 0:5], wmix_r[0:1, 0:5]),
         reads=[b_wmix_r], writes=[b_wmix_rb])
    p.op("pe", lambda e: e.matmul(pb[6][:, 0:8], ones_row[0:1, :], wmix_rb[0:1, :],
                                  start=True, stop=True),
         reads=[b_ones_row, b_wmix_rb], writes=[b_pb[6]])
    p.op("dve", lambda e: e.tensor_copy(wmix_c[:], pb[6][:, 0:8]),
         reads=[b_pb[6]], writes=[b_wmix_c])
    p.op("pe", lambda e: e.matmul(pb[6][:, 0:E], ones_row[0:1, :], eb_r[0:1, :],
                                  start=True, stop=True),
         reads=[b_ones_row, b_eb_r], writes=[b_pb[6]])
    p.op("dve", lambda e: e.tensor_copy(eb_b[:], pb[6][:, 0:E]),
         reads=[b_pb[6]], writes=[b_eb_b])

    p.op("dve", lambda e: e.memset(stenf[:], 0.0), writes=[b_stenf])
    mtmp = sb0("mtmp", [128, 128], F32)
    mtmp2 = sb0("mtmp2", [128, 128], F32)
    b_mtmp, b_mtmp2 = B("mtmp"), B("mtmp2")
    for ci, c in enumerate(CH_SHIFTS):
        p.op("dve", lambda e, c=c: e.tensor_single_scalar(mtmp[:], iotaD[:], float(c), ALU.is_equal),
             reads=[b_iotaD], writes=[b_mtmp])
        p.op("dve", lambda e, c=c: e.tensor_single_scalar(mtmp2[:], iotaD[:], float(c - 128), ALU.is_equal),
             reads=[b_iotaD], writes=[b_mtmp2])
        p.op("dve", lambda e: e.tensor_add(mtmp[:], mtmp[:], mtmp2[:]),
             reads=[b_mtmp, b_mtmp2], writes=[b_mtmp])
        p.op("dve", lambda e, ci=ci: e.scalar_tensor_tensor(
            stenf[:], mtmp[:], wmix_c[:, ci:ci + 1], stenf[:], ALU.mult, ALU.add),
            reads=[b_mtmp, b_wmix_c, b_stenf], writes=[b_stenf])
    p.op("act", lambda e: e.activation(stencil[:], stenf[:], AF.Copy, scale=SCALE),
         reads=[b_stenf], writes=[b_stencil])

    p.op("dve", lambda e: e.memset(amaskf[:], 0.0), writes=[b_amaskf])
    for si in range(NSEG):
        for dlt in SEG_DIAGS[si]:
            p.op("dve", lambda e, si=si, dlt=dlt: e.tensor_single_scalar(
                mtmp[:], iotaD[:], float(dlt), ALU.is_equal),
                reads=[b_iotaD], writes=[b_mtmp])
            p.op("dve", lambda e, si=si: e.tensor_max(
                amaskf[:, si * 128:(si + 1) * 128],
                amaskf[:, si * 128:(si + 1) * 128], mtmp[:]),
                reads=[b_amaskf, b_mtmp], writes=[b_amaskf])
    p.op("dve", lambda e: e.tensor_copy(amask[:], amaskf[:]),
         reads=[b_amaskf], writes=[b_amask])

    es0.close()
    p.barrier()

    es1 = ExitStack()
    sb1 = lambda n, s, d=F32: es1.enter_context(nc.sbuf_tensor(n, s, d))
    rowbuf = sb1("rowbuf", [1, T], F32)
    b_rowbuf = B("rowbuf")
    xh = [sb1(f"xh{c}", [128, T], BF16) for c in range(FC)]
    b_xh = [B(f"xh{c}") for c in range(FC)]
    def row(i):
        return rowbuf[0:1, 0:T]
    def rowslc(i, a, ln):
        return rowbuf[0:1, a:a + ln]
    for c in range(FC):
        p.dma("pool", out=xh[c][:], in_=xT[c * 128:(c + 1) * 128, :],
              writes=[b_xh[c]], key=f"xh{c % 2}")

    wqb = [sb1(f"wqb{i}", [128, FC * 128], BF16) for i in range(2)]
    b_wqb = [B(f"wqb{i}") for i in range(2)]
    wvb = [sb1("wvb0", [128, FC * 512], BF16)]
    b_wvb = [B("wvb0")]
    sqv = [sb1(f"sqv{i}", [128, 512], BF16) for i in range(2)]
    b_sqv = [B(f"sqv{i}") for i in range(2)]
    pj_sb = [sb1(f"pj_sb{i}", [128, 512], BF16) for i in range(4)]
    b_pj = [B(f"pj_sb{i}") for i in range(4)]
    bv_r = sb1("bv_r", [1, DM], BF16)
    b_bv_r = B("bv_r")
    p.dma("pool", out=bv_r[:], in_=bvr[:, :], writes=[b_bv_r])

    def qk_sweep(which):
        wsrc = wqm if which == "q" else wkm
        bias_r = bq_r if which == "q" else bk_r
        b_bias = b_bq_r if which == "q" else b_bk_r
        dst = qpb_s if which == "q" else kpb_s
        b_dst = b_qpb if which == "q" else b_kpb
        rowi = ROW_Q if which == "q" else ROW_K
        for g in range(2):
            for m in range(H):
                s = m % 2
                p.dma("pool", out=wqb[s][:].rearrange("p (fc q) -> p fc q", q=128),
                      in_=wsrc[m].rearrange("fc p q -> p fc q"),
                      writes=[b_wqb[s]], key=f"wqk{s}")
                for tci in range(4):
                    t0 = g * 2048 + tci * 512
                    bank = [0, 1, 2, 7][tci]
                    def proj(e, m=m, s=s, t0=t0, bank=bank):
                        for fc in range(FC):
                            e.matmul(pb[bank][:, :],
                                     wqb[s][:, fc * 128:(fc + 1) * 128],
                                     xh[fc][:, t0:t0 + 512],
                                     start=(fc == 0), stop=False)
                        return e.matmul(pb[bank][:, :],
                                        bias_r[0:1, m * 128:(m + 1) * 128],
                                        ones_r512[0:1, :],
                                        start=False, stop=True)
                    p.op("pe", proj, reads=[b_wqb[s], b_bias, b_ones_r512] + b_xh,
                         writes=[b_pb[bank]])
                    p.op("act", lambda e, bank=bank, tci=tci: e.activation(
                        pj_sb[tci][:], pb[bank][:, :], AF.Copy),
                        reads=[b_pb[bank]], writes=[b_pj[tci]])
                    p.op("act", lambda e, bank=bank, tci=tci: e.activation(
                        sqv[tci % 2][:], pb[bank][:, :], AF.Square),
                        reads=[b_pb[bank]], writes=[b_sqv[tci % 2]])
                    p.dma("sp", out=dst[m * 128:(m + 1) * 128, t0:t0 + 512],
                          in_=pj_sb[tci][:], reads=[b_pj[tci]], writes=[b_dst],
                          key=f"st_{which}{tci}")
                    rb = 3 + tci
                    p.op("pe", lambda e, rb=rb, m=m, tci=tci: e.matmul(
                        pb[rb][0:1, :], ones_col[:, 0:1], sqv[tci % 2][:],
                        start=(m == 0), stop=(m == H - 1)),
                        reads=[b_ones_col, b_sqv[tci % 2]], writes=[b_pb[rb]])
            for tci in range(4):
                t0 = g * 2048 + tci * 512
                rb = 3 + tci
                p.op("act", lambda e, rb=rb, t0=t0, rowi=rowi: e.activation(
                    rowslc(rowi, t0, 512), pb[rb][0:1, :], AF.Copy),
                    reads=[b_pb[rb]], writes=[b_rowbuf])
        p.op("act", lambda e: e.activation(
            row(0), row(0), AF.Ln, scale=1.0 / DM, bias=EPS_RMS),
            reads=[b_rowbuf, b_constcols], writes=[b_rowbuf])
        p.op("act", lambda e: e.activation(
            row(0), row(0), AF.Exp, scale=-0.5),
            reads=[b_rowbuf], writes=[b_rowbuf])
        p.op("act", lambda e, rowi=rowi: e.activation(
            rstd_rb[0:1, rowi * T:(rowi + 1) * T], row(0), AF.Copy),
            reads=[b_rowbuf], writes=[b_rstd_rb])

    def v_sweep():
        for g in range(2):
            for dc in range(4):
                s = 0
                p.dma("pool", out=wvb[s][:].rearrange("p (fc q) -> p fc q", q=512),
                      in_=wvdc[dc].rearrange("fc p q -> p fc q"),
                      writes=[b_wvb[s]], key=f"wv{s}")
                for tt in range(16):
                    t0 = g * 2048 + tt * 128
                    bank = [0, 1, 2, 7][tt % 4]
                    def vproj(e, s=s, t0=t0, dc=dc, bank=bank):
                        last = None
                        for fc in range(FC):
                            last = e.matmul(pb[bank][:, :],
                                            xh[fc][:, t0:t0 + 128],
                                            wvb[s][:, fc * 512:(fc + 1) * 512],
                                            start=(fc == 0), stop=False)
                        return e.matmul(pb[bank][:, :], ones_row[0:1, :],
                                        bv_r[0:1, dc * 512:(dc + 1) * 512],
                                        start=False, stop=True)
                    p.op("pe", vproj,
                         reads=[b_wvb[s], b_ones_row, b_bv_r] + b_xh,
                         writes=[b_pb[bank]])
                    st = tt % 4
                    p.op("act", lambda e, bank=bank, st=st: e.activation(
                        pj_sb[st][:], pb[bank][:, :], AF.Copy),
                        reads=[b_pb[bank]], writes=[b_pj[st]])
                    for hh in range(4):
                        h = dc * 4 + hh
                        p.dma("sp",
                              out=vtm_s[h, t0:t0 + 128, :],
                              in_=pj_sb[st][:, hh * 128:(hh + 1) * 128],
                              reads=[b_pj[st]], writes=[b_vtm], key=f"st_v{st}")

    qk_sweep("q")
    qk_sweep("k")
    v_sweep()



    es1.close()
    p.barrier()
    es2 = ExitStack()
    sb2 = lambda n, s, d=F32: es2.enter_context(nc.sbuf_tensor(n, s, d))

    qpbh = [sb2(f"qpbh{i}", [128, T], BF16) for i in range(2)]
    kpbh = [sb2(f"kpbh{i}", [128, T], BF16) for i in range(2)]
    vtmh = [sb2(f"vtmh{i}", [128, NT * 132], BF16) for i in range(2)]
    b_qpbh = [B(f"qpbh{i}") for i in range(2)]
    b_kpbh = [B(f"kpbh{i}") for i in range(2)]
    b_vtmh = [B(f"vtmh{i}") for i in range(2)]
    for i in range(2):
        p.op("dve", lambda e, i=i: e.memset(vtmh[i][:], 1.0), writes=[b_vtmh[i]])
    qn = sb2("qn", [128, T], BF16); b_qn = B("qn")
    kn = sb2("kn", [128, T], BF16); b_kn = B("kn")
    qm = sb2("qm", [128, T], BF16); b_qm = B("qm")
    emr = sb2("emr", [128, NSEG * 128], BF16); b_emr = B("emr")
    e_all = sb2("e_all", [128, NT * NSEG * 128], BF16); b_e_all = B("e_all")
    av_sb = sb2("av_sb", [128, 2048], BF16); b_av_sb = B("av_sb")
    chain = sb2("chain", [128, 1], F32); b_chain = B("chain")
    brow = sb2("brow", [1, 128], F32); b_brow = B("brow")
    bval = sb2("bval", [1, 1], F32); b_bval = B("bval")
    bvalb = sb2("bvalb", [1, 1], BF16); b_bvalb = B("bvalb")
    negb = sb2("negb", [128, 1], F32); b_negb = B("negb")
    zr16 = sb2("zr16", [128, 16], F32); b_zr16 = B("zr16")
    mcol = sb2("mcol", [128, 1], F32); b_mcol = B("mcol")

    for h in range(H):
        s = h % 2
        p.dma("act", out=qpbh[s][:], in_=qpb_s[h * 128:(h + 1) * 128, :],
              reads=[b_qpb], writes=[b_qpbh[s]])
        p.dma("act", out=kpbh[s][:], in_=kpb_s[h * 128:(h + 1) * 128, :],
              reads=[b_kpb], writes=[b_kpbh[s]])
        p.dma("act", out=vtmh[s][:].rearrange("p (tt q) -> p tt q", q=132)[:, :, 0:128],
              in_=vtm_s[h].rearrange("(tt p) q -> p tt q", p=128),
              reads=[b_vtm], writes=[b_vtmh[s]])
        for which in ("q", "k"):
            srcb = qpbh[s] if which == "q" else kpbh[s]
            b_srcb = b_qpbh[s] if which == "q" else b_kpbh[s]
            dstb = qn if which == "q" else kn
            b_dstb = b_qn if which == "q" else b_kn
            g_c = gq_c if which == "q" else gk_c
            b_g_c = b_gq_c if which == "q" else b_gk_c
            ri = ROW_Q if which == "q" else ROW_K
            for tc in range(8):
                t0 = tc * 512
                p.op("pe", lambda e, ri=ri, t0=t0: e.matmul(
                    pb[6][:, :], ones_row[0:1, :],
                    rstd_rb[0:1, ri * T + t0:ri * T + t0 + 512],
                    start=True, stop=True),
                    reads=[b_ones_row, b_rstd_rb], writes=[b_pb[6]])
                p.op("dve", lambda e, t0=t0, srcb=srcb, dstb=dstb, g_c=g_c, h=h:
                     e.scalar_tensor_tensor(
                         dstb[:, t0:t0 + 512], srcb[:, t0:t0 + 512],
                         g_c[:, h:h + 1], pb[6][:, :], ALU.mult, ALU.mult),
                     reads=[b_srcb, b_g_c, b_pb[6]], writes=[b_dstb])
        for tc in range(8):
            t0 = tc * 512
            p.op("pe", lambda e, t0=t0: e.matmul(
                pb[6][:, :], stencil[:, :], qn[:, t0:t0 + 512],
                start=True, stop=True),
                reads=[b_stencil, b_qn], writes=[b_pb[6]])
            p.op("act", lambda e, t0=t0: e.activation(
                qm[:, t0:t0 + 512], pb[6][:, :], AF.Copy),
                reads=[b_pb[6]], writes=[b_qm])
        p.op("dve", lambda e: e.memset(chain[:], -1e30), writes=[b_chain])
        for sc in (0, 8, 16, 24):
            p.op("pe", lambda e, sc=sc: e.matmul(
                pb[6][:, 0:128], kn[:, sc * 128:sc * 128 + 128],
                qm[:, sc * 128:sc * 128 + 128], start=True, stop=True),
                reads=[b_kn, b_qm], writes=[b_pb[6]])
            p.op("dve", lambda e: e.reduce_max(mcol[:, 0:1], pb[6][:, 0:128], axis=AX.X),
                 reads=[b_pb[6]], writes=[b_mcol])
            p.op("dve", lambda e: e.tensor_max(chain[:], chain[:], mcol[:]),
                 reads=[b_chain, b_mcol], writes=[b_chain])
        p.op("pe", lambda e: e.transpose(pb[6][0:1, 0:128], chain[:, 0:1], identf[:]),
             reads=[b_chain, b_identf], writes=[b_pb[6]])
        p.op("act", lambda e: e.activation(brow[:], pb[6][0:1, 0:128], AF.Copy),
             reads=[b_pb[6]], writes=[b_brow])
        p.op("dve", lambda e: e.reduce_max(bval[:, 0:1], brow[:], axis=AX.X),
             reads=[b_brow], writes=[b_bval])
        p.op("dve", lambda e: e.tensor_copy(bvalb[:], bval[:]),
             reads=[b_bval], writes=[b_bvalb])
        p.op("pe", lambda e: e.matmul(pb[6][:, 0:1], ones_row[0:1, :], bvalb[0:1, :],
                                      start=True, stop=True),
             reads=[b_ones_row, b_bvalb], writes=[b_pb[6]])
        p.op("act", lambda e: e.activation(negb[:], pb[6][:, 0:1], AF.Copy,
                                           scale=-1.0, bias=-5.0),
             reads=[b_pb[6], b_constcols], writes=[b_negb])

        for kc in range(NT):
            sbank = 2 + (kc % 2) * 2
            def smm(e, kc=kc, sbank=sbank):
                last = None
                pieces = []
                for (a2, l2) in circ_runs(((kc - 3) % NT) * 128, 896):
                    pieces.append((a2, l2))
                off = 0
                for (a2, l2) in pieces:
                    while l2 > 0:
                        bank = sbank + (1 if off >= 512 else 0)
                        boff = off - 512 if off >= 512 else off
                        take = min(l2, 512 - boff)
                        last = e.matmul(pb[bank][:, boff:boff + take],
                                        kn[:, kc * 128:kc * 128 + 128],
                                        qm[:, a2:a2 + take],
                                        start=True, stop=True)
                        off += take
                        a2 += take
                        l2 -= take
                q8 = ((kc + 8) % NT) * 128
                last = e.matmul(pb[sbank + 1][:, 384:512],
                                kn[:, kc * 128:kc * 128 + 128],
                                qm[:, q8:q8 + 128], start=True, stop=True)
                return last
            p.op("pe", smm, reads=[b_kn, b_qm],
                 writes=[b_pb[sbank], b_pb[sbank + 1]])
            for hf in range(2):
                p.op("act", lambda e, hf=hf, sbank=sbank: e.activation(
                    emr[:, hf * 512:(hf + 1) * 512], pb[sbank + hf][:, :],
                    AF.Exp, bias=negb[:, 0:1]),
                    reads=[b_pb[sbank + hf], b_negb], writes=[b_emr])
            p.op("dve", lambda e, kc=kc: e.tensor_mul(
                e_all[:, kc * 1024:(kc + 1) * 1024], emr[:], amask[:]),
                reads=[b_emr, b_amask], writes=[b_e_all])
        for tt in range(NT):
            avbank = tt % 2
            def avz(e, tt=tt, avbank=avbank, s=s):
                last = None
                for j, dseg in enumerate(SEGS):
                    kc = (tt - dseg) % NT
                    lw = e_all[:, (kc * 8 + j) * 128:(kc * 8 + j) * 128 + 128]
                    last = e.matmul(pb[avbank][:, 0:129], lw,
                                    vtmh[s][:, kc * 132:kc * 132 + 129],
                                    start=(j == 0), stop=(j == NSEG - 1))
                return last
            p.op("pe", avz, reads=[b_e_all, b_vtmh[s]],
                 writes=[b_pb[avbank]])
            p.op("dve", lambda e, avbank=avbank: e.reciprocal(
                zr16[:, 0:1], pb[avbank][:, 128:129]),
                reads=[b_pb[avbank]], writes=[b_zr16])
            p.op("act", lambda e, tt=tt, avbank=avbank: e.activation(
                av_sb[:, (tt % 16) * 128:(tt % 16 + 1) * 128],
                pb[avbank][:, 0:128], AF.Copy, scale=zr16[:, 0:1]),
                reads=[b_pb[avbank], b_zr16], writes=[b_av_sb])
            p.dma("sp", out=avtm_s[tt * 128:(tt + 1) * 128, h * 128:(h + 1) * 128],
                  in_=av_sb[:, (tt % 16) * 128:(tt % 16 + 1) * 128],
                  reads=[b_av_sb], writes=[b_avtm], key="st_av")

    es2.close()
    p.barrier()

    es3 = ExitStack()
    sb3 = lambda n, s, d=F32: es3.enter_context(nc.sbuf_tensor(n, s, d))
    avf = [sb3(f"avf{dc}", [128, T], BF16) for dc in range(FC)]
    b_avf = [B(f"avf{dc}") for dc in range(FC)]
    for dc in range(FC):
        p.dma_transpose("sp", out=avf[dc][:],
                        in_=avtm_s[:, dc * 128:(dc + 1) * 128],
                        reads=[b_avtm], writes=[b_avf[dc]], key=f"avf{dc % 2}")
    wob = [sb3(f"wob{i}", [128, FC * 128], BF16) for i in range(2)]
    b_wob = [B(f"wob{i}") for i in range(2)]
    xfb = [sb3(f"xfb{i}", [128, 512], F32) for i in range(2)]
    b_xfb = [B(f"xfb{i}") for i in range(2)]
    r1c = sb3("r1c", [128, 512], F32); b_r1c = B("r1c")
    r1bc = sb3("r1bc", [128, 512], BF16); b_r1bc = B("r1bc")

    for m in range(H):
        s = m % 2
        p.dma("pool", out=wob[s][:].rearrange("p (fc q) -> p fc q", q=128),
              in_=wom[m].rearrange("fc p q -> p fc q"),
              writes=[b_wob[s]], key=f"wob{s}")
        for tg in range(2):
            tg0 = tg * 2048
            bwr = [b_pb[j] for j in range(4)]
            def womm(e, s=s, tg0=tg0):
                last = None
                for dc in range(FC):
                    lw = wob[s][:, dc * 128:(dc + 1) * 128]
                    for tq in range(4):
                        last = e.matmul(pb[tq][:, :], lw,
                                        avf[dc][:, tg0 + tq * 512:tg0 + (tq + 1) * 512],
                                        start=(dc == 0), stop=(dc == FC - 1))
                return last
            p.op("pe", womm, reads=[b_wob[s]] + b_avf, writes=bwr)
            for tq in range(4):
                t0 = tg0 + tq * 512
                xs = tq % 2
                p.dma("act", out=xfb[xs][:], in_=xT[m * 128:(m + 1) * 128, t0:t0 + 512],
                      writes=[b_xfb[xs]], key=f"xfb{xs}")
                p.op("dve", lambda e, m=m, tq=tq, xs=xs: e.scalar_tensor_tensor(
                    r1c[:], pb[tq][:, :], bo_c[:, m:m + 1], xfb[xs][:],
                    ALU.add, ALU.add),
                    reads=[b_pb[tq], b_bo_c, b_xfb[xs]], writes=[b_r1c])
                p.dma("sp", out=r1_s[m * 128:(m + 1) * 128, t0:t0 + 512],
                      in_=r1c[:], reads=[b_r1c], writes=[b_r1], key="st_r1")
                p.op("act", lambda e: e.activation(r1bc[:], r1c[:], AF.Copy),
                     reads=[b_r1c], writes=[b_r1bc])
                p.dma("pool", out=r1b_s[m * 128:(m + 1) * 128, t0:t0 + 512],
                      in_=r1bc[:], reads=[b_r1bc], writes=[b_r1b], key="st_r1b")
    es3.close()
    p.barrier()

    def ln_stats(src_b16, b_src, rowbuf2, b_rowbuf2, scope_sb, tag):
        lb = [scope_sb(f"lnb{tag}{i}", [128, 512], BF16) for i in range(2)]
        b_lb = [B(f"lnb{tag}{i}") for i in range(2)]
        sqb = scope_sb(f"sqb{tag}", [128, 512], BF16)
        b_sqb = B(f"sqb{tag}")
        for tc in range(8):
            t0 = tc * 512
            for m in range(H):
                s = m % 2
                p.dma("act", out=lb[s][:], in_=src_b16[m * 128:(m + 1) * 128, t0:t0 + 512],
                      reads=[b_src], writes=[b_lb[s]], key=f"lnb{tag}{s}")
                p.op("act", lambda e, s=s: e.activation(sqb[:], lb[s][:], AF.Square),
                     reads=[b_lb[s]], writes=[b_sqb])
                p.op("pe", lambda e, s=s, m=m: e.matmul(
                    pb[2][0:1, :], ones_col[:, 0:1], lb[s][:],
                    start=(m == 0), stop=(m == H - 1)),
                    reads=[b_ones_col, b_lb[s]], writes=[b_pb[2]])
                p.op("pe", lambda e, m=m: e.matmul(
                    pb[3][0:1, :], ones_col[:, 0:1], sqb[:],
                    start=(m == 0), stop=(m == H - 1)),
                    reads=[b_ones_col, b_sqb], writes=[b_pb[3]])
            p.op("act", lambda e, t0=t0: e.activation(
                rowbuf2[0:1, t0:t0 + 512], pb[2][0:1, :], AF.Copy, scale=1.0 / DM),
                reads=[b_pb[2]], writes=[b_rowbuf2])
            p.op("act", lambda e, t0=t0: e.activation(
                rowbuf2[0:1, T + t0:T + t0 + 512], pb[3][0:1, :], AF.Copy, scale=1.0 / DM),
                reads=[b_pb[3]], writes=[b_rowbuf2])
        MU = rowbuf2[0:1, 0:T]
        MSQ = rowbuf2[0:1, T:2 * T]
        MU2 = rowbuf2[0:1, 2 * T:3 * T]
        p.op("act", lambda e: e.activation(MU2, MU, AF.Square),
             reads=[b_rowbuf2], writes=[b_rowbuf2])
        p.op("dve", lambda e: e.tensor_sub(MSQ, MSQ, MU2),
             reads=[b_rowbuf2], writes=[b_rowbuf2])
        p.op("act", lambda e: e.activation(MSQ, MSQ, AF.Ln, bias=EPS_LN),
             reads=[b_rowbuf2, b_constcols], writes=[b_rowbuf2])
        p.op("act", lambda e: e.activation(MSQ, MSQ, AF.Exp, scale=-0.5),
             reads=[b_rowbuf2], writes=[b_rowbuf2])
        p.op("dve", lambda e: e.tensor_mul(MU2, MU, MSQ),
             reads=[b_rowbuf2], writes=[b_rowbuf2])
        return MSQ, MU2

    def ln_apply(src_f32, b_src, rstd_row, b2_row, b_rows, dst, b_dst,
                 dst_b16, b_dst_b16, scope_sb, tag):
        rb16 = scope_sb(f"lnr{tag}", [1, 2 * T], BF16)
        b_rb16 = B(f"lnr{tag}")
        p.op("act", lambda e: e.activation(rb16[0:1, 0:T], rstd_row, AF.Copy),
             reads=[b_rows], writes=[b_rb16])
        p.op("act", lambda e: e.activation(rb16[0:1, T:2 * T], b2_row, AF.Copy),
             reads=[b_rows], writes=[b_rb16])
        lc = [scope_sb(f"lnc{tag}{i}", [128, 512], F32) for i in range(2)]
        b_lc = [B(f"lnc{tag}{i}") for i in range(2)]
        ot = scope_sb(f"lno{tag}", [128, 512], F32)
        b_ot = B(f"lno{tag}")
        ob = scope_sb(f"lnob{tag}", [128, 512], BF16)
        b_ob = B(f"lnob{tag}")
        for tc in range(8):
            t0 = tc * 512
            p.op("pe", lambda e, t0=t0: e.matmul(
                pb[2][:, :], ones_row[0:1, :], rb16[0:1, t0:t0 + 512],
                start=True, stop=True),
                reads=[b_ones_row, b_rb16], writes=[b_pb[2]])
            p.op("pe", lambda e, t0=t0: e.matmul(
                pb[3][:, :], ones_row[0:1, :], rb16[0:1, T + t0:T + t0 + 512],
                start=True, stop=True),
                reads=[b_ones_row, b_rb16], writes=[b_pb[3]])
            for m in range(H):
                s = m % 2
                p.dma("act", out=lc[s][:], in_=src_f32[m * 128:(m + 1) * 128, t0:t0 + 512],
                      reads=[b_src], writes=[b_lc[s]], key=f"lnc{tag}{s}")
                p.op("dve", lambda e, s=s: e.tensor_mul(ot[:], lc[s][:], pb[2][:, :]),
                     reads=[b_lc[s], b_pb[2]], writes=[b_ot])
                p.op("dve", lambda e: e.tensor_sub(ot[:], ot[:], pb[3][:, :]),
                     reads=[b_ot, b_pb[3]], writes=[b_ot])
                p.dma("sp", out=dst[m * 128:(m + 1) * 128, t0:t0 + 512],
                      in_=ot[:], reads=[b_ot], writes=[b_dst], key=f"lnst{tag}")
                if dst_b16 is not None:
                    p.op("act", lambda e: e.activation(ob[:], ot[:], AF.Copy),
                         reads=[b_ot], writes=[b_ob])
                    p.dma("pool", out=dst_b16[m * 128:(m + 1) * 128, t0:t0 + 512],
                          in_=ob[:], reads=[b_ob], writes=[b_dst_b16],
                          key=f"lnstb{tag}")

    es3b = ExitStack()
    sb3b = lambda n, s, d=F32: es3b.enter_context(nc.sbuf_tensor(n, s, d))
    rowbuf2 = sb3b("rowbuf2", [1, 3 * T], F32)
    b_rowbuf2 = B("rowbuf2")
    rstd1, b21 = ln_stats(r1b_s, b_r1b, rowbuf2, b_rowbuf2, sb3b, "1")
    ln_apply(r1_s, b_r1, rstd1, b21, b_rowbuf2, x1_s, b_x1, x1b_s, b_x1b,
             sb3b, "1")
    es3b.close()
    p.barrier()

    es4 = ExitStack()
    sb4 = lambda n, s, d=F32: es4.enter_context(nc.sbuf_tensor(n, s, d))
    rw_sb = sb4("rw_sb", [128, FC * E], BF16)
    b_rw = B("rw_sb")
    p.dma("pool", out=rw_sb[:].rearrange("p (fc e) -> p fc e", e=E),
          in_=rwT.rearrange("(fc p) e -> p fc e", p=128),
          writes=[b_rw])
    x4b = [sb4(f"x4b{i}", [128, 512], BF16) for i in range(2)]
    b_x4b = [B(f"x4b{i}") for i in range(2)]
    s_f = sb4("s_f", [16, T], F32); b_s_f = B("s_f")
    id16 = sb4("id16", [16, 16], F32); b_id16 = B("id16")
    id16i = sb4("id16i", [16, 16], F32); b_id16i = B("id16i")
    p.op("pool", lambda e: e.iota(id16i[:], pattern=[[1, 16]], base=0,
                                  channel_multiplier=-1,
                                  allow_small_or_imprecise_dtypes=True),
         writes=[b_id16i])
    p.op("dve", lambda e: e.tensor_single_scalar(id16[:], id16i[:], 0.0, ALU.is_equal),
         reads=[b_id16i], writes=[b_id16])
    wT_sb = sb4("wT_sb", [16, T], BF16); b_wT = B("wT_sb")
    s_tm = sb4("s_tm", [128, E], F32); b_s_tm = B("s_tm")
    h16r = sb4("h16r", [128, E], F32); b_h16r = B("h16r")
    t8r = sb4("t8r", [128, 8], F32); b_t8r = B("t8r")
    t8c = sb4("t8c", [128, 8], F32); b_t8c = B("t8c")
    mask16r = sb4("mask16r", [128, E], F32); b_mask16r = B("mask16r")
    wur = sb4("wur", [128, E], F32); b_wur = B("wur")
    wsum = sb4("wsum", [128, 2], F32); b_wsum = B("wsum")
    w_tmr = sb4("w_tmr", [128, E], F32); b_w_tmr = B("w_tmr")
    for tc in range(8):
        t0 = tc * 512
        xs = tc % 2
        for fc in range(FC):
            p.dma("act", out=x4b[xs][:], in_=x1b_s[fc * 128:(fc + 1) * 128, t0:t0 + 512],
                  reads=[b_x1b], writes=[b_x4b[xs]], key=f"x4b{xs}")
            p.op("pe", lambda e, fc=fc, xs=xs: e.matmul(
                pb[2][0:16, :], rw_sb[:, fc * E:(fc + 1) * E], x4b[xs][:],
                start=(fc == 0), stop=(fc == FC - 1)),
                reads=[b_rw, b_x4b[xs]], writes=[b_pb[2]])
            xs = (xs + 1) % 2
        p.op("act", lambda e, t0=t0: e.activation(
            s_f[0:16, t0:t0 + 512], pb[2][0:16, :], AF.Sigmoid),
            reads=[b_pb[2]], writes=[b_s_f])
    for tt in range(NT):
        t0 = tt * 128
        p.op("pe", lambda e, t0=t0: e.transpose(
            pb[3][:, 0:16], s_f[0:16, t0:t0 + 128], id16[:, :]),
            reads=[b_s_f, b_id16], writes=[b_pb[3]])
        p.op("dve", lambda e: e.tensor_copy(s_tm[:], pb[3][:, 0:16]),
             reads=[b_pb[3]], writes=[b_s_tm])
        p.op("dve", lambda e: e.tensor_add(h16r[:], s_tm[:], eb_b[:]),
             reads=[b_s_tm, b_eb_b], writes=[b_h16r])
        p.op("dve", lambda e: e.max(t8r[:], h16r[:]),
             reads=[b_h16r], writes=[b_t8r])
        p.op("act", lambda e: e.copy(t8c[:], t8r[:]),
             reads=[b_t8r], writes=[b_t8c])
        p.op("dve", lambda e: e.tensor_single_scalar(
            mask16r[:], h16r[:], t8c[:, 3:4], ALU.is_ge),
            reads=[b_h16r, b_t8c], writes=[b_mask16r])
        p.op("dve", lambda e: e.tensor_mul(wur[:], mask16r[:], s_tm[:]),
             reads=[b_mask16r, b_s_tm], writes=[b_wur])
        p.op("dve", lambda e: e.reduce_sum(wsum[:, 0:1], wur[:], axis=AX.X),
             reads=[b_wur], writes=[b_wsum])
        p.op("dve", lambda e: e.tensor_single_scalar(
            wsum[:, 1:2], wsum[:, 0:1], 1e-20, ALU.add),
            reads=[b_wsum], writes=[b_wsum])
        p.op("dve", lambda e: e.reciprocal(wsum[:, 0:1], wsum[:, 1:2]),
             reads=[b_wsum], writes=[b_wsum])
        p.op("dve", lambda e: e.tensor_scalar_mul(w_tmr[:], wur[:], wsum[:, 0:1]),
             reads=[b_wur, b_wsum], writes=[b_w_tmr])
        p.op("pe", lambda e: e.transpose(pb[3][0:16, 0:128], w_tmr[:, 0:16], identf[:]),
             reads=[b_w_tmr, b_identf], writes=[b_pb[3]])
        p.op("act", lambda e, t0=t0: e.activation(
            wT_sb[0:16, t0:t0 + 128], pb[3][0:16, 0:128], AF.Copy),
            reads=[b_pb[3]], writes=[b_wT])

    gupb = [sb4("gupb0", [128, FC * 2 * I], BF16)]
    b_gupb = [B("gupb0")]
    wrow = [sb4(f"wrow{i}", [1, 2048], BF16) for i in range(2)]
    b_wrow = [B(f"wrow{i}") for i in range(2)]
    x1w = [sb4(f"x1w{c}", [128, 2048], BF16) for c in range(FC)]
    b_x1w = [B(f"x1w{c}") for c in range(FC)]
    sg_sb = sb4("sg_sb", [128, IC * 2048], BF16); b_sg = B("sg_sb")
    wbc_sb = sb4("wbc_sb", [128, 2048], BF16); b_wbc = B("wbc_sb")
    mid_sb = sb4("mid_sb", [128, 2048], BF16); b_mid_sb = B("mid_sb")

    for tcg in range(2):
        tg0 = tcg * 2048
        for c in range(FC):
            p.dma("act", out=x1w[c][:], in_=x1b_s[c * 128:(c + 1) * 128, tg0:tg0 + 2048],
                  reads=[b_x1b], writes=[b_x1w[c]], key=f"x1w{c % 2}")
        for e_i in range(17):
            s = 0
            if e_i == 16:
                p.dma("pool", out=gupb[s][:].rearrange("p (fc g) -> p fc g", g=2 * I),
                      in_=sgu.rearrange("fc p g -> p fc g"),
                      writes=[b_gupb[s]], key=f"gupb{s}")
            if e_i < 16:
                p.dma("pool", out=gupb[s][:].rearrange("p (fc g) -> p fc g", g=2 * I),
                      in_=gup[e_i].rearrange("fc p g -> p fc g"),
                      writes=[b_gupb[s]], key=f"gupb{s}")
                p.dma("pool", out=wrow[e_i % 2][:], in_=wT_sb[e_i:e_i + 1, tg0:tg0 + 2048],
                      reads=[b_wT], writes=[b_wrow[e_i % 2]], key=f"wrow{e_i % 2}")
                for tq in range(4):
                    p.op("pe", lambda e, e_i=e_i, tq=tq: e.matmul(
                        pb[4 + tq][:, :], ones_row[0:1, :],
                        wrow[e_i % 2][0:1, tq * 512:(tq + 1) * 512],
                        start=True, stop=True),
                        reads=[b_ones_row, b_wrow[e_i % 2]], writes=[b_pb[4 + tq]])
                    p.op("act", lambda e, tq=tq: e.activation(
                        wbc_sb[:, tq * 512:(tq + 1) * 512], pb[4 + tq][:, :], AF.Copy),
                        reads=[b_pb[4 + tq]], writes=[b_wbc])
            for ic in range(IC):
                bs = (ic % 2) * 4
                bwr = [b_pb[bs + j] for j in range(4)]
                def gmm(e, s=s, ic=ic, bs=bs):
                    last = None
                    for fc in range(FC):
                        lw = gupb[s][:, fc * 2 * I + ic * 128:fc * 2 * I + (ic + 1) * 128]
                        for tq in range(4):
                            last = e.matmul(pb[bs + tq][:, :], lw,
                                            x1w[fc][:, tq * 512:(tq + 1) * 512],
                                            start=(fc == 0), stop=(fc == FC - 1))
                    return last
                p.op("pe", gmm, reads=[b_gupb[s]] + b_x1w, writes=bwr)
                for tq in range(4):
                    p.op("act", lambda e, ic=ic, bs=bs, tq=tq: e.activation(
                        sg_sb[:, ic * 2048 + tq * 512:ic * 2048 + (tq + 1) * 512],
                        pb[bs + tq][:, :], AF.Silu),
                        reads=[b_pb[bs + tq]], writes=[b_sg])
            if e_i < 16:
                for ic in range(IC):
                    p.op("dve", lambda e, ic=ic: e.tensor_mul(
                        sg_sb[:, ic * 2048:(ic + 1) * 2048],
                        sg_sb[:, ic * 2048:(ic + 1) * 2048], wbc_sb[:]),
                        reads=[b_sg, b_wbc], writes=[b_sg])
            for ic in range(IC):
                bs = (ic % 2) * 4
                bwr = [b_pb[bs + j] for j in range(4)]
                def umm(e, s=s, ic=ic, bs=bs):
                    last = None
                    for fc in range(FC):
                        lw = gupb[s][:, fc * 2 * I + I + ic * 128:fc * 2 * I + I + (ic + 1) * 128]
                        for tq in range(4):
                            last = e.matmul(pb[bs + tq][:, :], lw,
                                            x1w[fc][:, tq * 512:(tq + 1) * 512],
                                            start=(fc == 0), stop=(fc == FC - 1))
                    return last
                p.op("pe", umm, reads=[b_gupb[s]] + b_x1w, writes=bwr)
                for tq in range(4):
                    p.op("dve", lambda e, ic=ic, bs=bs, tq=tq: e.tensor_mul(
                        mid_sb[:, tq * 512:(tq + 1) * 512],
                        sg_sb[:, ic * 2048 + tq * 512:ic * 2048 + (tq + 1) * 512],
                        pb[bs + tq][:, :]),
                        reads=[b_sg, b_pb[bs + tq]], writes=[b_mid_sb])
                p.dma("sp", out=mid_s[e_i, ic * 128:(ic + 1) * 128, tg0:tg0 + 2048],
                      in_=mid_sb[:], reads=[b_mid_sb], writes=[b_mid], key="st_mid")
    es4.close()
    p.barrier()

    es4b = ExitStack()
    sb4b = lambda n, s, d=F32: es4b.enter_context(nc.sbuf_tensor(n, s, d))
    dcast = sb4b("dcast", [128, 102 * 128], BF16)
    b_dcast = B("dcast")
    for m in range(H):
        p.dma("pool", out=dcast[:], in_=dallm[m].rearrange("p kc q -> p (kc q)"),
              writes=[b_dcast], key="dcast_ld")
        p.dma("sp", out=dbf_s[m], in_=dcast[:],
              reads=[b_dcast], writes=[b_dbf], key="dcast_st")
    es4b.close()
    p.barrier()

    es4c = ExitStack()
    sb4c = lambda n, s, d=F32: es4c.enter_context(nc.sbuf_tensor(n, s, d))
    midwA = sb4c("midwA", [128, 48 * 512], BF16); b_midwA = B("midwA")
    midwB = sb4c("midwB", [128, 54 * 512], BF16); b_midwB = B("midwB")
    dwb = [sb4c(f"dwb{i}", [128, 102 * 128], BF16) for i in range(2)]
    b_dwb = [B(f"dwb{i}") for i in range(2)]
    x1fb = [sb4c(f"x1fb{i}", [128, 512], F32) for i in range(2)]
    b_x1fb = [B(f"x1fb{i}") for i in range(2)]
    r2c = [sb4c(f"r2c{i}", [128, 512], F32) for i in range(2)]
    b_r2c = [B(f"r2c{i}") for i in range(2)]
    r2bc = [sb4c(f"r2bc{i}", [128, 512], BF16) for i in range(2)]
    b_r2bc = [B(f"r2bc{i}") for i in range(2)]

    for tc in range(8):
        t0 = tc * 512
        for e_i in range(17):
            half = 0 if e_i < 8 else 1
            dst = midwA if half == 0 else midwB
            bdst = b_midwA if half == 0 else b_midwB
            base = e_i * IC if half == 0 else (e_i - 8) * IC
            p.dma("act", out=dst[:, base * 512:(base + IC) * 512]
                  .rearrange("p (ic t) -> p ic t", t=512),
                  in_=mid_s[e_i, :, t0:t0 + 512].rearrange("(ic p) t -> p ic t", p=128),
                  reads=[b_mid], writes=[bdst], key="midw" + ("A" if half == 0 else "B"))
        for m in range(H):
            s = m % 2
            p.dma("act", out=dwb[s][:], in_=dbf_s[m],
                  reads=[b_dbf], writes=[b_dwb[s]], key=f"dwb{s}")
            bank = m % 4
            def dmmA(e, s=s, bank=bank):
                last = None
                for eic in range(48):
                    last = e.matmul(pb[bank][:, :],
                                    dwb[s][:, eic * 128:(eic + 1) * 128],
                                    midwA[:, eic * 512:(eic + 1) * 512],
                                    start=(eic == 0), stop=False)
                return last
            def dmmB(e, s=s, bank=bank):
                last = None
                for eic in range(54):
                    last = e.matmul(pb[bank][:, :],
                                    dwb[s][:, (48 + eic) * 128:(48 + eic + 1) * 128],
                                    midwB[:, eic * 512:(eic + 1) * 512],
                                    start=False, stop=(eic == 53))
                return last
            p.op("pe", dmmA, reads=[b_dwb[s], b_midwA], writes=[b_pb[bank]])
            p.op("pe", dmmB, reads=[b_dwb[s], b_midwB], writes=[b_pb[bank]])
            xs = m % 2
            p.dma("pool", out=x1fb[xs][:], in_=x1_s[m * 128:(m + 1) * 128, t0:t0 + 512],
                  reads=[b_x1], writes=[b_x1fb[xs]], key=f"x1fb{xs}")
            p.op("dve", lambda e, bank=bank, xs=xs: e.tensor_add(
                r2c[xs][:], pb[bank][:, :], x1fb[xs][:]),
                reads=[b_pb[bank], b_x1fb[xs]], writes=[b_r2c[xs]])
            p.dma("sp", out=r2_s[m * 128:(m + 1) * 128, t0:t0 + 512],
                  in_=r2c[xs][:], reads=[b_r2c[xs]], writes=[b_r2], key=f"st_r2{xs}")
            p.op("act", lambda e, xs=xs: e.activation(r2bc[xs][:], r2c[xs][:], AF.Copy),
                 reads=[b_r2c[xs]], writes=[b_r2bc[xs]])
            p.dma("pool", out=r2b_s[m * 128:(m + 1) * 128, t0:t0 + 512],
                  in_=r2bc[xs][:], reads=[b_r2bc[xs]], writes=[b_r2b], key=f"st_r2b{xs}")
    es4c.close()
    p.barrier()

    es5 = ExitStack()
    sb5 = lambda n, s, d=F32: es5.enter_context(nc.sbuf_tensor(n, s, d))
    rowbuf3 = sb5("rowbuf3", [1, 3 * T], F32)
    b_rowbuf3 = B("rowbuf3")
    rstd2, b22 = ln_stats(r2b_s, b_r2b, rowbuf3, b_rowbuf3, sb5, "2")
    ln_apply(r2_s, b_r2, rstd2, b22, b_rowbuf3, out, b_out, None, None,
             sb5, "2")
    es5.close()
    p.barrier()
    out_names = ["out"]
    if "scratch" in dev_taps:
        out_names += ["qpb_s", "kpb_s", "vtm_s", "avtm_s", "x1_s", "mid_s"]
    if "rows" in dev_taps:
        rows_o = DT("rows_o", [1, 2 * T], F32, kind="ExternalOutput").ap()
        b_rows_o = B("rows_o")
        p.dma("sp", out=rows_o[:, :], in_=rowbuf[:], reads=[b_rowbuf],
              writes=[b_rows_o], key="st_rows")
        out_names.append("rows_o")
    p.finish("pool")
    p.emit()
    return nc, es, out_names


def _host_layout(inputs):
    x = inputs["L0_attn_x"]
    wq = np.asarray(inputs["L0_attn_wq"])
    wk = np.asarray(inputs["L0_attn_wk"])
    wv = np.asarray(inputs["L0_attn_wv"])
    wo = np.asarray(inputs["L0_attn_wo"])
    gup = np.asarray(inputs["L0_ffn_gate_up_proj"])
    down = np.asarray(inputs["L0_ffn_down_proj"])
    sgw = np.asarray(inputs["L0_ffn_shared_gate_w"])
    suw = np.asarray(inputs["L0_ffn_shared_up_w"])
    sdw = np.asarray(inputs["L0_ffn_shared_down_w"])

    def mblock(w):
        wT = np.ascontiguousarray(w.T)
        return np.ascontiguousarray(
            wT.reshape(FC, 128, H, 128).transpose(2, 0, 1, 3))

    wqm = mblock(wq); wkm = mblock(wk); wom = mblock(wo)
    wvT = np.ascontiguousarray(wv.T)
    wvdc = np.ascontiguousarray(
        wvT.reshape(FC, 128, 4, 512).transpose(2, 0, 1, 3))

    def cols(v):
        return np.ascontiguousarray(v.reshape(FC, 128).T)

    gupT = np.ascontiguousarray(
        gup.transpose(0, 2, 1).reshape(E, FC, 128, 2 * I))
    sguT = np.concatenate([sgw.T, suw.T], axis=1)
    sguT = np.ascontiguousarray(sguT.reshape(FC, 128, 2 * I))

    dT = np.concatenate([down[e].T for e in range(E)] + [sdw.T], axis=0)
    dallm = np.ascontiguousarray(
        dT.reshape(102, 128, H, 128).transpose(2, 1, 0, 3))

    base = {
        "wqm": wqm, "wkm": wkm, "wvdc": wvdc, "wom": wom,
        "bqr": np.asarray(inputs["L0_attn_bq"])[None, :],
        "bkr": np.asarray(inputs["L0_attn_bk"])[None, :],
        "boc": cols(np.asarray(inputs["L0_attn_bo"])),
        "gqc": cols(np.asarray(inputs["L0_attn_gq"])),
        "gkc": cols(np.asarray(inputs["L0_attn_gk"])),
        "bvr": np.asarray(inputs["L0_attn_bv"])[None, :],
        "wmix": np.asarray(inputs["L0_attn_w_mix"]),
        "rwT": np.ascontiguousarray(np.asarray(inputs["L0_ffn_router_w"]).T),
        "ebr": np.asarray(inputs["L0_ffn_e_bias"])[None, :],
        "gup": gupT, "sgu": sguT, "dallm": dallm,
    }
    in_maps = []
    for b in range(8):
        m = dict(base)
        m["xT"] = np.ascontiguousarray(np.asarray(x[b]).T)
        in_maps.append(m)
    return in_maps


def kernel(**inputs):
    in_maps = _host_layout(inputs)
    nc, es, out_names = build_program()
    try:
        res = run_bass_kernel_spmd(nc, in_maps, core_ids=list(range(8)))
    finally:
        es.close()
    outs = [np.ascontiguousarray(np.asarray(r["out"], np.float32).T)
            for r in res.results]
    return np.stack(outs, axis=0)


# version 07
# speedup vs baseline: 1.1182x; 1.1182x over previous
import math
from collections import defaultdict
from contextlib import ExitStack

import numpy as np
import concourse.bass as bass
import concourse.mybir as mybir
from concourse.bass_utils import run_bass_kernel_spmd

F32 = mybir.dt.float32
BF16 = mybir.dt.bfloat16
AF = mybir.ActivationFunctionType
ALU = mybir.AluOpType
AX = mybir.AxisListType

T = 4096
DM = 2048
H = 16
D = 128
FC = DM // 128
E = 16
I = 768
IC = I // 128
TOPK = 4
SCALE = D ** -0.5
EPS_RMS = 1e-6
EPS_LN = 1e-5
SEQ_SHIFTS = [0, 1, -1, 3, -3, 7, -7, 20, -20, 53, -53, 141, -141, 380, -380, 1024]
CH_SHIFTS = [0, 1, 2, 4, 8]
SEGS = [-3, -2, -1, 0, 1, 2, 3, 8]
NSEG = len(SEGS)
SEG_DIAGS = []
for _d in SEGS:
    SEG_DIAGS.append(sorted(s - _d * 128 for s in SEQ_SHIFTS
                            if -128 < s - _d * 128 < 128))
NT = T // 128


class Buf:
    __slots__ = ("name", "writer", "readers")

    def __init__(self, name):
        self.name = name
        self.writer = None
        self.readers = []


class Prog:
    ENGINES = ("pe", "act", "dve", "pool", "sp")

    def __init__(self, nc, es):
        self.nc = nc
        self.es = es
        self.sem = {}
        self.count = defaultdict(int)
        self.hwm = defaultdict(int)
        self.queue = {e: [] for e in self.ENGINES}
        self.nops = 0
        for e in self.ENGINES:
            self.sem[e] = es.enter_context(nc.semaphore(f"sem_{e}"))

    def buf(self, name):
        return Buf(name)

    def _dma_sem(self, key):
        skey = "dma_" + key
        if skey not in self.sem:
            self.sem[skey] = self.es.enter_context(
                self.nc.semaphore(f"sem_dma_{key}"))
        return skey

    def _wait(self, waiter_eng, src_stream, val):
        if val <= 0:
            return
        if src_stream.startswith("dma_"):
            val = self.count[src_stream]
        k = (waiter_eng, src_stream)
        if self.hwm[k] >= val:
            return
        self.hwm[k] = val
        sem = self.sem[src_stream]
        tgt = val * (16 if src_stream.startswith("dma_") else 1)
        self.queue[waiter_eng].append(lambda e: e.wait_ge(sem, tgt))

    def _pre(self, eng, reads, writes):
        need = {}
        bar = getattr(self, "barrier_counts", None)
        if bar:
            for s, c in bar.items():
                if c > need.get(s, 0):
                    need[s] = c
        for b in reads:
            if b.writer is not None:
                s, c = b.writer
                if c > need.get(s, 0):
                    need[s] = c
        for b in writes:
            if b.writer is not None:
                s, c = b.writer
                if c > need.get(s, 0):
                    need[s] = c
            for (s, c) in b.readers:
                if c > need.get(s, 0):
                    need[s] = c
        for s, c in need.items():
            self._wait(eng, s, c)

    def barrier(self):
        self.barrier_counts = dict(self.count)

    def _post(self, stream, reads, writes):
        self.count[stream] += 1
        c = self.count[stream]
        for b in writes:
            b.writer = (stream, c)
            b.readers = []
        for b in reads:
            b.readers.append((stream, c))
        return c

    def op(self, eng, fn, reads=(), writes=()):
        self._pre(eng, reads, writes)
        sem = self.sem[eng]
        self.queue[eng].append(lambda e: fn(e).then_inc(sem, 1))
        self.nops += 1
        return self._post(eng, reads, writes)

    def dma(self, queue, out=None, in_=None, reads=(), writes=(), key=None, **kw):
        if key is None:
            key = writes[0].name if writes else (reads[0].name if reads else queue)
        stream = self._dma_sem(key)
        self._pre(queue, reads, writes)
        sem = self.sem[stream]
        o, i = out, in_
        self.queue[queue].append(
            lambda e: e.dma_start(out=o, in_=i, **kw).then_inc(sem, 16))
        self.nops += 1
        return self._post(stream, reads, writes)

    def dma_transpose(self, queue, out, in_, reads=(), writes=(), key=None):
        if key is None:
            key = writes[0].name
        stream = self._dma_sem(key)
        self._pre(queue, reads, writes)
        sem = self.sem[stream]
        self.queue[queue].append(
            lambda e: e.dma_start(out=out, in_=in_, transpose=True).then_inc(sem, 16))
        self.nops += 1
        return self._post(stream, reads, writes)

    def finish(self, eng="pool"):
        for s, c in list(self.count.items()):
            if s != eng and c > 0:
                self._wait(eng, s, c)

    def emit(self):
        with self.nc.Block() as block:
            for eng_name, starter in (("pe", block.tensor), ("act", block.scalar),
                                      ("dve", block.vector), ("pool", block.gpsimd),
                                      ("sp", block.sync)):
                q = self.queue[eng_name]
                if q:
                    def body(e, q=q):
                        for fn in q:
                            fn(e)
                    starter(body)


def circ_runs(a, ln, period=T):
    a %= period
    out = []
    while ln > 0:
        take = min(ln, period - a)
        out.append((a, take))
        ln -= take
        a = 0
    return out


def build_program(dev_taps=()):
    es = ExitStack()
    nc = bass.Bass("TRN2", target_bir_lowering=False, debug=False)
    p = Prog(nc, es)

    kind_scr = "ExternalOutput" if "scratch" in dev_taps else "Internal"

    DT = nc.dram_tensor
    xT = DT("xT", [DM, T], F32, kind="ExternalInput").ap()
    wqm = DT("wqm", [H, FC, 128, 128], F32, kind="ExternalInput").ap()
    wkm = DT("wkm", [H, FC, 128, 128], F32, kind="ExternalInput").ap()
    wvdc = DT("wvdc", [4, FC, 128, 512], F32, kind="ExternalInput").ap()
    wom = DT("wom", [H, FC, 128, 128], F32, kind="ExternalInput").ap()
    bqr = DT("bqr", [1, DM], F32, kind="ExternalInput").ap()
    bkr = DT("bkr", [1, DM], F32, kind="ExternalInput").ap()
    boc = DT("boc", [128, FC], F32, kind="ExternalInput").ap()
    gqc = DT("gqc", [128, FC], F32, kind="ExternalInput").ap()
    gkc = DT("gkc", [128, FC], F32, kind="ExternalInput").ap()
    bvr = DT("bvr", [1, DM], F32, kind="ExternalInput").ap()
    wmix = DT("wmix", [1, 5], F32, kind="ExternalInput").ap()
    rwT = DT("rwT", [DM, E], F32, kind="ExternalInput").ap()
    ebr = DT("ebr", [1, E], F32, kind="ExternalInput").ap()
    gup = DT("gup", [E, FC, 128, 2 * I], F32, kind="ExternalInput").ap()
    sgu = DT("sgu", [FC, 128, 2 * I], F32, kind="ExternalInput").ap()
    dallm = DT("dallm", [H, 128, 102, 128], F32, kind="ExternalInput").ap()

    out = DT("out", [DM, T], F32, kind="ExternalOutput").ap()

    qpb_s = DT("qpb_s", [DM, T], BF16, kind=kind_scr).ap()
    kpb_s = DT("kpb_s", [DM, T], BF16, kind=kind_scr).ap()
    vtm_s = DT("vtm_s", [H, T, 128], BF16, kind=kind_scr).ap()
    avtm_s = DT("avtm_s", [T, DM], BF16, kind=kind_scr).ap()
    r1_s = DT("r1_s", [DM, T], F32, kind="Internal").ap()
    r1b_s = DT("r1b_s", [DM, T], BF16, kind="Internal").ap()
    x1_s = DT("x1_s", [DM, T], F32, kind=kind_scr).ap()
    x1b_s = DT("x1b_s", [DM, T], BF16, kind="Internal").ap()
    mid_s = DT("mid_s", [17, I, T], BF16, kind=kind_scr).ap()
    dbf_s = DT("dbf_s", [H, 128, 102 * 128], BF16, kind="Internal").ap()
    r2_s = DT("r2_s", [DM, T], F32, kind="Internal").ap()
    r2b_s = DT("r2b_s", [DM, T], BF16, kind="Internal").ap()

    sb = lambda n, s, d=F32: es.enter_context(nc.sbuf_tensor(n, s, d))
    pb = [es.enter_context(nc.psum_tensor(f"pb{i}", [128, 512], F32))
          for i in range(8)]
    b_pb = [p.buf(f"pb{i}") for i in range(8)]

    B = p.buf
    ones_row = sb("ones_row", [1, 128], BF16)
    ones_r512 = sb("ones_r512", [1, 512], BF16)
    ones_col = sb("ones_col", [128, 1], BF16)
    ident = sb("ident", [128, 128], BF16)
    identf = sb("identf", [128, 128], F32)
    stencil = sb("stencil", [128, 128], BF16)
    amask = sb("amask", [128, NSEG * 128], BF16)
    bq_r = sb("bq_r", [1, DM], BF16)
    bk_r = sb("bk_r", [1, DM], BF16)
    bo_c = sb("bo_c", [128, FC], F32)
    gq_c = sb("gq_c", [128, FC], F32)
    gk_c = sb("gk_c", [128, FC], F32)
    wmix_r = sb("wmix_r", [1, 8], F32)
    wmix_c = sb("wmix_c", [128, 8], F32)
    eb_r = sb("eb_r", [1, E], BF16)
    eb_b = sb("eb_b", [128, E], F32)
    b_ones_row, b_ones_col, b_ident = B("ones_row"), B("ones_col"), B("ident")
    b_identf = B("identf")
    b_ones_r512 = B("ones_r512")
    b_stencil = B("stencil")
    b_amask = B("amask")
    b_bq_r, b_bk_r, b_bo_c, b_gq_c, b_gk_c = B("bq_r"), B("bk_r"), B("bo_c"), B("gq_c"), B("gk_c")
    b_wmix_r, b_wmix_c, b_eb_r, b_eb_b = B("wmix_r"), B("wmix_c"), B("eb_r"), B("eb_b")

    b_qpb = B("qpb_s"); b_kpb = B("kpb_s"); b_vtm = B("vtm_s")
    b_avtm = B("avtm_s"); b_r1 = B("r1_s"); b_r1b = B("r1b_s")
    b_x1 = B("x1_s"); b_x1b = B("x1b_s"); b_mid = B("mid_s")
    b_dbf = B("dbf_s"); b_r2 = B("r2_s"); b_r2b = B("r2b_s"); b_out = B("out")

    constcols = sb("constcols", [128, 8], F32)
    b_constcols = B("constcols")
    _ccn = [0]
    def register_const(v):
        i = _ccn[0]
        _ccn[0] += 1
        p.op("dve", lambda e, i=i, v=v: e.memset(constcols[:, i:i + 1], float(v)),
             writes=[b_constcols])
        nc.const_aps.register(F32, float(v), constcols[:, i:i + 1])
    for _v in (EPS_RMS, EPS_LN, 1e-20):
        register_const(_v)

    rstd_rb = sb("rstd_rb", [1, 2 * T], BF16)
    b_rstd_rb = B("rstd_rb")
    ROW_Q, ROW_K = 0, 1
    es0 = ExitStack()
    sb0 = lambda n, s, d=F32: es0.enter_context(nc.sbuf_tensor(n, s, d))
    iotaD = sb0("iotaD", [128, 128], F32)
    stenf = sb0("stenf", [128, 128], F32)
    amaskf = sb0("amaskf", [128, NSEG * 128], F32)
    b_iotaD, b_stenf, b_amaskf = B("iotaD"), B("stenf"), B("amaskf")

    p.op("dve", lambda e: e.memset(ones_row[:], 1.0), writes=[b_ones_row])
    p.op("dve", lambda e: e.memset(ones_r512[:], 1.0), writes=[b_ones_r512])
    p.op("dve", lambda e: e.memset(ones_col[:], 1.0), writes=[b_ones_col])
    p.op("pool", lambda e: e.iota(iotaD[:], pattern=[[1, 128]], base=0,
                                  channel_multiplier=-1,
                                  allow_small_or_imprecise_dtypes=True),
         writes=[b_iotaD])
    p.op("dve", lambda e: e.tensor_single_scalar(identf[:], iotaD[:], 0.0, ALU.is_equal),
         reads=[b_iotaD], writes=[b_identf])
    p.op("dve", lambda e: e.tensor_copy(ident[:], identf[:]),
         reads=[b_identf], writes=[b_ident])
    for name, srcap, dst, bdst in (("boc", boc, bo_c, b_bo_c), ("gqc", gqc, gq_c, b_gq_c),
                                   ("gkc", gkc, gk_c, b_gk_c)):
        p.dma("sp", out=dst[:], in_=srcap[:, :], writes=[bdst])
    p.dma("pool", out=bq_r[:], in_=bqr[:, :], writes=[b_bq_r])
    p.dma("pool", out=bk_r[:], in_=bkr[:, :], writes=[b_bk_r])

    p.dma("sp", out=wmix_r[0:1, 0:5], in_=wmix[:, :], writes=[b_wmix_r])
    p.dma("pool", out=eb_r[:], in_=ebr[:, :], writes=[b_eb_r])

    wmix_rb = sb0("wmix_rb", [1, 8], BF16)
    b_wmix_rb = B("wmix_rb")
    p.op("dve", lambda e: e.memset(wmix_rb[:], 0.0), writes=[b_wmix_rb])
    p.op("dve", lambda e: e.tensor_copy(wmix_rb[0:1, 0:5], wmix_r[0:1, 0:5]),
         reads=[b_wmix_r], writes=[b_wmix_rb])
    p.op("pe", lambda e: e.matmul(pb[6][:, 0:8], ones_row[0:1, :], wmix_rb[0:1, :],
                                  start=True, stop=True),
         reads=[b_ones_row, b_wmix_rb], writes=[b_pb[6]])
    p.op("dve", lambda e: e.tensor_copy(wmix_c[:], pb[6][:, 0:8]),
         reads=[b_pb[6]], writes=[b_wmix_c])
    p.op("pe", lambda e: e.matmul(pb[6][:, 0:E], ones_row[0:1, :], eb_r[0:1, :],
                                  start=True, stop=True),
         reads=[b_ones_row, b_eb_r], writes=[b_pb[6]])
    p.op("dve", lambda e: e.tensor_copy(eb_b[:], pb[6][:, 0:E]),
         reads=[b_pb[6]], writes=[b_eb_b])

    p.op("dve", lambda e: e.memset(stenf[:], 0.0), writes=[b_stenf])
    mtmp = sb0("mtmp", [128, 128], F32)
    mtmp2 = sb0("mtmp2", [128, 128], F32)
    b_mtmp, b_mtmp2 = B("mtmp"), B("mtmp2")
    for ci, c in enumerate(CH_SHIFTS):
        p.op("dve", lambda e, c=c: e.tensor_single_scalar(mtmp[:], iotaD[:], float(c), ALU.is_equal),
             reads=[b_iotaD], writes=[b_mtmp])
        p.op("dve", lambda e, c=c: e.tensor_single_scalar(mtmp2[:], iotaD[:], float(c - 128), ALU.is_equal),
             reads=[b_iotaD], writes=[b_mtmp2])
        p.op("dve", lambda e: e.tensor_add(mtmp[:], mtmp[:], mtmp2[:]),
             reads=[b_mtmp, b_mtmp2], writes=[b_mtmp])
        p.op("dve", lambda e, ci=ci: e.scalar_tensor_tensor(
            stenf[:], mtmp[:], wmix_c[:, ci:ci + 1], stenf[:], ALU.mult, ALU.add),
            reads=[b_mtmp, b_wmix_c, b_stenf], writes=[b_stenf])
    p.op("act", lambda e: e.activation(stencil[:], stenf[:], AF.Copy, scale=SCALE),
         reads=[b_stenf], writes=[b_stencil])

    p.op("dve", lambda e: e.memset(amaskf[:], 0.0), writes=[b_amaskf])
    for si in range(NSEG):
        for dlt in SEG_DIAGS[si]:
            p.op("dve", lambda e, si=si, dlt=dlt: e.tensor_single_scalar(
                mtmp[:], iotaD[:], float(dlt), ALU.is_equal),
                reads=[b_iotaD], writes=[b_mtmp])
            p.op("dve", lambda e, si=si: e.tensor_max(
                amaskf[:, si * 128:(si + 1) * 128],
                amaskf[:, si * 128:(si + 1) * 128], mtmp[:]),
                reads=[b_amaskf, b_mtmp], writes=[b_amaskf])
    p.op("dve", lambda e: e.tensor_copy(amask[:], amaskf[:]),
         reads=[b_amaskf], writes=[b_amask])

    es0.close()
    p.barrier()

    es1 = ExitStack()
    sb1 = lambda n, s, d=F32: es1.enter_context(nc.sbuf_tensor(n, s, d))
    rowbuf = sb1("rowbuf", [1, T], F32)
    b_rowbuf = B("rowbuf")
    xh = [sb1(f"xh{c}", [128, T], BF16) for c in range(FC)]
    b_xh = [B(f"xh{c}") for c in range(FC)]
    def row(i):
        return rowbuf[0:1, 0:T]
    def rowslc(i, a, ln):
        return rowbuf[0:1, a:a + ln]
    for c in range(FC):
        p.dma("pool", out=xh[c][:], in_=xT[c * 128:(c + 1) * 128, :],
              writes=[b_xh[c]], key=f"xh{c % 2}")

    wqb = [sb1(f"wqb{i}", [128, FC * 128], BF16) for i in range(2)]
    b_wqb = [B(f"wqb{i}") for i in range(2)]
    wvb = [sb1("wvb0", [128, FC * 512], BF16)]
    b_wvb = [B("wvb0")]
    sqv = [sb1(f"sqv{i}", [128, 512], BF16) for i in range(2)]
    b_sqv = [B(f"sqv{i}") for i in range(2)]
    pj_sb = [sb1(f"pj_sb{i}", [128, 512], BF16) for i in range(4)]
    b_pj = [B(f"pj_sb{i}") for i in range(4)]
    bv_r = sb1("bv_r", [1, DM], BF16)
    b_bv_r = B("bv_r")
    p.dma("pool", out=bv_r[:], in_=bvr[:, :], writes=[b_bv_r])

    def qk_sweep(which):
        wsrc = wqm if which == "q" else wkm
        bias_r = bq_r if which == "q" else bk_r
        b_bias = b_bq_r if which == "q" else b_bk_r
        dst = qpb_s if which == "q" else kpb_s
        b_dst = b_qpb if which == "q" else b_kpb
        rowi = ROW_Q if which == "q" else ROW_K
        for g in range(2):
            for m in range(H):
                s = m % 2
                p.dma("pool", out=wqb[s][:].rearrange("p (fc q) -> p fc q", q=128),
                      in_=wsrc[m].rearrange("fc p q -> p fc q"),
                      writes=[b_wqb[s]], key=f"wqk{s}")
                for tci in range(4):
                    t0 = g * 2048 + tci * 512
                    bank = [0, 1, 2, 7][tci]
                    def proj(e, m=m, s=s, t0=t0, bank=bank):
                        for fc in range(FC):
                            e.matmul(pb[bank][:, :],
                                     wqb[s][:, fc * 128:(fc + 1) * 128],
                                     xh[fc][:, t0:t0 + 512],
                                     start=(fc == 0), stop=False)
                        return e.matmul(pb[bank][:, :],
                                        bias_r[0:1, m * 128:(m + 1) * 128],
                                        ones_r512[0:1, :],
                                        start=False, stop=True)
                    p.op("pe", proj, reads=[b_wqb[s], b_bias, b_ones_r512] + b_xh,
                         writes=[b_pb[bank]])
                    p.op("act", lambda e, bank=bank, tci=tci: e.activation(
                        pj_sb[tci][:], pb[bank][:, :], AF.Copy),
                        reads=[b_pb[bank]], writes=[b_pj[tci]])
                    p.op("act", lambda e, bank=bank, tci=tci: e.activation(
                        sqv[tci % 2][:], pb[bank][:, :], AF.Square),
                        reads=[b_pb[bank]], writes=[b_sqv[tci % 2]])
                    p.dma("sp", out=dst[m * 128:(m + 1) * 128, t0:t0 + 512],
                          in_=pj_sb[tci][:], reads=[b_pj[tci]], writes=[b_dst],
                          key=f"st_{which}{tci}")
                    rb = 3 + tci
                    p.op("pe", lambda e, rb=rb, m=m, tci=tci: e.matmul(
                        pb[rb][0:1, :], ones_col[:, 0:1], sqv[tci % 2][:],
                        start=(m == 0), stop=(m == H - 1)),
                        reads=[b_ones_col, b_sqv[tci % 2]], writes=[b_pb[rb]])
            for tci in range(4):
                t0 = g * 2048 + tci * 512
                rb = 3 + tci
                p.op("act", lambda e, rb=rb, t0=t0, rowi=rowi: e.activation(
                    rowslc(rowi, t0, 512), pb[rb][0:1, :], AF.Copy),
                    reads=[b_pb[rb]], writes=[b_rowbuf])
        p.op("act", lambda e: e.activation(
            row(0), row(0), AF.Ln, scale=1.0 / DM, bias=EPS_RMS),
            reads=[b_rowbuf, b_constcols], writes=[b_rowbuf])
        p.op("act", lambda e: e.activation(
            row(0), row(0), AF.Exp, scale=-0.5),
            reads=[b_rowbuf], writes=[b_rowbuf])
        p.op("act", lambda e, rowi=rowi: e.activation(
            rstd_rb[0:1, rowi * T:(rowi + 1) * T], row(0), AF.Copy),
            reads=[b_rowbuf], writes=[b_rstd_rb])

    def v_sweep():
        for g in range(2):
            for dc in range(4):
                s = 0
                p.dma("pool", out=wvb[s][:].rearrange("p (fc q) -> p fc q", q=512),
                      in_=wvdc[dc].rearrange("fc p q -> p fc q"),
                      writes=[b_wvb[s]], key=f"wv{s}")
                for tt in range(16):
                    t0 = g * 2048 + tt * 128
                    bank = [0, 1, 2, 7][tt % 4]
                    def vproj(e, s=s, t0=t0, dc=dc, bank=bank):
                        last = None
                        for fc in range(FC):
                            last = e.matmul(pb[bank][:, :],
                                            xh[fc][:, t0:t0 + 128],
                                            wvb[s][:, fc * 512:(fc + 1) * 512],
                                            start=(fc == 0), stop=False)
                        return e.matmul(pb[bank][:, :], ones_row[0:1, :],
                                        bv_r[0:1, dc * 512:(dc + 1) * 512],
                                        start=False, stop=True)
                    p.op("pe", vproj,
                         reads=[b_wvb[s], b_ones_row, b_bv_r] + b_xh,
                         writes=[b_pb[bank]])
                    st = tt % 4
                    p.op("act", lambda e, bank=bank, st=st: e.activation(
                        pj_sb[st][:], pb[bank][:, :], AF.Copy),
                        reads=[b_pb[bank]], writes=[b_pj[st]])
                    for hh in range(4):
                        h = dc * 4 + hh
                        p.dma("sp",
                              out=vtm_s[h, t0:t0 + 128, :],
                              in_=pj_sb[st][:, hh * 128:(hh + 1) * 128],
                              reads=[b_pj[st]], writes=[b_vtm], key=f"st_v{st}")

    qk_sweep("q")
    qk_sweep("k")
    v_sweep()



    es1.close()
    p.barrier()
    es2 = ExitStack()
    sb2 = lambda n, s, d=F32: es2.enter_context(nc.sbuf_tensor(n, s, d))

    qpbh = [sb2(f"qpbh{i}", [128, T], BF16) for i in range(2)]
    kpbh = [sb2(f"kpbh{i}", [128, T], BF16) for i in range(2)]
    vtmh = [sb2(f"vtmh{i}", [128, NT * 132], BF16) for i in range(2)]
    b_qpbh = [B(f"qpbh{i}") for i in range(2)]
    b_kpbh = [B(f"kpbh{i}") for i in range(2)]
    b_vtmh = [B(f"vtmh{i}") for i in range(2)]
    for i in range(2):
        p.op("dve", lambda e, i=i: e.memset(vtmh[i][:], 1.0), writes=[b_vtmh[i]])
    qn = sb2("qn", [128, T], BF16); b_qn = B("qn")
    kn = sb2("kn", [128, T], BF16); b_kn = B("kn")
    qm = sb2("qm", [128, T], BF16); b_qm = B("qm")
    emr = [sb2(f"emr{i}", [128, NSEG * 128], BF16) for i in range(2)]
    b_emr = [B(f"emr{i}") for i in range(2)]
    e_all = sb2("e_all", [128, NT * NSEG * 128], BF16); b_e_all = B("e_all")
    av_sb = sb2("av_sb", [128, 2048], BF16); b_av_sb = B("av_sb")
    chain = sb2("chain", [128, 1], F32); b_chain = B("chain")
    brow = sb2("brow", [1, 128], F32); b_brow = B("brow")
    bval = sb2("bval", [1, 1], F32); b_bval = B("bval")
    bvalb = sb2("bvalb", [1, 1], BF16); b_bvalb = B("bvalb")
    negb = sb2("negb", [128, 1], F32); b_negb = B("negb")
    zr16 = sb2("zr16", [128, 16], F32); b_zr16 = B("zr16")
    mcol = sb2("mcol", [128, 1], F32); b_mcol = B("mcol")

    for h in range(H):
        s = h % 2
        p.dma("act", out=qpbh[s][:], in_=qpb_s[h * 128:(h + 1) * 128, :],
              reads=[b_qpb], writes=[b_qpbh[s]])
        p.dma("act", out=kpbh[s][:], in_=kpb_s[h * 128:(h + 1) * 128, :],
              reads=[b_kpb], writes=[b_kpbh[s]])
        p.dma("act", out=vtmh[s][:].rearrange("p (tt q) -> p tt q", q=132)[:, :, 0:128],
              in_=vtm_s[h].rearrange("(tt p) q -> p tt q", p=128),
              reads=[b_vtm], writes=[b_vtmh[s]])
        for which in ("q", "k"):
            srcb = qpbh[s] if which == "q" else kpbh[s]
            b_srcb = b_qpbh[s] if which == "q" else b_kpbh[s]
            dstb = qn if which == "q" else kn
            b_dstb = b_qn if which == "q" else b_kn
            g_c = gq_c if which == "q" else gk_c
            b_g_c = b_gq_c if which == "q" else b_gk_c
            ri = ROW_Q if which == "q" else ROW_K
            for tc in range(8):
                t0 = tc * 512
                p.op("pe", lambda e, ri=ri, t0=t0: e.matmul(
                    pb[6][:, :], ones_row[0:1, :],
                    rstd_rb[0:1, ri * T + t0:ri * T + t0 + 512],
                    start=True, stop=True),
                    reads=[b_ones_row, b_rstd_rb], writes=[b_pb[6]])
                p.op("dve", lambda e, t0=t0, srcb=srcb, dstb=dstb, g_c=g_c, h=h:
                     e.scalar_tensor_tensor(
                         dstb[:, t0:t0 + 512], srcb[:, t0:t0 + 512],
                         g_c[:, h:h + 1], pb[6][:, :], ALU.mult, ALU.mult),
                     reads=[b_srcb, b_g_c, b_pb[6]], writes=[b_dstb])
        for tc in range(8):
            t0 = tc * 512
            p.op("pe", lambda e, t0=t0: e.matmul(
                pb[6][:, :], stencil[:, :], qn[:, t0:t0 + 512],
                start=True, stop=True),
                reads=[b_stencil, b_qn], writes=[b_pb[6]])
            p.op("act", lambda e, t0=t0: e.activation(
                qm[:, t0:t0 + 512], pb[6][:, :], AF.Copy),
                reads=[b_pb[6]], writes=[b_qm])
        p.op("dve", lambda e: e.memset(chain[:], -1e30), writes=[b_chain])
        for sc in (0, 8, 16, 24):
            p.op("pe", lambda e, sc=sc: e.matmul(
                pb[6][:, 0:128], kn[:, sc * 128:sc * 128 + 128],
                qm[:, sc * 128:sc * 128 + 128], start=True, stop=True),
                reads=[b_kn, b_qm], writes=[b_pb[6]])
            p.op("dve", lambda e: e.reduce_max(mcol[:, 0:1], pb[6][:, 0:128], axis=AX.X),
                 reads=[b_pb[6]], writes=[b_mcol])
            p.op("dve", lambda e: e.tensor_max(chain[:], chain[:], mcol[:]),
                 reads=[b_chain, b_mcol], writes=[b_chain])
        p.op("pe", lambda e: e.transpose(pb[6][0:1, 0:128], chain[:, 0:1], identf[:]),
             reads=[b_chain, b_identf], writes=[b_pb[6]])
        p.op("act", lambda e: e.activation(brow[:], pb[6][0:1, 0:128], AF.Copy),
             reads=[b_pb[6]], writes=[b_brow])
        p.op("dve", lambda e: e.reduce_max(bval[:, 0:1], brow[:], axis=AX.X),
             reads=[b_brow], writes=[b_bval])
        p.op("dve", lambda e: e.tensor_copy(bvalb[:], bval[:]),
             reads=[b_bval], writes=[b_bvalb])
        p.op("pe", lambda e: e.matmul(pb[6][:, 0:1], ones_row[0:1, :], bvalb[0:1, :],
                                      start=True, stop=True),
             reads=[b_ones_row, b_bvalb], writes=[b_pb[6]])
        p.op("act", lambda e: e.activation(negb[:], pb[6][:, 0:1], AF.Copy,
                                           scale=-1.0, bias=-5.0),
             reads=[b_pb[6], b_constcols], writes=[b_negb])

        for kc in range(NT):
            sbank = 2 + (kc % 2) * 2
            def smm(e, kc=kc, sbank=sbank):
                last = None
                pieces = []
                for (a2, l2) in circ_runs(((kc - 3) % NT) * 128, 896):
                    pieces.append((a2, l2))
                off = 0
                for (a2, l2) in pieces:
                    while l2 > 0:
                        bank = sbank + (1 if off >= 512 else 0)
                        boff = off - 512 if off >= 512 else off
                        take = min(l2, 512 - boff)
                        last = e.matmul(pb[bank][:, boff:boff + take],
                                        kn[:, kc * 128:kc * 128 + 128],
                                        qm[:, a2:a2 + take],
                                        start=True, stop=True)
                        off += take
                        a2 += take
                        l2 -= take
                q8 = ((kc + 8) % NT) * 128
                last = e.matmul(pb[sbank + 1][:, 384:512],
                                kn[:, kc * 128:kc * 128 + 128],
                                qm[:, q8:q8 + 128], start=True, stop=True)
                return last
            p.op("pe", smm, reads=[b_kn, b_qm],
                 writes=[b_pb[sbank], b_pb[sbank + 1]])
            kp = kc % 2
            for hf in range(2):
                p.op("act", lambda e, hf=hf, sbank=sbank, kp=kp: e.activation(
                    emr[kp][:, hf * 512:(hf + 1) * 512], pb[sbank + hf][:, :],
                    AF.Exp, bias=negb[:, 0:1]),
                    reads=[b_pb[sbank + hf], b_negb], writes=[b_emr[kp]])
            p.op("dve" if kp == 0 else "pool", lambda e, kc=kc, kp=kp: e.tensor_mul(
                e_all[:, kc * 1024:(kc + 1) * 1024], emr[kp][:], amask[:]),
                reads=[b_emr[kp], b_amask], writes=[b_e_all])
        for tt in range(NT):
            avbank = tt % 2
            def avz(e, tt=tt, avbank=avbank, s=s):
                last = None
                for j, dseg in enumerate(SEGS):
                    kc = (tt - dseg) % NT
                    lw = e_all[:, (kc * 8 + j) * 128:(kc * 8 + j) * 128 + 128]
                    last = e.matmul(pb[avbank][:, 0:129], lw,
                                    vtmh[s][:, kc * 132:kc * 132 + 129],
                                    start=(j == 0), stop=(j == NSEG - 1))
                return last
            p.op("pe", avz, reads=[b_e_all, b_vtmh[s]],
                 writes=[b_pb[avbank]])
            p.op("dve", lambda e, avbank=avbank: e.reciprocal(
                zr16[:, 0:1], pb[avbank][:, 128:129]),
                reads=[b_pb[avbank]], writes=[b_zr16])
            p.op("act", lambda e, tt=tt, avbank=avbank: e.activation(
                av_sb[:, (tt % 16) * 128:(tt % 16 + 1) * 128],
                pb[avbank][:, 0:128], AF.Copy, scale=zr16[:, 0:1]),
                reads=[b_pb[avbank], b_zr16], writes=[b_av_sb])
            p.dma("sp", out=avtm_s[tt * 128:(tt + 1) * 128, h * 128:(h + 1) * 128],
                  in_=av_sb[:, (tt % 16) * 128:(tt % 16 + 1) * 128],
                  reads=[b_av_sb], writes=[b_avtm], key="st_av")

    es2.close()
    p.barrier()

    es3 = ExitStack()
    sb3 = lambda n, s, d=F32: es3.enter_context(nc.sbuf_tensor(n, s, d))
    avf = [sb3(f"avf{dc}", [128, T], BF16) for dc in range(FC)]
    b_avf = [B(f"avf{dc}") for dc in range(FC)]
    for dc in range(FC):
        p.dma_transpose("sp", out=avf[dc][:],
                        in_=avtm_s[:, dc * 128:(dc + 1) * 128],
                        reads=[b_avtm], writes=[b_avf[dc]], key=f"avf{dc % 2}")
    wob = [sb3(f"wob{i}", [128, FC * 128], BF16) for i in range(2)]
    b_wob = [B(f"wob{i}") for i in range(2)]
    xfb = [sb3(f"xfb{i}", [128, 512], F32) for i in range(2)]
    b_xfb = [B(f"xfb{i}") for i in range(2)]
    r1c = sb3("r1c", [128, 512], F32); b_r1c = B("r1c")
    r1bc = sb3("r1bc", [128, 512], BF16); b_r1bc = B("r1bc")

    for m in range(H):
        s = m % 2
        p.dma("pool", out=wob[s][:].rearrange("p (fc q) -> p fc q", q=128),
              in_=wom[m].rearrange("fc p q -> p fc q"),
              writes=[b_wob[s]], key=f"wob{s}")
        for tg in range(2):
            tg0 = tg * 2048
            bwr = [b_pb[j] for j in range(4)]
            def womm(e, s=s, tg0=tg0):
                last = None
                for dc in range(FC):
                    lw = wob[s][:, dc * 128:(dc + 1) * 128]
                    for tq in range(4):
                        last = e.matmul(pb[tq][:, :], lw,
                                        avf[dc][:, tg0 + tq * 512:tg0 + (tq + 1) * 512],
                                        start=(dc == 0), stop=(dc == FC - 1))
                return last
            p.op("pe", womm, reads=[b_wob[s]] + b_avf, writes=bwr)
            for tq in range(4):
                t0 = tg0 + tq * 512
                xs = tq % 2
                p.dma("act", out=xfb[xs][:], in_=xT[m * 128:(m + 1) * 128, t0:t0 + 512],
                      writes=[b_xfb[xs]], key=f"xfb{xs}")
                p.op("dve", lambda e, m=m, tq=tq, xs=xs: e.scalar_tensor_tensor(
                    r1c[:], pb[tq][:, :], bo_c[:, m:m + 1], xfb[xs][:],
                    ALU.add, ALU.add),
                    reads=[b_pb[tq], b_bo_c, b_xfb[xs]], writes=[b_r1c])
                p.dma("sp", out=r1_s[m * 128:(m + 1) * 128, t0:t0 + 512],
                      in_=r1c[:], reads=[b_r1c], writes=[b_r1], key="st_r1")
                p.op("act", lambda e: e.activation(r1bc[:], r1c[:], AF.Copy),
                     reads=[b_r1c], writes=[b_r1bc])
                p.dma("pool", out=r1b_s[m * 128:(m + 1) * 128, t0:t0 + 512],
                      in_=r1bc[:], reads=[b_r1bc], writes=[b_r1b], key="st_r1b")
    es3.close()
    p.barrier()

    def ln_stats(src_b16, b_src, rowbuf2, b_rowbuf2, scope_sb, tag):
        lb = [scope_sb(f"lnb{tag}{i}", [128, 512], BF16) for i in range(2)]
        b_lb = [B(f"lnb{tag}{i}") for i in range(2)]
        sqb = scope_sb(f"sqb{tag}", [128, 512], BF16)
        b_sqb = B(f"sqb{tag}")
        for tc in range(8):
            t0 = tc * 512
            for m in range(H):
                s = m % 2
                p.dma("act", out=lb[s][:], in_=src_b16[m * 128:(m + 1) * 128, t0:t0 + 512],
                      reads=[b_src], writes=[b_lb[s]], key=f"lnb{tag}{s}")
                p.op("act", lambda e, s=s: e.activation(sqb[:], lb[s][:], AF.Square),
                     reads=[b_lb[s]], writes=[b_sqb])
                p.op("pe", lambda e, s=s, m=m: e.matmul(
                    pb[2][0:1, :], ones_col[:, 0:1], lb[s][:],
                    start=(m == 0), stop=(m == H - 1)),
                    reads=[b_ones_col, b_lb[s]], writes=[b_pb[2]])
                p.op("pe", lambda e, m=m: e.matmul(
                    pb[3][0:1, :], ones_col[:, 0:1], sqb[:],
                    start=(m == 0), stop=(m == H - 1)),
                    reads=[b_ones_col, b_sqb], writes=[b_pb[3]])
            p.op("act", lambda e, t0=t0: e.activation(
                rowbuf2[0:1, t0:t0 + 512], pb[2][0:1, :], AF.Copy, scale=1.0 / DM),
                reads=[b_pb[2]], writes=[b_rowbuf2])
            p.op("act", lambda e, t0=t0: e.activation(
                rowbuf2[0:1, T + t0:T + t0 + 512], pb[3][0:1, :], AF.Copy, scale=1.0 / DM),
                reads=[b_pb[3]], writes=[b_rowbuf2])
        MU = rowbuf2[0:1, 0:T]
        MSQ = rowbuf2[0:1, T:2 * T]
        MU2 = rowbuf2[0:1, 2 * T:3 * T]
        p.op("act", lambda e: e.activation(MU2, MU, AF.Square),
             reads=[b_rowbuf2], writes=[b_rowbuf2])
        p.op("dve", lambda e: e.tensor_sub(MSQ, MSQ, MU2),
             reads=[b_rowbuf2], writes=[b_rowbuf2])
        p.op("act", lambda e: e.activation(MSQ, MSQ, AF.Ln, bias=EPS_LN),
             reads=[b_rowbuf2, b_constcols], writes=[b_rowbuf2])
        p.op("act", lambda e: e.activation(MSQ, MSQ, AF.Exp, scale=-0.5),
             reads=[b_rowbuf2], writes=[b_rowbuf2])
        p.op("dve", lambda e: e.tensor_mul(MU2, MU, MSQ),
             reads=[b_rowbuf2], writes=[b_rowbuf2])
        return MSQ, MU2

    def ln_apply(src_f32, b_src, rstd_row, b2_row, b_rows, dst, b_dst,
                 dst_b16, b_dst_b16, scope_sb, tag):
        rb16 = scope_sb(f"lnr{tag}", [1, 2 * T], BF16)
        b_rb16 = B(f"lnr{tag}")
        p.op("act", lambda e: e.activation(rb16[0:1, 0:T], rstd_row, AF.Copy),
             reads=[b_rows], writes=[b_rb16])
        p.op("act", lambda e: e.activation(rb16[0:1, T:2 * T], b2_row, AF.Copy),
             reads=[b_rows], writes=[b_rb16])
        lc = [scope_sb(f"lnc{tag}{i}", [128, 512], F32) for i in range(2)]
        b_lc = [B(f"lnc{tag}{i}") for i in range(2)]
        ot = scope_sb(f"lno{tag}", [128, 512], F32)
        b_ot = B(f"lno{tag}")
        ob = scope_sb(f"lnob{tag}", [128, 512], BF16)
        b_ob = B(f"lnob{tag}")
        for tc in range(8):
            t0 = tc * 512
            p.op("pe", lambda e, t0=t0: e.matmul(
                pb[2][:, :], ones_row[0:1, :], rb16[0:1, t0:t0 + 512],
                start=True, stop=True),
                reads=[b_ones_row, b_rb16], writes=[b_pb[2]])
            p.op("pe", lambda e, t0=t0: e.matmul(
                pb[3][:, :], ones_row[0:1, :], rb16[0:1, T + t0:T + t0 + 512],
                start=True, stop=True),
                reads=[b_ones_row, b_rb16], writes=[b_pb[3]])
            for m in range(H):
                s = m % 2
                p.dma("act", out=lc[s][:], in_=src_f32[m * 128:(m + 1) * 128, t0:t0 + 512],
                      reads=[b_src], writes=[b_lc[s]], key=f"lnc{tag}{s}")
                p.op("dve", lambda e, s=s: e.tensor_mul(ot[:], lc[s][:], pb[2][:, :]),
                     reads=[b_lc[s], b_pb[2]], writes=[b_ot])
                p.op("dve", lambda e: e.tensor_sub(ot[:], ot[:], pb[3][:, :]),
                     reads=[b_ot, b_pb[3]], writes=[b_ot])
                p.dma("sp", out=dst[m * 128:(m + 1) * 128, t0:t0 + 512],
                      in_=ot[:], reads=[b_ot], writes=[b_dst], key=f"lnst{tag}")
                if dst_b16 is not None:
                    p.op("act", lambda e: e.activation(ob[:], ot[:], AF.Copy),
                         reads=[b_ot], writes=[b_ob])
                    p.dma("pool", out=dst_b16[m * 128:(m + 1) * 128, t0:t0 + 512],
                          in_=ob[:], reads=[b_ob], writes=[b_dst_b16],
                          key=f"lnstb{tag}")

    es3b = ExitStack()
    sb3b = lambda n, s, d=F32: es3b.enter_context(nc.sbuf_tensor(n, s, d))
    rowbuf2 = sb3b("rowbuf2", [1, 3 * T], F32)
    b_rowbuf2 = B("rowbuf2")
    rstd1, b21 = ln_stats(r1b_s, b_r1b, rowbuf2, b_rowbuf2, sb3b, "1")
    ln_apply(r1_s, b_r1, rstd1, b21, b_rowbuf2, x1_s, b_x1, x1b_s, b_x1b,
             sb3b, "1")
    es3b.close()
    p.barrier()

    es4 = ExitStack()
    sb4 = lambda n, s, d=F32: es4.enter_context(nc.sbuf_tensor(n, s, d))
    rw_sb = sb4("rw_sb", [128, FC * E], BF16)
    b_rw = B("rw_sb")
    p.dma("pool", out=rw_sb[:].rearrange("p (fc e) -> p fc e", e=E),
          in_=rwT.rearrange("(fc p) e -> p fc e", p=128),
          writes=[b_rw])
    x4b = [sb4(f"x4b{i}", [128, 512], BF16) for i in range(2)]
    b_x4b = [B(f"x4b{i}") for i in range(2)]
    s_f = sb4("s_f", [16, T], F32); b_s_f = B("s_f")
    id16 = sb4("id16", [16, 16], F32); b_id16 = B("id16")
    id16i = sb4("id16i", [16, 16], F32); b_id16i = B("id16i")
    p.op("pool", lambda e: e.iota(id16i[:], pattern=[[1, 16]], base=0,
                                  channel_multiplier=-1,
                                  allow_small_or_imprecise_dtypes=True),
         writes=[b_id16i])
    p.op("dve", lambda e: e.tensor_single_scalar(id16[:], id16i[:], 0.0, ALU.is_equal),
         reads=[b_id16i], writes=[b_id16])
    wT_sb = sb4("wT_sb", [16, T], BF16); b_wT = B("wT_sb")
    s_tm = sb4("s_tm", [128, E], F32); b_s_tm = B("s_tm")
    h16r = sb4("h16r", [128, E], F32); b_h16r = B("h16r")
    t8r = sb4("t8r", [128, 8], F32); b_t8r = B("t8r")
    t8c = sb4("t8c", [128, 8], F32); b_t8c = B("t8c")
    mask16r = sb4("mask16r", [128, E], F32); b_mask16r = B("mask16r")
    wur = sb4("wur", [128, E], F32); b_wur = B("wur")
    wsum = sb4("wsum", [128, 2], F32); b_wsum = B("wsum")
    w_tmr = sb4("w_tmr", [128, E], F32); b_w_tmr = B("w_tmr")
    for tc in range(8):
        t0 = tc * 512
        xs = tc % 2
        for fc in range(FC):
            p.dma("act", out=x4b[xs][:], in_=x1b_s[fc * 128:(fc + 1) * 128, t0:t0 + 512],
                  reads=[b_x1b], writes=[b_x4b[xs]], key=f"x4b{xs}")
            p.op("pe", lambda e, fc=fc, xs=xs: e.matmul(
                pb[2][0:16, :], rw_sb[:, fc * E:(fc + 1) * E], x4b[xs][:],
                start=(fc == 0), stop=(fc == FC - 1)),
                reads=[b_rw, b_x4b[xs]], writes=[b_pb[2]])
            xs = (xs + 1) % 2
        p.op("act", lambda e, t0=t0: e.activation(
            s_f[0:16, t0:t0 + 512], pb[2][0:16, :], AF.Sigmoid),
            reads=[b_pb[2]], writes=[b_s_f])
    for tt in range(NT):
        t0 = tt * 128
        p.op("pe", lambda e, t0=t0: e.transpose(
            pb[3][:, 0:16], s_f[0:16, t0:t0 + 128], id16[:, :]),
            reads=[b_s_f, b_id16], writes=[b_pb[3]])
        p.op("dve", lambda e: e.tensor_copy(s_tm[:], pb[3][:, 0:16]),
             reads=[b_pb[3]], writes=[b_s_tm])
        p.op("dve", lambda e: e.tensor_add(h16r[:], s_tm[:], eb_b[:]),
             reads=[b_s_tm, b_eb_b], writes=[b_h16r])
        p.op("dve", lambda e: e.max(t8r[:], h16r[:]),
             reads=[b_h16r], writes=[b_t8r])
        p.op("act", lambda e: e.copy(t8c[:], t8r[:]),
             reads=[b_t8r], writes=[b_t8c])
        p.op("dve", lambda e: e.tensor_single_scalar(
            mask16r[:], h16r[:], t8c[:, 3:4], ALU.is_ge),
            reads=[b_h16r, b_t8c], writes=[b_mask16r])
        p.op("dve", lambda e: e.tensor_mul(wur[:], mask16r[:], s_tm[:]),
             reads=[b_mask16r, b_s_tm], writes=[b_wur])
        p.op("dve", lambda e: e.reduce_sum(wsum[:, 0:1], wur[:], axis=AX.X),
             reads=[b_wur], writes=[b_wsum])
        p.op("dve", lambda e: e.tensor_single_scalar(
            wsum[:, 1:2], wsum[:, 0:1], 1e-20, ALU.add),
            reads=[b_wsum], writes=[b_wsum])
        p.op("dve", lambda e: e.reciprocal(wsum[:, 0:1], wsum[:, 1:2]),
             reads=[b_wsum], writes=[b_wsum])
        p.op("dve", lambda e: e.tensor_scalar_mul(w_tmr[:], wur[:], wsum[:, 0:1]),
             reads=[b_wur, b_wsum], writes=[b_w_tmr])
        p.op("pe", lambda e: e.transpose(pb[3][0:16, 0:128], w_tmr[:, 0:16], identf[:]),
             reads=[b_w_tmr, b_identf], writes=[b_pb[3]])
        p.op("act", lambda e, t0=t0: e.activation(
            wT_sb[0:16, t0:t0 + 128], pb[3][0:16, 0:128], AF.Copy),
            reads=[b_pb[3]], writes=[b_wT])

    gupb = [sb4("gupb0", [128, FC * 2 * I], BF16)]
    b_gupb = [B("gupb0")]
    wrow = [sb4(f"wrow{i}", [1, 2048], BF16) for i in range(2)]
    b_wrow = [B(f"wrow{i}") for i in range(2)]
    x1w = [sb4(f"x1w{c}", [128, 2048], BF16) for c in range(FC)]
    b_x1w = [B(f"x1w{c}") for c in range(FC)]
    sg_sb = sb4("sg_sb", [128, IC * 2048], BF16); b_sg = B("sg_sb")
    wbc_sb = sb4("wbc_sb", [128, 2048], BF16); b_wbc = B("wbc_sb")
    mid_sb = sb4("mid_sb", [128, 2048], BF16); b_mid_sb = B("mid_sb")

    for tcg in range(2):
        tg0 = tcg * 2048
        for c in range(FC):
            p.dma("act", out=x1w[c][:], in_=x1b_s[c * 128:(c + 1) * 128, tg0:tg0 + 2048],
                  reads=[b_x1b], writes=[b_x1w[c]], key=f"x1w{c % 2}")
        for e_i in range(17):
            s = 0
            if e_i == 16:
                p.dma("pool", out=gupb[s][:].rearrange("p (fc g) -> p fc g", g=2 * I),
                      in_=sgu.rearrange("fc p g -> p fc g"),
                      writes=[b_gupb[s]], key=f"gupb{s}")
            if e_i < 16:
                p.dma("pool", out=gupb[s][:].rearrange("p (fc g) -> p fc g", g=2 * I),
                      in_=gup[e_i].rearrange("fc p g -> p fc g"),
                      writes=[b_gupb[s]], key=f"gupb{s}")
                p.dma("pool", out=wrow[e_i % 2][:], in_=wT_sb[e_i:e_i + 1, tg0:tg0 + 2048],
                      reads=[b_wT], writes=[b_wrow[e_i % 2]], key=f"wrow{e_i % 2}")
                for tq in range(4):
                    p.op("pe", lambda e, e_i=e_i, tq=tq: e.matmul(
                        pb[4 + tq][:, :], ones_row[0:1, :],
                        wrow[e_i % 2][0:1, tq * 512:(tq + 1) * 512],
                        start=True, stop=True),
                        reads=[b_ones_row, b_wrow[e_i % 2]], writes=[b_pb[4 + tq]])
                    p.op("act", lambda e, tq=tq: e.activation(
                        wbc_sb[:, tq * 512:(tq + 1) * 512], pb[4 + tq][:, :], AF.Copy),
                        reads=[b_pb[4 + tq]], writes=[b_wbc])
            for ic in range(IC):
                bs = (ic % 2) * 4
                bwr = [b_pb[bs + j] for j in range(4)]
                def gmm(e, s=s, ic=ic, bs=bs):
                    last = None
                    for fc in range(FC):
                        lw = gupb[s][:, fc * 2 * I + ic * 128:fc * 2 * I + (ic + 1) * 128]
                        for tq in range(4):
                            last = e.matmul(pb[bs + tq][:, :], lw,
                                            x1w[fc][:, tq * 512:(tq + 1) * 512],
                                            start=(fc == 0), stop=(fc == FC - 1))
                    return last
                p.op("pe", gmm, reads=[b_gupb[s]] + b_x1w, writes=bwr)
                for tq in range(4):
                    p.op("act", lambda e, ic=ic, bs=bs, tq=tq: e.activation(
                        sg_sb[:, ic * 2048 + tq * 512:ic * 2048 + (tq + 1) * 512],
                        pb[bs + tq][:, :], AF.Silu),
                        reads=[b_pb[bs + tq]], writes=[b_sg])
            if e_i < 16:
                for ic in range(IC):
                    p.op("dve", lambda e, ic=ic: e.tensor_mul(
                        sg_sb[:, ic * 2048:(ic + 1) * 2048],
                        sg_sb[:, ic * 2048:(ic + 1) * 2048], wbc_sb[:]),
                        reads=[b_sg, b_wbc], writes=[b_sg])
            for ic in range(IC):
                bs = (ic % 2) * 4
                bwr = [b_pb[bs + j] for j in range(4)]
                def umm(e, s=s, ic=ic, bs=bs):
                    last = None
                    for fc in range(FC):
                        lw = gupb[s][:, fc * 2 * I + I + ic * 128:fc * 2 * I + I + (ic + 1) * 128]
                        for tq in range(4):
                            last = e.matmul(pb[bs + tq][:, :], lw,
                                            x1w[fc][:, tq * 512:(tq + 1) * 512],
                                            start=(fc == 0), stop=(fc == FC - 1))
                    return last
                p.op("pe", umm, reads=[b_gupb[s]] + b_x1w, writes=bwr)
                for tq in range(4):
                    p.op("dve", lambda e, ic=ic, bs=bs, tq=tq: e.tensor_mul(
                        mid_sb[:, tq * 512:(tq + 1) * 512],
                        sg_sb[:, ic * 2048 + tq * 512:ic * 2048 + (tq + 1) * 512],
                        pb[bs + tq][:, :]),
                        reads=[b_sg, b_pb[bs + tq]], writes=[b_mid_sb])
                p.dma("sp", out=mid_s[e_i, ic * 128:(ic + 1) * 128, tg0:tg0 + 2048],
                      in_=mid_sb[:], reads=[b_mid_sb], writes=[b_mid], key="st_mid")
    es4.close()
    p.barrier()

    es4b = ExitStack()
    sb4b = lambda n, s, d=F32: es4b.enter_context(nc.sbuf_tensor(n, s, d))
    dcast = sb4b("dcast", [128, 102 * 128], BF16)
    b_dcast = B("dcast")
    for m in range(H):
        p.dma("pool", out=dcast[:], in_=dallm[m].rearrange("p kc q -> p (kc q)"),
              writes=[b_dcast], key="dcast_ld")
        p.dma("sp", out=dbf_s[m], in_=dcast[:],
              reads=[b_dcast], writes=[b_dbf], key="dcast_st")
    es4b.close()
    p.barrier()

    es4c = ExitStack()
    sb4c = lambda n, s, d=F32: es4c.enter_context(nc.sbuf_tensor(n, s, d))
    midwA = sb4c("midwA", [128, 48 * 512], BF16); b_midwA = B("midwA")
    midwB = sb4c("midwB", [128, 54 * 512], BF16); b_midwB = B("midwB")
    dwb = [sb4c(f"dwb{i}", [128, 102 * 128], BF16) for i in range(2)]
    b_dwb = [B(f"dwb{i}") for i in range(2)]
    x1fb = [sb4c(f"x1fb{i}", [128, 512], F32) for i in range(2)]
    b_x1fb = [B(f"x1fb{i}") for i in range(2)]
    r2c = [sb4c(f"r2c{i}", [128, 512], F32) for i in range(2)]
    b_r2c = [B(f"r2c{i}") for i in range(2)]
    r2bc = [sb4c(f"r2bc{i}", [128, 512], BF16) for i in range(2)]
    b_r2bc = [B(f"r2bc{i}") for i in range(2)]

    for tc in range(8):
        t0 = tc * 512
        for e_i in range(17):
            half = 0 if e_i < 8 else 1
            dst = midwA if half == 0 else midwB
            bdst = b_midwA if half == 0 else b_midwB
            base = e_i * IC if half == 0 else (e_i - 8) * IC
            p.dma("sp", out=dst[:, base * 512:(base + IC) * 512]
                  .rearrange("p (ic t) -> p ic t", t=512),
                  in_=mid_s[e_i, :, t0:t0 + 512].rearrange("(ic p) t -> p ic t", p=128),
                  reads=[b_mid], writes=[bdst], key="midw" + ("A" if half == 0 else "B"))
        for m in range(H):
            s = m % 2
            p.dma("sp", out=dwb[s][:], in_=dbf_s[m],
                  reads=[b_dbf], writes=[b_dwb[s]], key=f"dwb{s}")
            bank = m % 4
            def dmmA(e, s=s, bank=bank):
                last = None
                for eic in range(48):
                    last = e.matmul(pb[bank][:, :],
                                    dwb[s][:, eic * 128:(eic + 1) * 128],
                                    midwA[:, eic * 512:(eic + 1) * 512],
                                    start=(eic == 0), stop=False)
                return last
            def dmmB(e, s=s, bank=bank):
                last = None
                for eic in range(54):
                    last = e.matmul(pb[bank][:, :],
                                    dwb[s][:, (48 + eic) * 128:(48 + eic + 1) * 128],
                                    midwB[:, eic * 512:(eic + 1) * 512],
                                    start=False, stop=(eic == 53))
                return last
            p.op("pe", dmmA, reads=[b_dwb[s], b_midwA], writes=[b_pb[bank]])
            p.op("pe", dmmB, reads=[b_dwb[s], b_midwB], writes=[b_pb[bank]])
            xs = m % 2
            p.dma("pool", out=x1fb[xs][:], in_=x1_s[m * 128:(m + 1) * 128, t0:t0 + 512],
                  reads=[b_x1], writes=[b_x1fb[xs]], key=f"x1fb{xs}")
            p.op("dve", lambda e, bank=bank, xs=xs: e.tensor_add(
                r2c[xs][:], pb[bank][:, :], x1fb[xs][:]),
                reads=[b_pb[bank], b_x1fb[xs]], writes=[b_r2c[xs]])
            p.dma("pool", out=r2_s[m * 128:(m + 1) * 128, t0:t0 + 512],
                  in_=r2c[xs][:], reads=[b_r2c[xs]], writes=[b_r2], key=f"st_r2{xs}")
            p.op("act", lambda e, xs=xs: e.activation(r2bc[xs][:], r2c[xs][:], AF.Copy),
                 reads=[b_r2c[xs]], writes=[b_r2bc[xs]])
            p.dma("pool", out=r2b_s[m * 128:(m + 1) * 128, t0:t0 + 512],
                  in_=r2bc[xs][:], reads=[b_r2bc[xs]], writes=[b_r2b], key=f"st_r2b{xs}")
    es4c.close()
    p.barrier()

    es5 = ExitStack()
    sb5 = lambda n, s, d=F32: es5.enter_context(nc.sbuf_tensor(n, s, d))
    rowbuf3 = sb5("rowbuf3", [1, 3 * T], F32)
    b_rowbuf3 = B("rowbuf3")
    rstd2, b22 = ln_stats(r2b_s, b_r2b, rowbuf3, b_rowbuf3, sb5, "2")
    ln_apply(r2_s, b_r2, rstd2, b22, b_rowbuf3, out, b_out, None, None,
             sb5, "2")
    es5.close()
    p.barrier()
    out_names = ["out"]
    if "scratch" in dev_taps:
        out_names += ["qpb_s", "kpb_s", "vtm_s", "avtm_s", "x1_s", "mid_s"]
    if "rows" in dev_taps:
        rows_o = DT("rows_o", [1, 2 * T], F32, kind="ExternalOutput").ap()
        b_rows_o = B("rows_o")
        p.dma("sp", out=rows_o[:, :], in_=rowbuf[:], reads=[b_rowbuf],
              writes=[b_rows_o], key="st_rows")
        out_names.append("rows_o")
    p.finish("pool")
    p.emit()
    return nc, es, out_names


def _host_layout(inputs):
    x = inputs["L0_attn_x"]
    wq = np.asarray(inputs["L0_attn_wq"])
    wk = np.asarray(inputs["L0_attn_wk"])
    wv = np.asarray(inputs["L0_attn_wv"])
    wo = np.asarray(inputs["L0_attn_wo"])
    gup = np.asarray(inputs["L0_ffn_gate_up_proj"])
    down = np.asarray(inputs["L0_ffn_down_proj"])
    sgw = np.asarray(inputs["L0_ffn_shared_gate_w"])
    suw = np.asarray(inputs["L0_ffn_shared_up_w"])
    sdw = np.asarray(inputs["L0_ffn_shared_down_w"])

    def mblock(w):
        wT = np.ascontiguousarray(w.T)
        return np.ascontiguousarray(
            wT.reshape(FC, 128, H, 128).transpose(2, 0, 1, 3))

    wqm = mblock(wq); wkm = mblock(wk); wom = mblock(wo)
    wvT = np.ascontiguousarray(wv.T)
    wvdc = np.ascontiguousarray(
        wvT.reshape(FC, 128, 4, 512).transpose(2, 0, 1, 3))

    def cols(v):
        return np.ascontiguousarray(v.reshape(FC, 128).T)

    gupT = np.ascontiguousarray(
        gup.transpose(0, 2, 1).reshape(E, FC, 128, 2 * I))
    sguT = np.concatenate([sgw.T, suw.T], axis=1)
    sguT = np.ascontiguousarray(sguT.reshape(FC, 128, 2 * I))

    dT = np.concatenate([down[e].T for e in range(E)] + [sdw.T], axis=0)
    dallm = np.ascontiguousarray(
        dT.reshape(102, 128, H, 128).transpose(2, 1, 0, 3))

    base = {
        "wqm": wqm, "wkm": wkm, "wvdc": wvdc, "wom": wom,
        "bqr": np.asarray(inputs["L0_attn_bq"])[None, :],
        "bkr": np.asarray(inputs["L0_attn_bk"])[None, :],
        "boc": cols(np.asarray(inputs["L0_attn_bo"])),
        "gqc": cols(np.asarray(inputs["L0_attn_gq"])),
        "gkc": cols(np.asarray(inputs["L0_attn_gk"])),
        "bvr": np.asarray(inputs["L0_attn_bv"])[None, :],
        "wmix": np.asarray(inputs["L0_attn_w_mix"]),
        "rwT": np.ascontiguousarray(np.asarray(inputs["L0_ffn_router_w"]).T),
        "ebr": np.asarray(inputs["L0_ffn_e_bias"])[None, :],
        "gup": gupT, "sgu": sguT, "dallm": dallm,
    }
    in_maps = []
    for b in range(8):
        m = dict(base)
        m["xT"] = np.ascontiguousarray(np.asarray(x[b]).T)
        in_maps.append(m)
    return in_maps


def kernel(**inputs):
    in_maps = _host_layout(inputs)
    nc, es, out_names = build_program()
    try:
        res = run_bass_kernel_spmd(nc, in_maps, core_ids=list(range(8)))
    finally:
        es.close()
    outs = [np.ascontiguousarray(np.asarray(r["out"], np.float32).T)
            for r in res.results]
    return np.stack(outs, axis=0)


# version 08
# speedup vs baseline: 1.1384x; 1.0180x over previous
import math
from collections import defaultdict
from contextlib import ExitStack

import numpy as np
import concourse.bass as bass
import concourse.mybir as mybir
from concourse.bass_utils import run_bass_kernel_spmd

F32 = mybir.dt.float32
BF16 = mybir.dt.bfloat16
AF = mybir.ActivationFunctionType
ALU = mybir.AluOpType
AX = mybir.AxisListType

T = 4096
DM = 2048
H = 16
D = 128
FC = DM // 128
E = 16
I = 768
IC = I // 128
TOPK = 4
SCALE = D ** -0.5
EPS_RMS = 1e-6
EPS_LN = 1e-5
SEQ_SHIFTS = [0, 1, -1, 3, -3, 7, -7, 20, -20, 53, -53, 141, -141, 380, -380, 1024]
CH_SHIFTS = [0, 1, 2, 4, 8]
SEGS = [-3, -2, -1, 0, 1, 2, 3, 8]
NSEG = len(SEGS)
SEG_DIAGS = []
for _d in SEGS:
    SEG_DIAGS.append(sorted(s - _d * 128 for s in SEQ_SHIFTS
                            if -128 < s - _d * 128 < 128))
NT = T // 128


class Buf:
    __slots__ = ("name", "writer", "readers")

    def __init__(self, name):
        self.name = name
        self.writer = None
        self.readers = []


class Prog:
    ENGINES = ("pe", "act", "dve", "pool", "sp")

    def __init__(self, nc, es):
        self.nc = nc
        self.es = es
        self.sem = {}
        self.count = defaultdict(int)
        self.hwm = defaultdict(int)
        self.queue = {e: [] for e in self.ENGINES}
        self.nops = 0
        for e in self.ENGINES:
            self.sem[e] = es.enter_context(nc.semaphore(f"sem_{e}"))

    def buf(self, name):
        return Buf(name)

    def _dma_sem(self, key):
        skey = "dma_" + key
        if skey not in self.sem:
            self.sem[skey] = self.es.enter_context(
                self.nc.semaphore(f"sem_dma_{key}"))
        return skey

    def _wait(self, waiter_eng, src_stream, val):
        if val <= 0:
            return
        if src_stream.startswith("dma_"):
            val = self.count[src_stream]
        k = (waiter_eng, src_stream)
        if self.hwm[k] >= val:
            return
        self.hwm[k] = val
        sem = self.sem[src_stream]
        tgt = val * (16 if src_stream.startswith("dma_") else 1)
        self.queue[waiter_eng].append(lambda e: e.wait_ge(sem, tgt))

    def _pre(self, eng, reads, writes):
        need = {}
        bar = getattr(self, "barrier_counts", None)
        if bar:
            for s, c in bar.items():
                if c > need.get(s, 0):
                    need[s] = c
        for b in reads:
            if b.writer is not None:
                s, c = b.writer
                if c > need.get(s, 0):
                    need[s] = c
        for b in writes:
            if b.writer is not None:
                s, c = b.writer
                if c > need.get(s, 0):
                    need[s] = c
            for (s, c) in b.readers:
                if c > need.get(s, 0):
                    need[s] = c
        for s, c in need.items():
            self._wait(eng, s, c)

    def barrier(self):
        self.barrier_counts = dict(self.count)

    def _post(self, stream, reads, writes):
        self.count[stream] += 1
        c = self.count[stream]
        for b in writes:
            b.writer = (stream, c)
            b.readers = []
        for b in reads:
            b.readers.append((stream, c))
        return c

    def op(self, eng, fn, reads=(), writes=()):
        self._pre(eng, reads, writes)
        sem = self.sem[eng]
        self.queue[eng].append(lambda e: fn(e).then_inc(sem, 1))
        self.nops += 1
        return self._post(eng, reads, writes)

    def dma(self, queue, out=None, in_=None, reads=(), writes=(), key=None, **kw):
        if key is None:
            key = writes[0].name if writes else (reads[0].name if reads else queue)
        stream = self._dma_sem(key)
        self._pre(queue, reads, writes)
        sem = self.sem[stream]
        o, i = out, in_
        self.queue[queue].append(
            lambda e: e.dma_start(out=o, in_=i, **kw).then_inc(sem, 16))
        self.nops += 1
        return self._post(stream, reads, writes)

    def dma_transpose(self, queue, out, in_, reads=(), writes=(), key=None):
        if key is None:
            key = writes[0].name
        stream = self._dma_sem(key)
        self._pre(queue, reads, writes)
        sem = self.sem[stream]
        self.queue[queue].append(
            lambda e: e.dma_start(out=out, in_=in_, transpose=True).then_inc(sem, 16))
        self.nops += 1
        return self._post(stream, reads, writes)

    def finish(self, eng="pool"):
        for s, c in list(self.count.items()):
            if s != eng and c > 0:
                self._wait(eng, s, c)

    def emit(self):
        with self.nc.Block() as block:
            for eng_name, starter in (("pe", block.tensor), ("act", block.scalar),
                                      ("dve", block.vector), ("pool", block.gpsimd),
                                      ("sp", block.sync)):
                q = self.queue[eng_name]
                if q:
                    def body(e, q=q):
                        for fn in q:
                            fn(e)
                    starter(body)


def circ_runs(a, ln, period=T):
    a %= period
    out = []
    while ln > 0:
        take = min(ln, period - a)
        out.append((a, take))
        ln -= take
        a = 0
    return out


def build_program(dev_taps=()):
    es = ExitStack()
    nc = bass.Bass("TRN2", target_bir_lowering=False, debug=False)
    p = Prog(nc, es)

    kind_scr = "ExternalOutput" if "scratch" in dev_taps else "Internal"

    DT = nc.dram_tensor
    xT = DT("xT", [DM, T], F32, kind="ExternalInput").ap()
    wqm = DT("wqm", [H, FC, 128, 128], F32, kind="ExternalInput").ap()
    wkm = DT("wkm", [H, FC, 128, 128], F32, kind="ExternalInput").ap()
    wvdc = DT("wvdc", [4, FC, 128, 512], F32, kind="ExternalInput").ap()
    wom = DT("wom", [H, FC, 128, 128], F32, kind="ExternalInput").ap()
    bqr = DT("bqr", [1, DM], F32, kind="ExternalInput").ap()
    bkr = DT("bkr", [1, DM], F32, kind="ExternalInput").ap()
    boc = DT("boc", [128, FC], F32, kind="ExternalInput").ap()
    gqc = DT("gqc", [128, FC], F32, kind="ExternalInput").ap()
    gkc = DT("gkc", [128, FC], F32, kind="ExternalInput").ap()
    bvr = DT("bvr", [1, DM], F32, kind="ExternalInput").ap()
    wmix = DT("wmix", [1, 5], F32, kind="ExternalInput").ap()
    rwT = DT("rwT", [DM, E], F32, kind="ExternalInput").ap()
    ebr = DT("ebr", [1, E], F32, kind="ExternalInput").ap()
    gup = DT("gup", [E, FC, 128, 2 * I], F32, kind="ExternalInput").ap()
    sgu = DT("sgu", [FC, 128, 2 * I], F32, kind="ExternalInput").ap()
    dallm = DT("dallm", [H, 128, 102, 128], F32, kind="ExternalInput").ap()

    out = DT("out", [DM, T], F32, kind="ExternalOutput").ap()

    qpb_s = DT("qpb_s", [DM, T], BF16, kind=kind_scr).ap()
    kpb_s = DT("kpb_s", [DM, T], BF16, kind=kind_scr).ap()
    vtm_s = DT("vtm_s", [H, T, 128], BF16, kind=kind_scr).ap()
    avtm_s = DT("avtm_s", [T, DM], BF16, kind=kind_scr).ap()
    r1_s = DT("r1_s", [DM, T], F32, kind="Internal").ap()
    r1b_s = DT("r1b_s", [DM, T], BF16, kind="Internal").ap()
    x1_s = DT("x1_s", [DM, T], F32, kind=kind_scr).ap()
    x1b_s = DT("x1b_s", [DM, T], BF16, kind="Internal").ap()
    mid_s = DT("mid_s", [17, I, T], BF16, kind=kind_scr).ap()
    dbf_s = DT("dbf_s", [H, 128, 102 * 128], BF16, kind="Internal").ap()
    r2_s = DT("r2_s", [DM, T], F32, kind="Internal").ap()
    r2b_s = DT("r2b_s", [DM, T], BF16, kind="Internal").ap()

    sb = lambda n, s, d=F32: es.enter_context(nc.sbuf_tensor(n, s, d))
    pb = [es.enter_context(nc.psum_tensor(f"pb{i}", [128, 512], F32))
          for i in range(8)]
    b_pb = [p.buf(f"pb{i}") for i in range(8)]

    B = p.buf
    ones_row = sb("ones_row", [1, 128], BF16)
    ones_r512 = sb("ones_r512", [1, 512], BF16)
    ones_col = sb("ones_col", [128, 1], BF16)
    ident = sb("ident", [128, 128], BF16)
    identf = sb("identf", [128, 128], F32)
    stencil = sb("stencil", [128, 128], BF16)
    amask = sb("amask", [128, NSEG * 128], BF16)
    bq_r = sb("bq_r", [1, DM], BF16)
    bk_r = sb("bk_r", [1, DM], BF16)
    bo_c = sb("bo_c", [128, FC], F32)
    gq_c = sb("gq_c", [128, FC], F32)
    gk_c = sb("gk_c", [128, FC], F32)
    wmix_r = sb("wmix_r", [1, 8], F32)
    wmix_c = sb("wmix_c", [128, 8], F32)
    eb_r = sb("eb_r", [1, E], BF16)
    eb_b = sb("eb_b", [128, E], F32)
    b_ones_row, b_ones_col, b_ident = B("ones_row"), B("ones_col"), B("ident")
    b_identf = B("identf")
    b_ones_r512 = B("ones_r512")
    b_stencil = B("stencil")
    b_amask = B("amask")
    b_bq_r, b_bk_r, b_bo_c, b_gq_c, b_gk_c = B("bq_r"), B("bk_r"), B("bo_c"), B("gq_c"), B("gk_c")
    b_wmix_r, b_wmix_c, b_eb_r, b_eb_b = B("wmix_r"), B("wmix_c"), B("eb_r"), B("eb_b")

    b_qpb = B("qpb_s"); b_kpb = B("kpb_s"); b_vtm = B("vtm_s")
    b_avtm = B("avtm_s"); b_r1 = B("r1_s"); b_r1b = B("r1b_s")
    b_x1 = B("x1_s"); b_x1b = B("x1b_s"); b_mid = B("mid_s")
    b_dbf = B("dbf_s"); b_r2 = B("r2_s"); b_r2b = B("r2b_s"); b_out = B("out")

    constcols = sb("constcols", [128, 8], F32)
    b_constcols = B("constcols")
    _ccn = [0]
    def register_const(v):
        i = _ccn[0]
        _ccn[0] += 1
        p.op("dve", lambda e, i=i, v=v: e.memset(constcols[:, i:i + 1], float(v)),
             writes=[b_constcols])
        nc.const_aps.register(F32, float(v), constcols[:, i:i + 1])
    for _v in (EPS_RMS, EPS_LN, 1e-20):
        register_const(_v)

    rstd_rb = sb("rstd_rb", [1, 2 * T], BF16)
    b_rstd_rb = B("rstd_rb")
    ROW_Q, ROW_K = 0, 1
    es0 = ExitStack()
    sb0 = lambda n, s, d=F32: es0.enter_context(nc.sbuf_tensor(n, s, d))
    iotaD = sb0("iotaD", [128, 128], F32)
    stenf = sb0("stenf", [128, 128], F32)
    amaskf = sb0("amaskf", [128, NSEG * 128], F32)
    b_iotaD, b_stenf, b_amaskf = B("iotaD"), B("stenf"), B("amaskf")

    p.op("dve", lambda e: e.memset(ones_row[:], 1.0), writes=[b_ones_row])
    p.op("dve", lambda e: e.memset(ones_r512[:], 1.0), writes=[b_ones_r512])
    p.op("dve", lambda e: e.memset(ones_col[:], 1.0), writes=[b_ones_col])
    p.op("pool", lambda e: e.iota(iotaD[:], pattern=[[1, 128]], base=0,
                                  channel_multiplier=-1,
                                  allow_small_or_imprecise_dtypes=True),
         writes=[b_iotaD])
    p.op("dve", lambda e: e.tensor_single_scalar(identf[:], iotaD[:], 0.0, ALU.is_equal),
         reads=[b_iotaD], writes=[b_identf])
    p.op("dve", lambda e: e.tensor_copy(ident[:], identf[:]),
         reads=[b_identf], writes=[b_ident])
    for name, srcap, dst, bdst in (("boc", boc, bo_c, b_bo_c), ("gqc", gqc, gq_c, b_gq_c),
                                   ("gkc", gkc, gk_c, b_gk_c)):
        p.dma("sp", out=dst[:], in_=srcap[:, :], writes=[bdst])
    p.dma("pool", out=bq_r[:], in_=bqr[:, :], writes=[b_bq_r])
    p.dma("pool", out=bk_r[:], in_=bkr[:, :], writes=[b_bk_r])

    p.dma("sp", out=wmix_r[0:1, 0:5], in_=wmix[:, :], writes=[b_wmix_r])
    p.dma("pool", out=eb_r[:], in_=ebr[:, :], writes=[b_eb_r])

    wmix_rb = sb0("wmix_rb", [1, 8], BF16)
    b_wmix_rb = B("wmix_rb")
    p.op("dve", lambda e: e.memset(wmix_rb[:], 0.0), writes=[b_wmix_rb])
    p.op("dve", lambda e: e.tensor_copy(wmix_rb[0:1, 0:5], wmix_r[0:1, 0:5]),
         reads=[b_wmix_r], writes=[b_wmix_rb])
    p.op("pe", lambda e: e.matmul(pb[6][:, 0:8], ones_row[0:1, :], wmix_rb[0:1, :],
                                  start=True, stop=True),
         reads=[b_ones_row, b_wmix_rb], writes=[b_pb[6]])
    p.op("dve", lambda e: e.tensor_copy(wmix_c[:], pb[6][:, 0:8]),
         reads=[b_pb[6]], writes=[b_wmix_c])
    p.op("pe", lambda e: e.matmul(pb[6][:, 0:E], ones_row[0:1, :], eb_r[0:1, :],
                                  start=True, stop=True),
         reads=[b_ones_row, b_eb_r], writes=[b_pb[6]])
    p.op("dve", lambda e: e.tensor_copy(eb_b[:], pb[6][:, 0:E]),
         reads=[b_pb[6]], writes=[b_eb_b])

    p.op("dve", lambda e: e.memset(stenf[:], 0.0), writes=[b_stenf])
    mtmp = sb0("mtmp", [128, 128], F32)
    mtmp2 = sb0("mtmp2", [128, 128], F32)
    b_mtmp, b_mtmp2 = B("mtmp"), B("mtmp2")
    for ci, c in enumerate(CH_SHIFTS):
        p.op("dve", lambda e, c=c: e.tensor_single_scalar(mtmp[:], iotaD[:], float(c), ALU.is_equal),
             reads=[b_iotaD], writes=[b_mtmp])
        p.op("dve", lambda e, c=c: e.tensor_single_scalar(mtmp2[:], iotaD[:], float(c - 128), ALU.is_equal),
             reads=[b_iotaD], writes=[b_mtmp2])
        p.op("dve", lambda e: e.tensor_add(mtmp[:], mtmp[:], mtmp2[:]),
             reads=[b_mtmp, b_mtmp2], writes=[b_mtmp])
        p.op("dve", lambda e, ci=ci: e.scalar_tensor_tensor(
            stenf[:], mtmp[:], wmix_c[:, ci:ci + 1], stenf[:], ALU.mult, ALU.add),
            reads=[b_mtmp, b_wmix_c, b_stenf], writes=[b_stenf])
    p.op("act", lambda e: e.activation(stencil[:], stenf[:], AF.Copy, scale=SCALE),
         reads=[b_stenf], writes=[b_stencil])

    p.op("dve", lambda e: e.memset(amaskf[:], 0.0), writes=[b_amaskf])
    for si in range(NSEG):
        for dlt in SEG_DIAGS[si]:
            p.op("dve", lambda e, si=si, dlt=dlt: e.tensor_single_scalar(
                mtmp[:], iotaD[:], float(dlt), ALU.is_equal),
                reads=[b_iotaD], writes=[b_mtmp])
            p.op("dve", lambda e, si=si: e.tensor_max(
                amaskf[:, si * 128:(si + 1) * 128],
                amaskf[:, si * 128:(si + 1) * 128], mtmp[:]),
                reads=[b_amaskf, b_mtmp], writes=[b_amaskf])
    p.op("dve", lambda e: e.tensor_copy(amask[:], amaskf[:]),
         reads=[b_amaskf], writes=[b_amask])

    es0.close()
    p.barrier()

    es1 = ExitStack()
    sb1 = lambda n, s, d=F32: es1.enter_context(nc.sbuf_tensor(n, s, d))
    rowbuf = sb1("rowbuf", [1, T], F32)
    b_rowbuf = B("rowbuf")
    xh = [sb1(f"xh{c}", [128, T], BF16) for c in range(FC)]
    b_xh = [B(f"xh{c}") for c in range(FC)]
    def row(i):
        return rowbuf[0:1, 0:T]
    def rowslc(i, a, ln):
        return rowbuf[0:1, a:a + ln]
    for c in range(FC):
        p.dma("pool", out=xh[c][:], in_=xT[c * 128:(c + 1) * 128, :],
              writes=[b_xh[c]], key=f"xh{c % 2}")

    wqb = [sb1(f"wqb{i}", [128, FC * 128], BF16) for i in range(2)]
    b_wqb = [B(f"wqb{i}") for i in range(2)]
    wvb = [sb1("wvb0", [128, FC * 512], BF16)]
    b_wvb = [B("wvb0")]
    sqv = [sb1(f"sqv{i}", [128, 512], BF16) for i in range(2)]
    b_sqv = [B(f"sqv{i}") for i in range(2)]
    pj_sb = [sb1(f"pj_sb{i}", [128, 512], BF16) for i in range(4)]
    b_pj = [B(f"pj_sb{i}") for i in range(4)]
    bv_r = sb1("bv_r", [1, DM], BF16)
    b_bv_r = B("bv_r")
    p.dma("pool", out=bv_r[:], in_=bvr[:, :], writes=[b_bv_r])

    def qk_sweep(which):
        wsrc = wqm if which == "q" else wkm
        bias_r = bq_r if which == "q" else bk_r
        b_bias = b_bq_r if which == "q" else b_bk_r
        dst = qpb_s if which == "q" else kpb_s
        b_dst = b_qpb if which == "q" else b_kpb
        rowi = ROW_Q if which == "q" else ROW_K
        TCB = [0, 1, 2, 7]
        for g in range(2):
            for m in range(H):
                s = m % 2
                p.dma("pool", out=wqb[s][:].rearrange("p (fc q) -> p fc q", q=128),
                      in_=wsrc[m].rearrange("fc p q -> p fc q"),
                      writes=[b_wqb[s]], key=f"wqk{s}")
                tg0 = g * 2048
                for hb in range(2):
                    tcis = (hb * 2, hb * 2 + 1)
                    bwr = [b_pb[TCB[t]] for t in tcis]
                    def proj(e, m=m, s=s, tg0=tg0, tcis=tcis):
                        last = None
                        for fc in range(FC):
                            lw = wqb[s][:, fc * 128:(fc + 1) * 128]
                            for tci in tcis:
                                last = e.matmul(pb[TCB[tci]][:, :], lw,
                                                xh[fc][:, tg0 + tci * 512:tg0 + (tci + 1) * 512],
                                                start=(fc == 0), stop=False)
                        for tci in tcis:
                            last = e.matmul(pb[TCB[tci]][:, :],
                                            bias_r[0:1, m * 128:(m + 1) * 128],
                                            ones_r512[0:1, :],
                                            start=False, stop=True)
                        return last
                    p.op("pe", proj, reads=[b_wqb[s], b_bias, b_ones_r512] + b_xh,
                         writes=bwr)
                for tci in range(4):
                    t0 = g * 2048 + tci * 512
                    bank = TCB[tci]
                    p.op("act", lambda e, bank=bank, tci=tci: e.activation(
                        pj_sb[tci][:], pb[bank][:, :], AF.Copy),
                        reads=[b_pb[bank]], writes=[b_pj[tci]])
                    p.op("act", lambda e, bank=bank, tci=tci: e.activation(
                        sqv[tci % 2][:], pb[bank][:, :], AF.Square),
                        reads=[b_pb[bank]], writes=[b_sqv[tci % 2]])
                    p.dma("sp", out=dst[m * 128:(m + 1) * 128, t0:t0 + 512],
                          in_=pj_sb[tci][:], reads=[b_pj[tci]], writes=[b_dst],
                          key=f"st_{which}{tci}")
                    rb = 3 + tci
                    p.op("pe", lambda e, rb=rb, m=m, tci=tci: e.matmul(
                        pb[rb][0:1, :], ones_col[:, 0:1], sqv[tci % 2][:],
                        start=(m == 0), stop=(m == H - 1)),
                        reads=[b_ones_col, b_sqv[tci % 2]], writes=[b_pb[rb]])
            for tci in range(4):
                t0 = g * 2048 + tci * 512
                rb = 3 + tci
                p.op("act", lambda e, rb=rb, t0=t0, rowi=rowi: e.activation(
                    rowslc(rowi, t0, 512), pb[rb][0:1, :], AF.Copy),
                    reads=[b_pb[rb]], writes=[b_rowbuf])
        p.op("act", lambda e: e.activation(
            row(0), row(0), AF.Ln, scale=1.0 / DM, bias=EPS_RMS),
            reads=[b_rowbuf, b_constcols], writes=[b_rowbuf])
        p.op("act", lambda e: e.activation(
            row(0), row(0), AF.Exp, scale=-0.5),
            reads=[b_rowbuf], writes=[b_rowbuf])
        p.op("act", lambda e, rowi=rowi: e.activation(
            rstd_rb[0:1, rowi * T:(rowi + 1) * T], row(0), AF.Copy),
            reads=[b_rowbuf], writes=[b_rstd_rb])

    def v_sweep():
        for g in range(2):
            for dc in range(4):
                s = 0
                p.dma("pool", out=wvb[s][:].rearrange("p (fc q) -> p fc q", q=512),
                      in_=wvdc[dc].rearrange("fc p q -> p fc q"),
                      writes=[b_wvb[s]], key=f"wv{s}")
                for tt in range(16):
                    t0 = g * 2048 + tt * 128
                    bank = [0, 1, 2, 7][tt % 4]
                    def vproj(e, s=s, t0=t0, dc=dc, bank=bank):
                        last = None
                        for fc in range(FC):
                            last = e.matmul(pb[bank][:, :],
                                            xh[fc][:, t0:t0 + 128],
                                            wvb[s][:, fc * 512:(fc + 1) * 512],
                                            start=(fc == 0), stop=False)
                        return e.matmul(pb[bank][:, :], ones_row[0:1, :],
                                        bv_r[0:1, dc * 512:(dc + 1) * 512],
                                        start=False, stop=True)
                    p.op("pe", vproj,
                         reads=[b_wvb[s], b_ones_row, b_bv_r] + b_xh,
                         writes=[b_pb[bank]])
                    st = tt % 4
                    p.op("act", lambda e, bank=bank, st=st: e.activation(
                        pj_sb[st][:], pb[bank][:, :], AF.Copy),
                        reads=[b_pb[bank]], writes=[b_pj[st]])
                    for hh in range(4):
                        h = dc * 4 + hh
                        p.dma("sp",
                              out=vtm_s[h, t0:t0 + 128, :],
                              in_=pj_sb[st][:, hh * 128:(hh + 1) * 128],
                              reads=[b_pj[st]], writes=[b_vtm], key=f"st_v{st}")

    qk_sweep("q")
    qk_sweep("k")
    v_sweep()



    es1.close()
    p.barrier()
    es2 = ExitStack()
    sb2 = lambda n, s, d=F32: es2.enter_context(nc.sbuf_tensor(n, s, d))

    qpbh = [sb2(f"qpbh{i}", [128, T], BF16) for i in range(2)]
    kpbh = [sb2(f"kpbh{i}", [128, T], BF16) for i in range(2)]
    vtmh = [sb2(f"vtmh{i}", [128, NT * 132], BF16) for i in range(2)]
    b_qpbh = [B(f"qpbh{i}") for i in range(2)]
    b_kpbh = [B(f"kpbh{i}") for i in range(2)]
    b_vtmh = [B(f"vtmh{i}") for i in range(2)]
    for i in range(2):
        p.op("dve", lambda e, i=i: e.memset(vtmh[i][:], 1.0), writes=[b_vtmh[i]])
    qn = sb2("qn", [128, T], BF16); b_qn = B("qn")
    kn = sb2("kn", [128, T], BF16); b_kn = B("kn")
    qm = sb2("qm", [128, T], BF16); b_qm = B("qm")
    emr = [sb2(f"emr{i}", [128, NSEG * 128], BF16) for i in range(2)]
    b_emr = [B(f"emr{i}") for i in range(2)]
    e_all = sb2("e_all", [128, NT * NSEG * 128], BF16); b_e_all = B("e_all")
    av_sb = sb2("av_sb", [128, 2048], BF16); b_av_sb = B("av_sb")
    chain = sb2("chain", [128, 1], F32); b_chain = B("chain")
    brow = sb2("brow", [1, 128], F32); b_brow = B("brow")
    bval = sb2("bval", [1, 1], F32); b_bval = B("bval")
    bvalb = sb2("bvalb", [1, 1], BF16); b_bvalb = B("bvalb")
    negb = sb2("negb", [128, 1], F32); b_negb = B("negb")
    zr16 = sb2("zr16", [128, 16], F32); b_zr16 = B("zr16")
    mcol = sb2("mcol", [128, 1], F32); b_mcol = B("mcol")

    for h in range(H):
        s = h % 2
        p.dma("act", out=qpbh[s][:], in_=qpb_s[h * 128:(h + 1) * 128, :],
              reads=[b_qpb], writes=[b_qpbh[s]])
        p.dma("act", out=kpbh[s][:], in_=kpb_s[h * 128:(h + 1) * 128, :],
              reads=[b_kpb], writes=[b_kpbh[s]])
        p.dma("act", out=vtmh[s][:].rearrange("p (tt q) -> p tt q", q=132)[:, :, 0:128],
              in_=vtm_s[h].rearrange("(tt p) q -> p tt q", p=128),
              reads=[b_vtm], writes=[b_vtmh[s]])
        for which in ("q", "k"):
            srcb = qpbh[s] if which == "q" else kpbh[s]
            b_srcb = b_qpbh[s] if which == "q" else b_kpbh[s]
            dstb = qn if which == "q" else kn
            b_dstb = b_qn if which == "q" else b_kn
            g_c = gq_c if which == "q" else gk_c
            b_g_c = b_gq_c if which == "q" else b_gk_c
            ri = ROW_Q if which == "q" else ROW_K
            for tc in range(8):
                t0 = tc * 512
                p.op("pe", lambda e, ri=ri, t0=t0: e.matmul(
                    pb[6][:, :], ones_row[0:1, :],
                    rstd_rb[0:1, ri * T + t0:ri * T + t0 + 512],
                    start=True, stop=True),
                    reads=[b_ones_row, b_rstd_rb], writes=[b_pb[6]])
                p.op("dve", lambda e, t0=t0, srcb=srcb, dstb=dstb, g_c=g_c, h=h:
                     e.scalar_tensor_tensor(
                         dstb[:, t0:t0 + 512], srcb[:, t0:t0 + 512],
                         g_c[:, h:h + 1], pb[6][:, :], ALU.mult, ALU.mult),
                     reads=[b_srcb, b_g_c, b_pb[6]], writes=[b_dstb])
        for tc in range(8):
            t0 = tc * 512
            p.op("pe", lambda e, t0=t0: e.matmul(
                pb[6][:, :], stencil[:, :], qn[:, t0:t0 + 512],
                start=True, stop=True),
                reads=[b_stencil, b_qn], writes=[b_pb[6]])
            p.op("act", lambda e, t0=t0: e.activation(
                qm[:, t0:t0 + 512], pb[6][:, :], AF.Copy),
                reads=[b_pb[6]], writes=[b_qm])
        p.op("dve", lambda e: e.memset(chain[:], -1e30), writes=[b_chain])
        for sc in (0, 8, 16, 24):
            p.op("pe", lambda e, sc=sc: e.matmul(
                pb[6][:, 0:128], kn[:, sc * 128:sc * 128 + 128],
                qm[:, sc * 128:sc * 128 + 128], start=True, stop=True),
                reads=[b_kn, b_qm], writes=[b_pb[6]])
            p.op("dve", lambda e: e.reduce_max(mcol[:, 0:1], pb[6][:, 0:128], axis=AX.X),
                 reads=[b_pb[6]], writes=[b_mcol])
            p.op("dve", lambda e: e.tensor_max(chain[:], chain[:], mcol[:]),
                 reads=[b_chain, b_mcol], writes=[b_chain])
        p.op("pe", lambda e: e.transpose(pb[6][0:1, 0:128], chain[:, 0:1], identf[:]),
             reads=[b_chain, b_identf], writes=[b_pb[6]])
        p.op("act", lambda e: e.activation(brow[:], pb[6][0:1, 0:128], AF.Copy),
             reads=[b_pb[6]], writes=[b_brow])
        p.op("dve", lambda e: e.reduce_max(bval[:, 0:1], brow[:], axis=AX.X),
             reads=[b_brow], writes=[b_bval])
        p.op("dve", lambda e: e.tensor_copy(bvalb[:], bval[:]),
             reads=[b_bval], writes=[b_bvalb])
        p.op("pe", lambda e: e.matmul(pb[6][:, 0:1], ones_row[0:1, :], bvalb[0:1, :],
                                      start=True, stop=True),
             reads=[b_ones_row, b_bvalb], writes=[b_pb[6]])
        p.op("act", lambda e: e.activation(negb[:], pb[6][:, 0:1], AF.Copy,
                                           scale=-1.0, bias=-5.0),
             reads=[b_pb[6], b_constcols], writes=[b_negb])

        for kc in range(NT):
            sbank = 2 + (kc % 2) * 2
            def smm(e, kc=kc, sbank=sbank):
                last = None
                pieces = []
                for (a2, l2) in circ_runs(((kc - 3) % NT) * 128, 896):
                    pieces.append((a2, l2))
                off = 0
                for (a2, l2) in pieces:
                    while l2 > 0:
                        bank = sbank + (1 if off >= 512 else 0)
                        boff = off - 512 if off >= 512 else off
                        take = min(l2, 512 - boff)
                        last = e.matmul(pb[bank][:, boff:boff + take],
                                        kn[:, kc * 128:kc * 128 + 128],
                                        qm[:, a2:a2 + take],
                                        start=True, stop=True)
                        off += take
                        a2 += take
                        l2 -= take
                q8 = ((kc + 8) % NT) * 128
                last = e.matmul(pb[sbank + 1][:, 384:512],
                                kn[:, kc * 128:kc * 128 + 128],
                                qm[:, q8:q8 + 128], start=True, stop=True)
                return last
            p.op("pe", smm, reads=[b_kn, b_qm],
                 writes=[b_pb[sbank], b_pb[sbank + 1]])
            kp = kc % 2
            for hf in range(2):
                p.op("act", lambda e, hf=hf, sbank=sbank, kp=kp: e.activation(
                    emr[kp][:, hf * 512:(hf + 1) * 512], pb[sbank + hf][:, :],
                    AF.Exp, bias=negb[:, 0:1]),
                    reads=[b_pb[sbank + hf], b_negb], writes=[b_emr[kp]])
            p.op("dve" if kp == 0 else "pool", lambda e, kc=kc, kp=kp: e.tensor_mul(
                e_all[:, kc * 1024:(kc + 1) * 1024], emr[kp][:], amask[:]),
                reads=[b_emr[kp], b_amask], writes=[b_e_all])
        for tt in range(NT):
            avbank = tt % 2
            def avz(e, tt=tt, avbank=avbank, s=s):
                last = None
                for j, dseg in enumerate(SEGS):
                    kc = (tt - dseg) % NT
                    lw = e_all[:, (kc * 8 + j) * 128:(kc * 8 + j) * 128 + 128]
                    last = e.matmul(pb[avbank][:, 0:129], lw,
                                    vtmh[s][:, kc * 132:kc * 132 + 129],
                                    start=(j == 0), stop=(j == NSEG - 1))
                return last
            p.op("pe", avz, reads=[b_e_all, b_vtmh[s]],
                 writes=[b_pb[avbank]])
            p.op("dve", lambda e, avbank=avbank: e.reciprocal(
                zr16[:, 0:1], pb[avbank][:, 128:129]),
                reads=[b_pb[avbank]], writes=[b_zr16])
            p.op("act", lambda e, tt=tt, avbank=avbank: e.activation(
                av_sb[:, (tt % 16) * 128:(tt % 16 + 1) * 128],
                pb[avbank][:, 0:128], AF.Copy, scale=zr16[:, 0:1]),
                reads=[b_pb[avbank], b_zr16], writes=[b_av_sb])
            p.dma("sp", out=avtm_s[tt * 128:(tt + 1) * 128, h * 128:(h + 1) * 128],
                  in_=av_sb[:, (tt % 16) * 128:(tt % 16 + 1) * 128],
                  reads=[b_av_sb], writes=[b_avtm], key="st_av")

    es2.close()
    p.barrier()

    es3 = ExitStack()
    sb3 = lambda n, s, d=F32: es3.enter_context(nc.sbuf_tensor(n, s, d))
    avf = [sb3(f"avf{dc}", [128, T], BF16) for dc in range(FC)]
    b_avf = [B(f"avf{dc}") for dc in range(FC)]
    for dc in range(FC):
        p.dma_transpose("sp", out=avf[dc][:],
                        in_=avtm_s[:, dc * 128:(dc + 1) * 128],
                        reads=[b_avtm], writes=[b_avf[dc]], key=f"avf{dc % 2}")
    wob = [sb3(f"wob{i}", [128, FC * 128], BF16) for i in range(2)]
    b_wob = [B(f"wob{i}") for i in range(2)]
    xfb = [sb3(f"xfb{i}", [128, 512], F32) for i in range(2)]
    b_xfb = [B(f"xfb{i}") for i in range(2)]
    r1c = sb3("r1c", [128, 512], F32); b_r1c = B("r1c")
    r1bc = sb3("r1bc", [128, 512], BF16); b_r1bc = B("r1bc")

    for m in range(H):
        s = m % 2
        p.dma("pool", out=wob[s][:].rearrange("p (fc q) -> p fc q", q=128),
              in_=wom[m].rearrange("fc p q -> p fc q"),
              writes=[b_wob[s]], key=f"wob{s}")
        for tg in range(2):
            tg0 = tg * 2048
            bwr = [b_pb[j] for j in range(4)]
            def womm(e, s=s, tg0=tg0):
                last = None
                for dc in range(FC):
                    lw = wob[s][:, dc * 128:(dc + 1) * 128]
                    for tq in range(4):
                        last = e.matmul(pb[tq][:, :], lw,
                                        avf[dc][:, tg0 + tq * 512:tg0 + (tq + 1) * 512],
                                        start=(dc == 0), stop=(dc == FC - 1))
                return last
            p.op("pe", womm, reads=[b_wob[s]] + b_avf, writes=bwr)
            for tq in range(4):
                t0 = tg0 + tq * 512
                xs = tq % 2
                p.dma("act", out=xfb[xs][:], in_=xT[m * 128:(m + 1) * 128, t0:t0 + 512],
                      writes=[b_xfb[xs]], key=f"xfb{xs}")
                p.op("dve", lambda e, m=m, tq=tq, xs=xs: e.scalar_tensor_tensor(
                    r1c[:], pb[tq][:, :], bo_c[:, m:m + 1], xfb[xs][:],
                    ALU.add, ALU.add),
                    reads=[b_pb[tq], b_bo_c, b_xfb[xs]], writes=[b_r1c])
                p.dma("sp", out=r1_s[m * 128:(m + 1) * 128, t0:t0 + 512],
                      in_=r1c[:], reads=[b_r1c], writes=[b_r1], key="st_r1")
                p.op("act", lambda e: e.activation(r1bc[:], r1c[:], AF.Copy),
                     reads=[b_r1c], writes=[b_r1bc])
                p.dma("pool", out=r1b_s[m * 128:(m + 1) * 128, t0:t0 + 512],
                      in_=r1bc[:], reads=[b_r1bc], writes=[b_r1b], key="st_r1b")
    es3.close()
    p.barrier()

    def ln_stats(src_b16, b_src, rowbuf2, b_rowbuf2, scope_sb, tag):
        lb = [scope_sb(f"lnb{tag}{i}", [128, 512], BF16) for i in range(2)]
        b_lb = [B(f"lnb{tag}{i}") for i in range(2)]
        sqb = scope_sb(f"sqb{tag}", [128, 512], BF16)
        b_sqb = B(f"sqb{tag}")
        for tc in range(8):
            t0 = tc * 512
            for m in range(H):
                s = m % 2
                p.dma("act", out=lb[s][:], in_=src_b16[m * 128:(m + 1) * 128, t0:t0 + 512],
                      reads=[b_src], writes=[b_lb[s]], key=f"lnb{tag}{s}")
                p.op("act", lambda e, s=s: e.activation(sqb[:], lb[s][:], AF.Square),
                     reads=[b_lb[s]], writes=[b_sqb])
                p.op("pe", lambda e, s=s, m=m: e.matmul(
                    pb[2][0:1, :], ones_col[:, 0:1], lb[s][:],
                    start=(m == 0), stop=(m == H - 1)),
                    reads=[b_ones_col, b_lb[s]], writes=[b_pb[2]])
                p.op("pe", lambda e, m=m: e.matmul(
                    pb[3][0:1, :], ones_col[:, 0:1], sqb[:],
                    start=(m == 0), stop=(m == H - 1)),
                    reads=[b_ones_col, b_sqb], writes=[b_pb[3]])
            p.op("act", lambda e, t0=t0: e.activation(
                rowbuf2[0:1, t0:t0 + 512], pb[2][0:1, :], AF.Copy, scale=1.0 / DM),
                reads=[b_pb[2]], writes=[b_rowbuf2])
            p.op("act", lambda e, t0=t0: e.activation(
                rowbuf2[0:1, T + t0:T + t0 + 512], pb[3][0:1, :], AF.Copy, scale=1.0 / DM),
                reads=[b_pb[3]], writes=[b_rowbuf2])
        MU = rowbuf2[0:1, 0:T]
        MSQ = rowbuf2[0:1, T:2 * T]
        MU2 = rowbuf2[0:1, 2 * T:3 * T]
        p.op("act", lambda e: e.activation(MU2, MU, AF.Square),
             reads=[b_rowbuf2], writes=[b_rowbuf2])
        p.op("dve", lambda e: e.tensor_sub(MSQ, MSQ, MU2),
             reads=[b_rowbuf2], writes=[b_rowbuf2])
        p.op("act", lambda e: e.activation(MSQ, MSQ, AF.Ln, bias=EPS_LN),
             reads=[b_rowbuf2, b_constcols], writes=[b_rowbuf2])
        p.op("act", lambda e: e.activation(MSQ, MSQ, AF.Exp, scale=-0.5),
             reads=[b_rowbuf2], writes=[b_rowbuf2])
        p.op("dve", lambda e: e.tensor_mul(MU2, MU, MSQ),
             reads=[b_rowbuf2], writes=[b_rowbuf2])
        return MSQ, MU2

    def ln_apply(src_f32, b_src, rstd_row, b2_row, b_rows, dst, b_dst,
                 dst_b16, b_dst_b16, scope_sb, tag):
        rb16 = scope_sb(f"lnr{tag}", [1, 2 * T], BF16)
        b_rb16 = B(f"lnr{tag}")
        p.op("act", lambda e: e.activation(rb16[0:1, 0:T], rstd_row, AF.Copy),
             reads=[b_rows], writes=[b_rb16])
        p.op("act", lambda e: e.activation(rb16[0:1, T:2 * T], b2_row, AF.Copy),
             reads=[b_rows], writes=[b_rb16])
        lc = [scope_sb(f"lnc{tag}{i}", [128, 512], F32) for i in range(2)]
        b_lc = [B(f"lnc{tag}{i}") for i in range(2)]
        ot = scope_sb(f"lno{tag}", [128, 512], F32)
        b_ot = B(f"lno{tag}")
        ob = scope_sb(f"lnob{tag}", [128, 512], BF16)
        b_ob = B(f"lnob{tag}")
        for tc in range(8):
            t0 = tc * 512
            p.op("pe", lambda e, t0=t0: e.matmul(
                pb[2][:, :], ones_row[0:1, :], rb16[0:1, t0:t0 + 512],
                start=True, stop=True),
                reads=[b_ones_row, b_rb16], writes=[b_pb[2]])
            p.op("pe", lambda e, t0=t0: e.matmul(
                pb[3][:, :], ones_row[0:1, :], rb16[0:1, T + t0:T + t0 + 512],
                start=True, stop=True),
                reads=[b_ones_row, b_rb16], writes=[b_pb[3]])
            for m in range(H):
                s = m % 2
                p.dma("act", out=lc[s][:], in_=src_f32[m * 128:(m + 1) * 128, t0:t0 + 512],
                      reads=[b_src], writes=[b_lc[s]], key=f"lnc{tag}{s}")
                p.op("dve", lambda e, s=s: e.tensor_mul(ot[:], lc[s][:], pb[2][:, :]),
                     reads=[b_lc[s], b_pb[2]], writes=[b_ot])
                p.op("dve", lambda e: e.tensor_sub(ot[:], ot[:], pb[3][:, :]),
                     reads=[b_ot, b_pb[3]], writes=[b_ot])
                p.dma("sp", out=dst[m * 128:(m + 1) * 128, t0:t0 + 512],
                      in_=ot[:], reads=[b_ot], writes=[b_dst], key=f"lnst{tag}")
                if dst_b16 is not None:
                    p.op("act", lambda e: e.activation(ob[:], ot[:], AF.Copy),
                         reads=[b_ot], writes=[b_ob])
                    p.dma("pool", out=dst_b16[m * 128:(m + 1) * 128, t0:t0 + 512],
                          in_=ob[:], reads=[b_ob], writes=[b_dst_b16],
                          key=f"lnstb{tag}")

    es3b = ExitStack()
    sb3b = lambda n, s, d=F32: es3b.enter_context(nc.sbuf_tensor(n, s, d))
    rowbuf2 = sb3b("rowbuf2", [1, 3 * T], F32)
    b_rowbuf2 = B("rowbuf2")
    rstd1, b21 = ln_stats(r1b_s, b_r1b, rowbuf2, b_rowbuf2, sb3b, "1")
    ln_apply(r1_s, b_r1, rstd1, b21, b_rowbuf2, x1_s, b_x1, x1b_s, b_x1b,
             sb3b, "1")
    es3b.close()
    p.barrier()

    es4 = ExitStack()
    sb4 = lambda n, s, d=F32: es4.enter_context(nc.sbuf_tensor(n, s, d))
    rw_sb = sb4("rw_sb", [128, FC * E], BF16)
    b_rw = B("rw_sb")
    p.dma("pool", out=rw_sb[:].rearrange("p (fc e) -> p fc e", e=E),
          in_=rwT.rearrange("(fc p) e -> p fc e", p=128),
          writes=[b_rw])
    x4b = [sb4(f"x4b{i}", [128, 512], BF16) for i in range(2)]
    b_x4b = [B(f"x4b{i}") for i in range(2)]
    s_f = sb4("s_f", [16, T], F32); b_s_f = B("s_f")
    id16 = sb4("id16", [16, 16], F32); b_id16 = B("id16")
    id16i = sb4("id16i", [16, 16], F32); b_id16i = B("id16i")
    p.op("pool", lambda e: e.iota(id16i[:], pattern=[[1, 16]], base=0,
                                  channel_multiplier=-1,
                                  allow_small_or_imprecise_dtypes=True),
         writes=[b_id16i])
    p.op("dve", lambda e: e.tensor_single_scalar(id16[:], id16i[:], 0.0, ALU.is_equal),
         reads=[b_id16i], writes=[b_id16])
    wT_sb = sb4("wT_sb", [16, T], BF16); b_wT = B("wT_sb")
    s_tm = sb4("s_tm", [128, E], F32); b_s_tm = B("s_tm")
    h16r = sb4("h16r", [128, E], F32); b_h16r = B("h16r")
    t8r = sb4("t8r", [128, 8], F32); b_t8r = B("t8r")
    t8c = sb4("t8c", [128, 8], F32); b_t8c = B("t8c")
    mask16r = sb4("mask16r", [128, E], F32); b_mask16r = B("mask16r")
    wur = sb4("wur", [128, E], F32); b_wur = B("wur")
    wsum = sb4("wsum", [128, 2], F32); b_wsum = B("wsum")
    w_tmr = sb4("w_tmr", [128, E], F32); b_w_tmr = B("w_tmr")
    for tc in range(8):
        t0 = tc * 512
        xs = tc % 2
        for fc in range(FC):
            p.dma("act", out=x4b[xs][:], in_=x1b_s[fc * 128:(fc + 1) * 128, t0:t0 + 512],
                  reads=[b_x1b], writes=[b_x4b[xs]], key=f"x4b{xs}")
            p.op("pe", lambda e, fc=fc, xs=xs: e.matmul(
                pb[2][0:16, :], rw_sb[:, fc * E:(fc + 1) * E], x4b[xs][:],
                start=(fc == 0), stop=(fc == FC - 1)),
                reads=[b_rw, b_x4b[xs]], writes=[b_pb[2]])
            xs = (xs + 1) % 2
        p.op("act", lambda e, t0=t0: e.activation(
            s_f[0:16, t0:t0 + 512], pb[2][0:16, :], AF.Sigmoid),
            reads=[b_pb[2]], writes=[b_s_f])
    for tt in range(NT):
        t0 = tt * 128
        p.op("pe", lambda e, t0=t0: e.transpose(
            pb[3][:, 0:16], s_f[0:16, t0:t0 + 128], id16[:, :]),
            reads=[b_s_f, b_id16], writes=[b_pb[3]])
        p.op("dve", lambda e: e.tensor_copy(s_tm[:], pb[3][:, 0:16]),
             reads=[b_pb[3]], writes=[b_s_tm])
        p.op("dve", lambda e: e.tensor_add(h16r[:], s_tm[:], eb_b[:]),
             reads=[b_s_tm, b_eb_b], writes=[b_h16r])
        p.op("dve", lambda e: e.max(t8r[:], h16r[:]),
             reads=[b_h16r], writes=[b_t8r])
        p.op("act", lambda e: e.copy(t8c[:], t8r[:]),
             reads=[b_t8r], writes=[b_t8c])
        p.op("dve", lambda e: e.tensor_single_scalar(
            mask16r[:], h16r[:], t8c[:, 3:4], ALU.is_ge),
            reads=[b_h16r, b_t8c], writes=[b_mask16r])
        p.op("dve", lambda e: e.tensor_mul(wur[:], mask16r[:], s_tm[:]),
             reads=[b_mask16r, b_s_tm], writes=[b_wur])
        p.op("dve", lambda e: e.reduce_sum(wsum[:, 0:1], wur[:], axis=AX.X),
             reads=[b_wur], writes=[b_wsum])
        p.op("dve", lambda e: e.tensor_single_scalar(
            wsum[:, 1:2], wsum[:, 0:1], 1e-20, ALU.add),
            reads=[b_wsum], writes=[b_wsum])
        p.op("dve", lambda e: e.reciprocal(wsum[:, 0:1], wsum[:, 1:2]),
             reads=[b_wsum], writes=[b_wsum])
        p.op("dve", lambda e: e.tensor_scalar_mul(w_tmr[:], wur[:], wsum[:, 0:1]),
             reads=[b_wur, b_wsum], writes=[b_w_tmr])
        p.op("pe", lambda e: e.transpose(pb[3][0:16, 0:128], w_tmr[:, 0:16], identf[:]),
             reads=[b_w_tmr, b_identf], writes=[b_pb[3]])
        p.op("act", lambda e, t0=t0: e.activation(
            wT_sb[0:16, t0:t0 + 128], pb[3][0:16, 0:128], AF.Copy),
            reads=[b_pb[3]], writes=[b_wT])

    gupb = [sb4("gupb0", [128, FC * 2 * I], BF16)]
    b_gupb = [B("gupb0")]
    wrow = [sb4(f"wrow{i}", [1, 2048], BF16) for i in range(2)]
    b_wrow = [B(f"wrow{i}") for i in range(2)]
    x1w = [sb4(f"x1w{c}", [128, 2048], BF16) for c in range(FC)]
    b_x1w = [B(f"x1w{c}") for c in range(FC)]
    sg_sb = sb4("sg_sb", [128, IC * 2048], BF16); b_sg = B("sg_sb")
    wbc_sb = sb4("wbc_sb", [128, 2048], BF16); b_wbc = B("wbc_sb")
    mid_sb = sb4("mid_sb", [128, 2048], BF16); b_mid_sb = B("mid_sb")

    for tcg in range(2):
        tg0 = tcg * 2048
        for c in range(FC):
            p.dma("act", out=x1w[c][:], in_=x1b_s[c * 128:(c + 1) * 128, tg0:tg0 + 2048],
                  reads=[b_x1b], writes=[b_x1w[c]], key=f"x1w{c % 2}")
        for e_i in range(17):
            s = 0
            if e_i == 16:
                p.dma("pool", out=gupb[s][:].rearrange("p (fc g) -> p fc g", g=2 * I),
                      in_=sgu.rearrange("fc p g -> p fc g"),
                      writes=[b_gupb[s]], key=f"gupb{s}")
            if e_i < 16:
                p.dma("pool", out=gupb[s][:].rearrange("p (fc g) -> p fc g", g=2 * I),
                      in_=gup[e_i].rearrange("fc p g -> p fc g"),
                      writes=[b_gupb[s]], key=f"gupb{s}")
                p.dma("pool", out=wrow[e_i % 2][:], in_=wT_sb[e_i:e_i + 1, tg0:tg0 + 2048],
                      reads=[b_wT], writes=[b_wrow[e_i % 2]], key=f"wrow{e_i % 2}")
                for tq in range(4):
                    p.op("pe", lambda e, e_i=e_i, tq=tq: e.matmul(
                        pb[4 + tq][:, :], ones_row[0:1, :],
                        wrow[e_i % 2][0:1, tq * 512:(tq + 1) * 512],
                        start=True, stop=True),
                        reads=[b_ones_row, b_wrow[e_i % 2]], writes=[b_pb[4 + tq]])
                    p.op("act", lambda e, tq=tq: e.activation(
                        wbc_sb[:, tq * 512:(tq + 1) * 512], pb[4 + tq][:, :], AF.Copy),
                        reads=[b_pb[4 + tq]], writes=[b_wbc])
            for ic in range(IC):
                bs = (ic % 2) * 4
                bwr = [b_pb[bs + j] for j in range(4)]
                def gmm(e, s=s, ic=ic, bs=bs):
                    last = None
                    for fc in range(FC):
                        lw = gupb[s][:, fc * 2 * I + ic * 128:fc * 2 * I + (ic + 1) * 128]
                        for tq in range(4):
                            last = e.matmul(pb[bs + tq][:, :], lw,
                                            x1w[fc][:, tq * 512:(tq + 1) * 512],
                                            start=(fc == 0), stop=(fc == FC - 1))
                    return last
                p.op("pe", gmm, reads=[b_gupb[s]] + b_x1w, writes=bwr)
                for tq in range(4):
                    p.op("act", lambda e, ic=ic, bs=bs, tq=tq: e.activation(
                        sg_sb[:, ic * 2048 + tq * 512:ic * 2048 + (tq + 1) * 512],
                        pb[bs + tq][:, :], AF.Silu),
                        reads=[b_pb[bs + tq]], writes=[b_sg])
            if e_i < 16:
                for ic in range(IC):
                    p.op("dve", lambda e, ic=ic: e.tensor_mul(
                        sg_sb[:, ic * 2048:(ic + 1) * 2048],
                        sg_sb[:, ic * 2048:(ic + 1) * 2048], wbc_sb[:]),
                        reads=[b_sg, b_wbc], writes=[b_sg])
            for ic in range(IC):
                bs = (ic % 2) * 4
                bwr = [b_pb[bs + j] for j in range(4)]
                def umm(e, s=s, ic=ic, bs=bs):
                    last = None
                    for fc in range(FC):
                        lw = gupb[s][:, fc * 2 * I + I + ic * 128:fc * 2 * I + I + (ic + 1) * 128]
                        for tq in range(4):
                            last = e.matmul(pb[bs + tq][:, :], lw,
                                            x1w[fc][:, tq * 512:(tq + 1) * 512],
                                            start=(fc == 0), stop=(fc == FC - 1))
                    return last
                p.op("pe", umm, reads=[b_gupb[s]] + b_x1w, writes=bwr)
                for tq in range(4):
                    p.op("dve", lambda e, ic=ic, bs=bs, tq=tq: e.tensor_mul(
                        mid_sb[:, tq * 512:(tq + 1) * 512],
                        sg_sb[:, ic * 2048 + tq * 512:ic * 2048 + (tq + 1) * 512],
                        pb[bs + tq][:, :]),
                        reads=[b_sg, b_pb[bs + tq]], writes=[b_mid_sb])
                p.dma("sp", out=mid_s[e_i, ic * 128:(ic + 1) * 128, tg0:tg0 + 2048],
                      in_=mid_sb[:], reads=[b_mid_sb], writes=[b_mid], key="st_mid")
    es4.close()
    p.barrier()

    es4b = ExitStack()
    sb4b = lambda n, s, d=F32: es4b.enter_context(nc.sbuf_tensor(n, s, d))
    dcast = sb4b("dcast", [128, 102 * 128], BF16)
    b_dcast = B("dcast")
    for m in range(H):
        p.dma("pool", out=dcast[:], in_=dallm[m].rearrange("p kc q -> p (kc q)"),
              writes=[b_dcast], key="dcast_ld")
        p.dma("sp", out=dbf_s[m], in_=dcast[:],
              reads=[b_dcast], writes=[b_dbf], key="dcast_st")
    es4b.close()
    p.barrier()

    es4c = ExitStack()
    sb4c = lambda n, s, d=F32: es4c.enter_context(nc.sbuf_tensor(n, s, d))
    midwA = sb4c("midwA", [128, 48 * 512], BF16); b_midwA = B("midwA")
    midwB = sb4c("midwB", [128, 54 * 512], BF16); b_midwB = B("midwB")
    dwb = [sb4c(f"dwb{i}", [128, 102 * 128], BF16) for i in range(2)]
    b_dwb = [B(f"dwb{i}") for i in range(2)]
    x1fb = [sb4c(f"x1fb{i}", [128, 512], F32) for i in range(2)]
    b_x1fb = [B(f"x1fb{i}") for i in range(2)]
    r2c = [sb4c(f"r2c{i}", [128, 512], F32) for i in range(2)]
    b_r2c = [B(f"r2c{i}") for i in range(2)]
    r2bc = [sb4c(f"r2bc{i}", [128, 512], BF16) for i in range(2)]
    b_r2bc = [B(f"r2bc{i}") for i in range(2)]

    for tc in range(8):
        t0 = tc * 512
        for e_i in range(17):
            half = 0 if e_i < 8 else 1
            dst = midwA if half == 0 else midwB
            bdst = b_midwA if half == 0 else b_midwB
            base = e_i * IC if half == 0 else (e_i - 8) * IC
            p.dma("sp", out=dst[:, base * 512:(base + IC) * 512]
                  .rearrange("p (ic t) -> p ic t", t=512),
                  in_=mid_s[e_i, :, t0:t0 + 512].rearrange("(ic p) t -> p ic t", p=128),
                  reads=[b_mid], writes=[bdst], key="midw" + ("A" if half == 0 else "B"))
        for m in range(H):
            s = m % 2
            p.dma("sp", out=dwb[s][:], in_=dbf_s[m],
                  reads=[b_dbf], writes=[b_dwb[s]], key=f"dwb{s}")
            bank = m % 4
            def dmmA(e, s=s, bank=bank):
                last = None
                for eic in range(48):
                    last = e.matmul(pb[bank][:, :],
                                    dwb[s][:, eic * 128:(eic + 1) * 128],
                                    midwA[:, eic * 512:(eic + 1) * 512],
                                    start=(eic == 0), stop=False)
                return last
            def dmmB(e, s=s, bank=bank):
                last = None
                for eic in range(54):
                    last = e.matmul(pb[bank][:, :],
                                    dwb[s][:, (48 + eic) * 128:(48 + eic + 1) * 128],
                                    midwB[:, eic * 512:(eic + 1) * 512],
                                    start=False, stop=(eic == 53))
                return last
            p.op("pe", dmmA, reads=[b_dwb[s], b_midwA], writes=[b_pb[bank]])
            p.op("pe", dmmB, reads=[b_dwb[s], b_midwB], writes=[b_pb[bank]])
            xs = m % 2
            p.dma("pool", out=x1fb[xs][:], in_=x1_s[m * 128:(m + 1) * 128, t0:t0 + 512],
                  reads=[b_x1], writes=[b_x1fb[xs]], key=f"x1fb{xs}")
            p.op("dve", lambda e, bank=bank, xs=xs: e.tensor_add(
                r2c[xs][:], pb[bank][:, :], x1fb[xs][:]),
                reads=[b_pb[bank], b_x1fb[xs]], writes=[b_r2c[xs]])
            p.dma("pool", out=r2_s[m * 128:(m + 1) * 128, t0:t0 + 512],
                  in_=r2c[xs][:], reads=[b_r2c[xs]], writes=[b_r2], key=f"st_r2{xs}")
            p.op("act", lambda e, xs=xs: e.activation(r2bc[xs][:], r2c[xs][:], AF.Copy),
                 reads=[b_r2c[xs]], writes=[b_r2bc[xs]])
            p.dma("pool", out=r2b_s[m * 128:(m + 1) * 128, t0:t0 + 512],
                  in_=r2bc[xs][:], reads=[b_r2bc[xs]], writes=[b_r2b], key=f"st_r2b{xs}")
    es4c.close()
    p.barrier()

    es5 = ExitStack()
    sb5 = lambda n, s, d=F32: es5.enter_context(nc.sbuf_tensor(n, s, d))
    rowbuf3 = sb5("rowbuf3", [1, 3 * T], F32)
    b_rowbuf3 = B("rowbuf3")
    rstd2, b22 = ln_stats(r2b_s, b_r2b, rowbuf3, b_rowbuf3, sb5, "2")
    ln_apply(r2_s, b_r2, rstd2, b22, b_rowbuf3, out, b_out, None, None,
             sb5, "2")
    es5.close()
    p.barrier()
    out_names = ["out"]
    if "scratch" in dev_taps:
        out_names += ["qpb_s", "kpb_s", "vtm_s", "avtm_s", "x1_s", "mid_s"]
    if "rows" in dev_taps:
        rows_o = DT("rows_o", [1, 2 * T], F32, kind="ExternalOutput").ap()
        b_rows_o = B("rows_o")
        p.dma("sp", out=rows_o[:, :], in_=rowbuf[:], reads=[b_rowbuf],
              writes=[b_rows_o], key="st_rows")
        out_names.append("rows_o")
    p.finish("pool")
    p.emit()
    return nc, es, out_names


def _host_layout(inputs):
    x = inputs["L0_attn_x"]
    wq = np.asarray(inputs["L0_attn_wq"])
    wk = np.asarray(inputs["L0_attn_wk"])
    wv = np.asarray(inputs["L0_attn_wv"])
    wo = np.asarray(inputs["L0_attn_wo"])
    gup = np.asarray(inputs["L0_ffn_gate_up_proj"])
    down = np.asarray(inputs["L0_ffn_down_proj"])
    sgw = np.asarray(inputs["L0_ffn_shared_gate_w"])
    suw = np.asarray(inputs["L0_ffn_shared_up_w"])
    sdw = np.asarray(inputs["L0_ffn_shared_down_w"])

    def mblock(w):
        wT = np.ascontiguousarray(w.T)
        return np.ascontiguousarray(
            wT.reshape(FC, 128, H, 128).transpose(2, 0, 1, 3))

    wqm = mblock(wq); wkm = mblock(wk); wom = mblock(wo)
    wvT = np.ascontiguousarray(wv.T)
    wvdc = np.ascontiguousarray(
        wvT.reshape(FC, 128, 4, 512).transpose(2, 0, 1, 3))

    def cols(v):
        return np.ascontiguousarray(v.reshape(FC, 128).T)

    gupT = np.ascontiguousarray(
        gup.transpose(0, 2, 1).reshape(E, FC, 128, 2 * I))
    sguT = np.concatenate([sgw.T, suw.T], axis=1)
    sguT = np.ascontiguousarray(sguT.reshape(FC, 128, 2 * I))

    dT = np.concatenate([down[e].T for e in range(E)] + [sdw.T], axis=0)
    dallm = np.ascontiguousarray(
        dT.reshape(102, 128, H, 128).transpose(2, 1, 0, 3))

    base = {
        "wqm": wqm, "wkm": wkm, "wvdc": wvdc, "wom": wom,
        "bqr": np.asarray(inputs["L0_attn_bq"])[None, :],
        "bkr": np.asarray(inputs["L0_attn_bk"])[None, :],
        "boc": cols(np.asarray(inputs["L0_attn_bo"])),
        "gqc": cols(np.asarray(inputs["L0_attn_gq"])),
        "gkc": cols(np.asarray(inputs["L0_attn_gk"])),
        "bvr": np.asarray(inputs["L0_attn_bv"])[None, :],
        "wmix": np.asarray(inputs["L0_attn_w_mix"]),
        "rwT": np.ascontiguousarray(np.asarray(inputs["L0_ffn_router_w"]).T),
        "ebr": np.asarray(inputs["L0_ffn_e_bias"])[None, :],
        "gup": gupT, "sgu": sguT, "dallm": dallm,
    }
    in_maps = []
    for b in range(8):
        m = dict(base)
        m["xT"] = np.ascontiguousarray(np.asarray(x[b]).T)
        in_maps.append(m)
    return in_maps


def kernel(**inputs):
    in_maps = _host_layout(inputs)
    nc, es, out_names = build_program()
    try:
        res = run_bass_kernel_spmd(nc, in_maps, core_ids=list(range(8)))
    finally:
        es.close()
    outs = [np.ascontiguousarray(np.asarray(r["out"], np.float32).T)
            for r in res.results]
    return np.stack(outs, axis=0)


# version 09
# speedup vs baseline: 1.1598x; 1.0188x over previous
import math
from collections import defaultdict
from contextlib import ExitStack

import numpy as np
import concourse.bass as bass
import concourse.mybir as mybir
from concourse.bass_utils import run_bass_kernel_spmd

F32 = mybir.dt.float32
BF16 = mybir.dt.bfloat16
AF = mybir.ActivationFunctionType
ALU = mybir.AluOpType
AX = mybir.AxisListType

T = 4096
DM = 2048
H = 16
D = 128
FC = DM // 128
E = 16
I = 768
IC = I // 128
TOPK = 4
SCALE = D ** -0.5
EPS_RMS = 1e-6
EPS_LN = 1e-5
SEQ_SHIFTS = [0, 1, -1, 3, -3, 7, -7, 20, -20, 53, -53, 141, -141, 380, -380, 1024]
CH_SHIFTS = [0, 1, 2, 4, 8]
SEGS = [-3, -2, -1, 0, 1, 2, 3, 8]
NSEG = len(SEGS)
SEG_DIAGS = []
for _d in SEGS:
    SEG_DIAGS.append(sorted(s - _d * 128 for s in SEQ_SHIFTS
                            if -128 < s - _d * 128 < 128))
NT = T // 128


class Buf:
    __slots__ = ("name", "writer", "readers")

    def __init__(self, name):
        self.name = name
        self.writer = None
        self.readers = []


class Prog:
    ENGINES = ("pe", "act", "dve", "pool", "sp")

    def __init__(self, nc, es):
        self.nc = nc
        self.es = es
        self.sem = {}
        self.count = defaultdict(int)
        self.hwm = defaultdict(int)
        self.queue = {e: [] for e in self.ENGINES}
        self.nops = 0
        for e in self.ENGINES:
            self.sem[e] = es.enter_context(nc.semaphore(f"sem_{e}"))

    def buf(self, name):
        return Buf(name)

    def _dma_sem(self, key):
        skey = "dma_" + key
        if skey not in self.sem:
            self.sem[skey] = self.es.enter_context(
                self.nc.semaphore(f"sem_dma_{key}"))
        return skey

    def _wait(self, waiter_eng, src_stream, val):
        if val <= 0:
            return
        if src_stream.startswith("dma_"):
            val = self.count[src_stream]
        k = (waiter_eng, src_stream)
        if self.hwm[k] >= val:
            return
        self.hwm[k] = val
        sem = self.sem[src_stream]
        tgt = val * (16 if src_stream.startswith("dma_") else 1)
        self.queue[waiter_eng].append(lambda e: e.wait_ge(sem, tgt))

    def _pre(self, eng, reads, writes):
        need = {}
        bar = getattr(self, "barrier_counts", None)
        if bar:
            for s, c in bar.items():
                if c > need.get(s, 0):
                    need[s] = c
        for b in reads:
            if b.writer is not None:
                s, c = b.writer
                if c > need.get(s, 0):
                    need[s] = c
        for b in writes:
            if b.writer is not None:
                s, c = b.writer
                if c > need.get(s, 0):
                    need[s] = c
            for (s, c) in b.readers:
                if c > need.get(s, 0):
                    need[s] = c
        for s, c in need.items():
            self._wait(eng, s, c)

    def barrier(self):
        self.barrier_counts = dict(self.count)

    def _post(self, stream, reads, writes):
        self.count[stream] += 1
        c = self.count[stream]
        for b in writes:
            b.writer = (stream, c)
            b.readers = []
        for b in reads:
            b.readers.append((stream, c))
        return c

    def op(self, eng, fn, reads=(), writes=()):
        self._pre(eng, reads, writes)
        sem = self.sem[eng]
        self.queue[eng].append(lambda e: fn(e).then_inc(sem, 1))
        self.nops += 1
        return self._post(eng, reads, writes)

    def dma(self, queue, out=None, in_=None, reads=(), writes=(), key=None, **kw):
        if key is None:
            key = writes[0].name if writes else (reads[0].name if reads else queue)
        stream = self._dma_sem(key)
        self._pre(queue, reads, writes)
        sem = self.sem[stream]
        o, i = out, in_
        self.queue[queue].append(
            lambda e: e.dma_start(out=o, in_=i, **kw).then_inc(sem, 16))
        self.nops += 1
        return self._post(stream, reads, writes)

    def dma_transpose(self, queue, out, in_, reads=(), writes=(), key=None):
        if key is None:
            key = writes[0].name
        stream = self._dma_sem(key)
        self._pre(queue, reads, writes)
        sem = self.sem[stream]
        self.queue[queue].append(
            lambda e: e.dma_start(out=out, in_=in_, transpose=True).then_inc(sem, 16))
        self.nops += 1
        return self._post(stream, reads, writes)

    def finish(self, eng="pool"):
        for s, c in list(self.count.items()):
            if s != eng and c > 0:
                self._wait(eng, s, c)

    def emit(self):
        with self.nc.Block() as block:
            for eng_name, starter in (("pe", block.tensor), ("act", block.scalar),
                                      ("dve", block.vector), ("pool", block.gpsimd),
                                      ("sp", block.sync)):
                q = self.queue[eng_name]
                if q:
                    def body(e, q=q):
                        for fn in q:
                            fn(e)
                    starter(body)


def circ_runs(a, ln, period=T):
    a %= period
    out = []
    while ln > 0:
        take = min(ln, period - a)
        out.append((a, take))
        ln -= take
        a = 0
    return out


def build_program(dev_taps=()):
    es = ExitStack()
    nc = bass.Bass("TRN2", target_bir_lowering=False, debug=False)
    p = Prog(nc, es)

    kind_scr = "ExternalOutput" if "scratch" in dev_taps else "Internal"

    DT = nc.dram_tensor
    xT = DT("xT", [DM, T], F32, kind="ExternalInput").ap()
    wqm = DT("wqm", [H, FC, 128, 128], F32, kind="ExternalInput").ap()
    wkm = DT("wkm", [H, FC, 128, 128], F32, kind="ExternalInput").ap()
    wvdc = DT("wvdc", [4, FC, 128, 512], F32, kind="ExternalInput").ap()
    wom = DT("wom", [H, FC, 128, 128], F32, kind="ExternalInput").ap()
    bqr = DT("bqr", [1, DM], F32, kind="ExternalInput").ap()
    bkr = DT("bkr", [1, DM], F32, kind="ExternalInput").ap()
    boc = DT("boc", [128, FC], F32, kind="ExternalInput").ap()
    gqc = DT("gqc", [128, FC], F32, kind="ExternalInput").ap()
    gkc = DT("gkc", [128, FC], F32, kind="ExternalInput").ap()
    bvr = DT("bvr", [1, DM], F32, kind="ExternalInput").ap()
    wmix = DT("wmix", [1, 5], F32, kind="ExternalInput").ap()
    rwT = DT("rwT", [DM, E], F32, kind="ExternalInput").ap()
    ebr = DT("ebr", [1, E], F32, kind="ExternalInput").ap()
    gup = DT("gup", [E, FC, 128, 2 * I], F32, kind="ExternalInput").ap()
    sgu = DT("sgu", [FC, 128, 2 * I], F32, kind="ExternalInput").ap()
    dallm = DT("dallm", [H, 128, 102, 128], F32, kind="ExternalInput").ap()

    out = DT("out", [DM, T], F32, kind="ExternalOutput").ap()

    qpb_s = DT("qpb_s", [DM, T], BF16, kind=kind_scr).ap()
    kpb_s = DT("kpb_s", [DM, T], BF16, kind=kind_scr).ap()
    vtm_s = DT("vtm_s", [H, T, 128], BF16, kind=kind_scr).ap()
    avtm_s = DT("avtm_s", [T, DM], BF16, kind=kind_scr).ap()
    r1_s = DT("r1_s", [DM, T], F32, kind="Internal").ap()
    r1b_s = DT("r1b_s", [DM, T], BF16, kind="Internal").ap()
    x1_s = DT("x1_s", [DM, T], F32, kind=kind_scr).ap()
    x1b_s = DT("x1b_s", [DM, T], BF16, kind="Internal").ap()
    mid_s = DT("mid_s", [17, I, T], BF16, kind=kind_scr).ap()
    dbf_s = DT("dbf_s", [H, 128, 102 * 128], BF16, kind="Internal").ap()
    r2_s = DT("r2_s", [DM, T], F32, kind="Internal").ap()
    r2b_s = DT("r2b_s", [DM, T], BF16, kind="Internal").ap()

    sb = lambda n, s, d=F32: es.enter_context(nc.sbuf_tensor(n, s, d))
    pb = [es.enter_context(nc.psum_tensor(f"pb{i}", [128, 512], F32))
          for i in range(8)]
    b_pb = [p.buf(f"pb{i}") for i in range(8)]

    B = p.buf
    ones_row = sb("ones_row", [1, 128], BF16)
    ones_r512 = sb("ones_r512", [1, 512], BF16)
    ones_col = sb("ones_col", [128, 1], BF16)
    ident = sb("ident", [128, 128], BF16)
    identf = sb("identf", [128, 128], F32)
    stencil = sb("stencil", [128, 128], BF16)
    amask = sb("amask", [128, NSEG * 128], BF16)
    bq_r = sb("bq_r", [1, DM], BF16)
    bk_r = sb("bk_r", [1, DM], BF16)
    bo_c = sb("bo_c", [128, FC], F32)
    gq_c = sb("gq_c", [128, FC], F32)
    gk_c = sb("gk_c", [128, FC], F32)
    wmix_r = sb("wmix_r", [1, 8], F32)
    wmix_c = sb("wmix_c", [128, 8], F32)
    eb_r = sb("eb_r", [1, E], BF16)
    eb_b = sb("eb_b", [128, E], F32)
    b_ones_row, b_ones_col, b_ident = B("ones_row"), B("ones_col"), B("ident")
    b_identf = B("identf")
    b_ones_r512 = B("ones_r512")
    b_stencil = B("stencil")
    b_amask = B("amask")
    b_bq_r, b_bk_r, b_bo_c, b_gq_c, b_gk_c = B("bq_r"), B("bk_r"), B("bo_c"), B("gq_c"), B("gk_c")
    b_wmix_r, b_wmix_c, b_eb_r, b_eb_b = B("wmix_r"), B("wmix_c"), B("eb_r"), B("eb_b")

    b_qpb = B("qpb_s"); b_kpb = B("kpb_s"); b_vtm = B("vtm_s")
    b_avtm = B("avtm_s"); b_r1 = B("r1_s"); b_r1b = B("r1b_s")
    b_x1 = B("x1_s"); b_mid = B("mid_s")
    b_x1btc = [B(f"x1b_s{i}") for i in range(8)]
    b_dbf = B("dbf_s"); b_r2 = B("r2_s"); b_r2b = B("r2b_s"); b_out = B("out")

    constcols = sb("constcols", [128, 8], F32)
    b_constcols = B("constcols")
    _ccn = [0]
    def register_const(v):
        i = _ccn[0]
        _ccn[0] += 1
        p.op("dve", lambda e, i=i, v=v: e.memset(constcols[:, i:i + 1], float(v)),
             writes=[b_constcols])
        nc.const_aps.register(F32, float(v), constcols[:, i:i + 1])
    for _v in (EPS_RMS, EPS_LN, 1e-20):
        register_const(_v)

    rstd_rb = sb("rstd_rb", [1, 2 * T], BF16)
    b_rstd_rb = B("rstd_rb")
    ROW_Q, ROW_K = 0, 1
    es0 = ExitStack()
    sb0 = lambda n, s, d=F32: es0.enter_context(nc.sbuf_tensor(n, s, d))
    iotaD = sb0("iotaD", [128, 128], F32)
    stenf = sb0("stenf", [128, 128], F32)
    amaskf = sb0("amaskf", [128, NSEG * 128], F32)
    b_iotaD, b_stenf, b_amaskf = B("iotaD"), B("stenf"), B("amaskf")

    p.op("dve", lambda e: e.memset(ones_row[:], 1.0), writes=[b_ones_row])
    p.op("dve", lambda e: e.memset(ones_r512[:], 1.0), writes=[b_ones_r512])
    p.op("dve", lambda e: e.memset(ones_col[:], 1.0), writes=[b_ones_col])
    p.op("pool", lambda e: e.iota(iotaD[:], pattern=[[1, 128]], base=0,
                                  channel_multiplier=-1,
                                  allow_small_or_imprecise_dtypes=True),
         writes=[b_iotaD])
    p.op("dve", lambda e: e.tensor_single_scalar(identf[:], iotaD[:], 0.0, ALU.is_equal),
         reads=[b_iotaD], writes=[b_identf])
    p.op("dve", lambda e: e.tensor_copy(ident[:], identf[:]),
         reads=[b_identf], writes=[b_ident])
    for name, srcap, dst, bdst in (("boc", boc, bo_c, b_bo_c), ("gqc", gqc, gq_c, b_gq_c),
                                   ("gkc", gkc, gk_c, b_gk_c)):
        p.dma("sp", out=dst[:], in_=srcap[:, :], writes=[bdst])
    p.dma("pool", out=bq_r[:], in_=bqr[:, :], writes=[b_bq_r])
    p.dma("pool", out=bk_r[:], in_=bkr[:, :], writes=[b_bk_r])

    p.dma("sp", out=wmix_r[0:1, 0:5], in_=wmix[:, :], writes=[b_wmix_r])
    p.dma("pool", out=eb_r[:], in_=ebr[:, :], writes=[b_eb_r])

    wmix_rb = sb0("wmix_rb", [1, 8], BF16)
    b_wmix_rb = B("wmix_rb")
    p.op("dve", lambda e: e.memset(wmix_rb[:], 0.0), writes=[b_wmix_rb])
    p.op("dve", lambda e: e.tensor_copy(wmix_rb[0:1, 0:5], wmix_r[0:1, 0:5]),
         reads=[b_wmix_r], writes=[b_wmix_rb])
    p.op("pe", lambda e: e.matmul(pb[6][:, 0:8], ones_row[0:1, :], wmix_rb[0:1, :],
                                  start=True, stop=True),
         reads=[b_ones_row, b_wmix_rb], writes=[b_pb[6]])
    p.op("dve", lambda e: e.tensor_copy(wmix_c[:], pb[6][:, 0:8]),
         reads=[b_pb[6]], writes=[b_wmix_c])
    p.op("pe", lambda e: e.matmul(pb[6][:, 0:E], ones_row[0:1, :], eb_r[0:1, :],
                                  start=True, stop=True),
         reads=[b_ones_row, b_eb_r], writes=[b_pb[6]])
    p.op("dve", lambda e: e.tensor_copy(eb_b[:], pb[6][:, 0:E]),
         reads=[b_pb[6]], writes=[b_eb_b])

    p.op("dve", lambda e: e.memset(stenf[:], 0.0), writes=[b_stenf])
    mtmp = sb0("mtmp", [128, 128], F32)
    mtmp2 = sb0("mtmp2", [128, 128], F32)
    b_mtmp, b_mtmp2 = B("mtmp"), B("mtmp2")
    for ci, c in enumerate(CH_SHIFTS):
        p.op("dve", lambda e, c=c: e.tensor_single_scalar(mtmp[:], iotaD[:], float(c), ALU.is_equal),
             reads=[b_iotaD], writes=[b_mtmp])
        p.op("dve", lambda e, c=c: e.tensor_single_scalar(mtmp2[:], iotaD[:], float(c - 128), ALU.is_equal),
             reads=[b_iotaD], writes=[b_mtmp2])
        p.op("dve", lambda e: e.tensor_add(mtmp[:], mtmp[:], mtmp2[:]),
             reads=[b_mtmp, b_mtmp2], writes=[b_mtmp])
        p.op("dve", lambda e, ci=ci: e.scalar_tensor_tensor(
            stenf[:], mtmp[:], wmix_c[:, ci:ci + 1], stenf[:], ALU.mult, ALU.add),
            reads=[b_mtmp, b_wmix_c, b_stenf], writes=[b_stenf])
    p.op("act", lambda e: e.activation(stencil[:], stenf[:], AF.Copy, scale=SCALE),
         reads=[b_stenf], writes=[b_stencil])

    p.op("dve", lambda e: e.memset(amaskf[:], 0.0), writes=[b_amaskf])
    for si in range(NSEG):
        for dlt in SEG_DIAGS[si]:
            p.op("dve", lambda e, si=si, dlt=dlt: e.tensor_single_scalar(
                mtmp[:], iotaD[:], float(dlt), ALU.is_equal),
                reads=[b_iotaD], writes=[b_mtmp])
            p.op("dve", lambda e, si=si: e.tensor_max(
                amaskf[:, si * 128:(si + 1) * 128],
                amaskf[:, si * 128:(si + 1) * 128], mtmp[:]),
                reads=[b_amaskf, b_mtmp], writes=[b_amaskf])
    p.op("dve", lambda e: e.tensor_copy(amask[:], amaskf[:]),
         reads=[b_amaskf], writes=[b_amask])

    es0.close()
    p.barrier()

    es1 = ExitStack()
    sb1 = lambda n, s, d=F32: es1.enter_context(nc.sbuf_tensor(n, s, d))
    rowbuf = sb1("rowbuf", [1, T], F32)
    b_rowbuf = B("rowbuf")
    xh = [sb1(f"xh{c}", [128, T], BF16) for c in range(FC)]
    b_xh = [B(f"xh{c}") for c in range(FC)]
    def row(i):
        return rowbuf[0:1, 0:T]
    def rowslc(i, a, ln):
        return rowbuf[0:1, a:a + ln]
    for c in range(FC):
        p.dma("pool", out=xh[c][:], in_=xT[c * 128:(c + 1) * 128, :],
              writes=[b_xh[c]], key=f"xh{c % 2}")

    wqb = [sb1(f"wqb{i}", [128, FC * 128], BF16) for i in range(2)]
    b_wqb = [B(f"wqb{i}") for i in range(2)]
    wvb = [sb1("wvb0", [128, FC * 512], BF16)]
    b_wvb = [B("wvb0")]
    sqv = [sb1(f"sqv{i}", [128, 512], BF16) for i in range(2)]
    b_sqv = [B(f"sqv{i}") for i in range(2)]
    pj_sb = [sb1(f"pj_sb{i}", [128, 512], BF16) for i in range(4)]
    b_pj = [B(f"pj_sb{i}") for i in range(4)]
    bv_r = sb1("bv_r", [1, DM], BF16)
    b_bv_r = B("bv_r")
    p.dma("pool", out=bv_r[:], in_=bvr[:, :], writes=[b_bv_r])

    def qk_sweep(which):
        wsrc = wqm if which == "q" else wkm
        bias_r = bq_r if which == "q" else bk_r
        b_bias = b_bq_r if which == "q" else b_bk_r
        dst = qpb_s if which == "q" else kpb_s
        b_dst = b_qpb if which == "q" else b_kpb
        rowi = ROW_Q if which == "q" else ROW_K
        TCB = [0, 1, 2, 7]
        for g in range(2):
            for m in range(H):
                s = m % 2
                p.dma("pool", out=wqb[s][:].rearrange("p (fc q) -> p fc q", q=128),
                      in_=wsrc[m].rearrange("fc p q -> p fc q"),
                      writes=[b_wqb[s]], key=f"wqk{s}")
                tg0 = g * 2048
                for hb in range(2):
                    tcis = (hb * 2, hb * 2 + 1)
                    bwr = [b_pb[TCB[t]] for t in tcis]
                    def proj(e, m=m, s=s, tg0=tg0, tcis=tcis):
                        last = None
                        for fc in range(FC):
                            lw = wqb[s][:, fc * 128:(fc + 1) * 128]
                            for tci in tcis:
                                last = e.matmul(pb[TCB[tci]][:, :], lw,
                                                xh[fc][:, tg0 + tci * 512:tg0 + (tci + 1) * 512],
                                                start=(fc == 0), stop=False)
                        for tci in tcis:
                            last = e.matmul(pb[TCB[tci]][:, :],
                                            bias_r[0:1, m * 128:(m + 1) * 128],
                                            ones_r512[0:1, :],
                                            start=False, stop=True)
                        return last
                    p.op("pe", proj, reads=[b_wqb[s], b_bias, b_ones_r512] + b_xh,
                         writes=bwr)
                for tci in range(4):
                    t0 = g * 2048 + tci * 512
                    bank = TCB[tci]
                    p.op("act", lambda e, bank=bank, tci=tci: e.activation(
                        pj_sb[tci][:], pb[bank][:, :], AF.Copy),
                        reads=[b_pb[bank]], writes=[b_pj[tci]])
                    p.op("act", lambda e, bank=bank, tci=tci: e.activation(
                        sqv[tci % 2][:], pb[bank][:, :], AF.Square),
                        reads=[b_pb[bank]], writes=[b_sqv[tci % 2]])
                    p.dma("sp", out=dst[m * 128:(m + 1) * 128, t0:t0 + 512],
                          in_=pj_sb[tci][:], reads=[b_pj[tci]], writes=[b_dst],
                          key=f"st_{which}{tci}")
                    rb = 3 + tci
                    p.op("pe", lambda e, rb=rb, m=m, tci=tci: e.matmul(
                        pb[rb][0:1, :], ones_col[:, 0:1], sqv[tci % 2][:],
                        start=(m == 0), stop=(m == H - 1)),
                        reads=[b_ones_col, b_sqv[tci % 2]], writes=[b_pb[rb]])
            for tci in range(4):
                t0 = g * 2048 + tci * 512
                rb = 3 + tci
                p.op("act", lambda e, rb=rb, t0=t0, rowi=rowi: e.activation(
                    rowslc(rowi, t0, 512), pb[rb][0:1, :], AF.Copy),
                    reads=[b_pb[rb]], writes=[b_rowbuf])
        p.op("act", lambda e: e.activation(
            row(0), row(0), AF.Ln, scale=1.0 / DM, bias=EPS_RMS),
            reads=[b_rowbuf, b_constcols], writes=[b_rowbuf])
        p.op("act", lambda e: e.activation(
            row(0), row(0), AF.Exp, scale=-0.5),
            reads=[b_rowbuf], writes=[b_rowbuf])
        p.op("act", lambda e, rowi=rowi: e.activation(
            rstd_rb[0:1, rowi * T:(rowi + 1) * T], row(0), AF.Copy),
            reads=[b_rowbuf], writes=[b_rstd_rb])

    def v_sweep():
        for g in range(2):
            for dc in range(4):
                s = 0
                p.dma("pool", out=wvb[s][:].rearrange("p (fc q) -> p fc q", q=512),
                      in_=wvdc[dc].rearrange("fc p q -> p fc q"),
                      writes=[b_wvb[s]], key=f"wv{s}")
                for tt in range(16):
                    t0 = g * 2048 + tt * 128
                    bank = [0, 1, 2, 7][tt % 4]
                    def vproj(e, s=s, t0=t0, dc=dc, bank=bank):
                        last = None
                        for fc in range(FC):
                            last = e.matmul(pb[bank][:, :],
                                            xh[fc][:, t0:t0 + 128],
                                            wvb[s][:, fc * 512:(fc + 1) * 512],
                                            start=(fc == 0), stop=False)
                        return e.matmul(pb[bank][:, :], ones_row[0:1, :],
                                        bv_r[0:1, dc * 512:(dc + 1) * 512],
                                        start=False, stop=True)
                    p.op("pe", vproj,
                         reads=[b_wvb[s], b_ones_row, b_bv_r] + b_xh,
                         writes=[b_pb[bank]])
                    st = tt % 4
                    p.op("act", lambda e, bank=bank, st=st: e.activation(
                        pj_sb[st][:], pb[bank][:, :], AF.Copy),
                        reads=[b_pb[bank]], writes=[b_pj[st]])
                    for hh in range(4):
                        h = dc * 4 + hh
                        p.dma("sp",
                              out=vtm_s[h, t0:t0 + 128, :],
                              in_=pj_sb[st][:, hh * 128:(hh + 1) * 128],
                              reads=[b_pj[st]], writes=[b_vtm], key=f"st_v{st}")

    qk_sweep("q")
    qk_sweep("k")
    v_sweep()



    es1.close()
    p.barrier()
    es2 = ExitStack()
    sb2 = lambda n, s, d=F32: es2.enter_context(nc.sbuf_tensor(n, s, d))

    qpbh = [sb2(f"qpbh{i}", [128, T], BF16) for i in range(2)]
    kpbh = [sb2(f"kpbh{i}", [128, T], BF16) for i in range(2)]
    vtmh = [sb2(f"vtmh{i}", [128, NT * 132], BF16) for i in range(2)]
    b_qpbh = [B(f"qpbh{i}") for i in range(2)]
    b_kpbh = [B(f"kpbh{i}") for i in range(2)]
    b_vtmh = [B(f"vtmh{i}") for i in range(2)]
    for i in range(2):
        p.op("dve", lambda e, i=i: e.memset(vtmh[i][:], 1.0), writes=[b_vtmh[i]])
    qn = sb2("qn", [128, T], BF16); b_qn = B("qn")
    kn = sb2("kn", [128, T], BF16); b_kn = B("kn")
    qm = sb2("qm", [128, T], BF16); b_qm = B("qm")
    emr = [sb2(f"emr{i}", [128, NSEG * 128], BF16) for i in range(2)]
    b_emr = [B(f"emr{i}") for i in range(2)]
    e_all = sb2("e_all", [128, NT * NSEG * 128], BF16)
    b_e_allg = [B(f"e_all{gidx}") for gidx in range(4)]
    av_sb = sb2("av_sb", [128, 2048], BF16); b_av_sb = B("av_sb")
    chain = sb2("chain", [128, 1], F32); b_chain = B("chain")
    brow = sb2("brow", [1, 128], F32); b_brow = B("brow")
    bval = sb2("bval", [1, 1], F32); b_bval = B("bval")
    bvalb = sb2("bvalb", [1, 1], BF16); b_bvalb = B("bvalb")
    negb = sb2("negb", [128, 1], F32); b_negb = B("negb")
    zr16 = sb2("zr16", [128, 16], F32); b_zr16 = B("zr16")
    mcol = sb2("mcol", [128, 1], F32); b_mcol = B("mcol")

    for h in range(H):
        s = h % 2
        p.dma("act", out=qpbh[s][:], in_=qpb_s[h * 128:(h + 1) * 128, :],
              reads=[b_qpb], writes=[b_qpbh[s]])
        p.dma("act", out=kpbh[s][:], in_=kpb_s[h * 128:(h + 1) * 128, :],
              reads=[b_kpb], writes=[b_kpbh[s]])
        p.dma("act", out=vtmh[s][:].rearrange("p (tt q) -> p tt q", q=132)[:, :, 0:128],
              in_=vtm_s[h].rearrange("(tt p) q -> p tt q", p=128),
              reads=[b_vtm], writes=[b_vtmh[s]])
        for which in ("q", "k"):
            srcb = qpbh[s] if which == "q" else kpbh[s]
            b_srcb = b_qpbh[s] if which == "q" else b_kpbh[s]
            dstb = qn if which == "q" else kn
            b_dstb = b_qn if which == "q" else b_kn
            g_c = gq_c if which == "q" else gk_c
            b_g_c = b_gq_c if which == "q" else b_gk_c
            ri = ROW_Q if which == "q" else ROW_K
            for tc in range(8):
                t0 = tc * 512
                p.op("pe", lambda e, ri=ri, t0=t0: e.matmul(
                    pb[6][:, :], ones_row[0:1, :],
                    rstd_rb[0:1, ri * T + t0:ri * T + t0 + 512],
                    start=True, stop=True),
                    reads=[b_ones_row, b_rstd_rb], writes=[b_pb[6]])
                p.op("dve", lambda e, t0=t0, srcb=srcb, dstb=dstb, g_c=g_c, h=h:
                     e.scalar_tensor_tensor(
                         dstb[:, t0:t0 + 512], srcb[:, t0:t0 + 512],
                         g_c[:, h:h + 1], pb[6][:, :], ALU.mult, ALU.mult),
                     reads=[b_srcb, b_g_c, b_pb[6]], writes=[b_dstb])
        for tc in range(8):
            t0 = tc * 512
            p.op("pe", lambda e, t0=t0: e.matmul(
                pb[6][:, :], stencil[:, :], qn[:, t0:t0 + 512],
                start=True, stop=True),
                reads=[b_stencil, b_qn], writes=[b_pb[6]])
            p.op("act", lambda e, t0=t0: e.activation(
                qm[:, t0:t0 + 512], pb[6][:, :], AF.Copy),
                reads=[b_pb[6]], writes=[b_qm])
        p.op("dve", lambda e: e.memset(chain[:], -1e30), writes=[b_chain])
        for sc in (0, 8, 16, 24):
            p.op("pe", lambda e, sc=sc: e.matmul(
                pb[6][:, 0:128], kn[:, sc * 128:sc * 128 + 128],
                qm[:, sc * 128:sc * 128 + 128], start=True, stop=True),
                reads=[b_kn, b_qm], writes=[b_pb[6]])
            p.op("dve", lambda e: e.reduce_max(mcol[:, 0:1], pb[6][:, 0:128], axis=AX.X),
                 reads=[b_pb[6]], writes=[b_mcol])
            p.op("dve", lambda e: e.tensor_max(chain[:], chain[:], mcol[:]),
                 reads=[b_chain, b_mcol], writes=[b_chain])
        p.op("pe", lambda e: e.transpose(pb[6][0:1, 0:128], chain[:, 0:1], identf[:]),
             reads=[b_chain, b_identf], writes=[b_pb[6]])
        p.op("act", lambda e: e.activation(brow[:], pb[6][0:1, 0:128], AF.Copy),
             reads=[b_pb[6]], writes=[b_brow])
        p.op("dve", lambda e: e.reduce_max(bval[:, 0:1], brow[:], axis=AX.X),
             reads=[b_brow], writes=[b_bval])
        p.op("dve", lambda e: e.tensor_copy(bvalb[:], bval[:]),
             reads=[b_bval], writes=[b_bvalb])
        p.op("pe", lambda e: e.matmul(pb[6][:, 0:1], ones_row[0:1, :], bvalb[0:1, :],
                                      start=True, stop=True),
             reads=[b_ones_row, b_bvalb], writes=[b_pb[6]])
        p.op("act", lambda e: e.activation(negb[:], pb[6][:, 0:1], AF.Copy,
                                           scale=-1.0, bias=-5.0),
             reads=[b_pb[6], b_constcols], writes=[b_negb])

        for kc in range(NT):
            sbank = (kc % 3) * 2
            def smm(e, kc=kc, sbank=sbank):
                last = None
                pieces = []
                for (a2, l2) in circ_runs(((kc - 3) % NT) * 128, 896):
                    pieces.append((a2, l2))
                off = 0
                for (a2, l2) in pieces:
                    while l2 > 0:
                        bank = sbank + (1 if off >= 512 else 0)
                        boff = off - 512 if off >= 512 else off
                        take = min(l2, 512 - boff)
                        last = e.matmul(pb[bank][:, boff:boff + take],
                                        kn[:, kc * 128:kc * 128 + 128],
                                        qm[:, a2:a2 + take],
                                        start=True, stop=True)
                        off += take
                        a2 += take
                        l2 -= take
                q8 = ((kc + 8) % NT) * 128
                last = e.matmul(pb[sbank + 1][:, 384:512],
                                kn[:, kc * 128:kc * 128 + 128],
                                qm[:, q8:q8 + 128], start=True, stop=True)
                return last
            p.op("pe", smm, reads=[b_kn, b_qm],
                 writes=[b_pb[sbank], b_pb[sbank + 1]])
            kp = kc % 2
            for hf in range(2):
                p.op("act", lambda e, hf=hf, sbank=sbank, kp=kp: e.activation(
                    emr[kp][:, hf * 512:(hf + 1) * 512], pb[sbank + hf][:, :],
                    AF.Exp, bias=negb[:, 0:1]),
                    reads=[b_pb[sbank + hf], b_negb], writes=[b_emr[kp]])
            p.op("dve" if kp == 0 else "pool", lambda e, kc=kc, kp=kp: e.tensor_mul(
                e_all[:, kc * 1024:(kc + 1) * 1024], emr[kp][:], amask[:]),
                reads=[b_emr[kp], b_amask], writes=[b_e_allg[kc // 8]])
        for tt in range(NT):
            avbank = tt % 2
            def avz(e, tt=tt, avbank=avbank, s=s):
                last = None
                for j, dseg in enumerate(SEGS):
                    kc = (tt - dseg) % NT
                    lw = e_all[:, (kc * 8 + j) * 128:(kc * 8 + j) * 128 + 128]
                    last = e.matmul(pb[avbank][:, 0:129], lw,
                                    vtmh[s][:, kc * 132:kc * 132 + 129],
                                    start=(j == 0), stop=(j == NSEG - 1))
                return last
            gset = sorted({((tt - dseg) % NT) // 8 for dseg in SEGS})
            p.op("pe", avz, reads=[b_e_allg[gidx] for gidx in gset] + [b_vtmh[s]],
                 writes=[b_pb[avbank]])
            p.op("dve", lambda e, avbank=avbank: e.reciprocal(
                zr16[:, 0:1], pb[avbank][:, 128:129]),
                reads=[b_pb[avbank]], writes=[b_zr16])
            p.op("act", lambda e, tt=tt, avbank=avbank: e.activation(
                av_sb[:, (tt % 16) * 128:(tt % 16 + 1) * 128],
                pb[avbank][:, 0:128], AF.Copy, scale=zr16[:, 0:1]),
                reads=[b_pb[avbank], b_zr16], writes=[b_av_sb])
            p.dma("sp", out=avtm_s[tt * 128:(tt + 1) * 128, h * 128:(h + 1) * 128],
                  in_=av_sb[:, (tt % 16) * 128:(tt % 16 + 1) * 128],
                  reads=[b_av_sb], writes=[b_avtm], key="st_av")

    es2.close()
    p.barrier()

    es3 = ExitStack()
    sb3 = lambda n, s, d=F32: es3.enter_context(nc.sbuf_tensor(n, s, d))
    avf = [sb3(f"avf{dc}", [128, T], BF16) for dc in range(FC)]
    b_avf = [B(f"avf{dc}") for dc in range(FC)]
    for dc in range(FC):
        p.dma_transpose("sp", out=avf[dc][:],
                        in_=avtm_s[:, dc * 128:(dc + 1) * 128],
                        reads=[b_avtm], writes=[b_avf[dc]], key=f"avf{dc % 2}")
    wob = [sb3(f"wob{i}", [128, FC * 128], BF16) for i in range(2)]
    b_wob = [B(f"wob{i}") for i in range(2)]
    xfb = [sb3(f"xfb{i}", [128, 512], F32) for i in range(2)]
    b_xfb = [B(f"xfb{i}") for i in range(2)]
    r1c = sb3("r1c", [128, 512], F32); b_r1c = B("r1c")
    r1bc = sb3("r1bc", [128, 512], BF16); b_r1bc = B("r1bc")

    for m in range(H):
        s = m % 2
        p.dma("pool", out=wob[s][:].rearrange("p (fc q) -> p fc q", q=128),
              in_=wom[m].rearrange("fc p q -> p fc q"),
              writes=[b_wob[s]], key=f"wob{s}")
        for tg in range(2):
            tg0 = tg * 2048
            bwr = [b_pb[j] for j in range(4)]
            def womm(e, s=s, tg0=tg0):
                last = None
                for dc in range(FC):
                    lw = wob[s][:, dc * 128:(dc + 1) * 128]
                    for tq in range(4):
                        last = e.matmul(pb[tq][:, :], lw,
                                        avf[dc][:, tg0 + tq * 512:tg0 + (tq + 1) * 512],
                                        start=(dc == 0), stop=(dc == FC - 1))
                return last
            p.op("pe", womm, reads=[b_wob[s]] + b_avf, writes=bwr)
            for tq in range(4):
                t0 = tg0 + tq * 512
                xs = tq % 2
                p.dma("act", out=xfb[xs][:], in_=xT[m * 128:(m + 1) * 128, t0:t0 + 512],
                      writes=[b_xfb[xs]], key=f"xfb{xs}")
                p.op("dve", lambda e, m=m, tq=tq, xs=xs: e.scalar_tensor_tensor(
                    r1c[:], pb[tq][:, :], bo_c[:, m:m + 1], xfb[xs][:],
                    ALU.add, ALU.add),
                    reads=[b_pb[tq], b_bo_c, b_xfb[xs]], writes=[b_r1c])
                p.dma("sp", out=r1_s[m * 128:(m + 1) * 128, t0:t0 + 512],
                      in_=r1c[:], reads=[b_r1c], writes=[b_r1], key="st_r1")
                p.op("act", lambda e: e.activation(r1bc[:], r1c[:], AF.Copy),
                     reads=[b_r1c], writes=[b_r1bc])
                p.dma("pool", out=r1b_s[m * 128:(m + 1) * 128, t0:t0 + 512],
                      in_=r1bc[:], reads=[b_r1bc], writes=[b_r1b], key="st_r1b")
    es3.close()
    p.barrier()

    def ln_stats(src_b16, b_src, rowbuf2, b_rowbuf2, scope_sb, tag):
        lb = [scope_sb(f"lnb{tag}{i}", [128, 512], BF16) for i in range(2)]
        b_lb = [B(f"lnb{tag}{i}") for i in range(2)]
        sqb = scope_sb(f"sqb{tag}", [128, 512], BF16)
        b_sqb = B(f"sqb{tag}")
        for tc in range(8):
            t0 = tc * 512
            for m in range(H):
                s = m % 2
                p.dma("act", out=lb[s][:], in_=src_b16[m * 128:(m + 1) * 128, t0:t0 + 512],
                      reads=[b_src], writes=[b_lb[s]], key=f"lnb{tag}{s}")
                p.op("act", lambda e, s=s: e.activation(sqb[:], lb[s][:], AF.Square),
                     reads=[b_lb[s]], writes=[b_sqb])
                p.op("pe", lambda e, s=s, m=m: e.matmul(
                    pb[2][0:1, :], ones_col[:, 0:1], lb[s][:],
                    start=(m == 0), stop=(m == H - 1)),
                    reads=[b_ones_col, b_lb[s]], writes=[b_pb[2]])
                p.op("pe", lambda e, m=m: e.matmul(
                    pb[3][0:1, :], ones_col[:, 0:1], sqb[:],
                    start=(m == 0), stop=(m == H - 1)),
                    reads=[b_ones_col, b_sqb], writes=[b_pb[3]])
            p.op("act", lambda e, t0=t0: e.activation(
                rowbuf2[0:1, t0:t0 + 512], pb[2][0:1, :], AF.Copy, scale=1.0 / DM),
                reads=[b_pb[2]], writes=[b_rowbuf2])
            p.op("act", lambda e, t0=t0: e.activation(
                rowbuf2[0:1, T + t0:T + t0 + 512], pb[3][0:1, :], AF.Copy, scale=1.0 / DM),
                reads=[b_pb[3]], writes=[b_rowbuf2])
        MU = rowbuf2[0:1, 0:T]
        MSQ = rowbuf2[0:1, T:2 * T]
        MU2 = rowbuf2[0:1, 2 * T:3 * T]
        p.op("act", lambda e: e.activation(MU2, MU, AF.Square),
             reads=[b_rowbuf2], writes=[b_rowbuf2])
        p.op("dve", lambda e: e.tensor_sub(MSQ, MSQ, MU2),
             reads=[b_rowbuf2], writes=[b_rowbuf2])
        p.op("act", lambda e: e.activation(MSQ, MSQ, AF.Ln, bias=EPS_LN),
             reads=[b_rowbuf2, b_constcols], writes=[b_rowbuf2])
        p.op("act", lambda e: e.activation(MSQ, MSQ, AF.Exp, scale=-0.5),
             reads=[b_rowbuf2], writes=[b_rowbuf2])
        p.op("dve", lambda e: e.tensor_mul(MU2, MU, MSQ),
             reads=[b_rowbuf2], writes=[b_rowbuf2])
        return MSQ, MU2

    def ln_apply(src_f32, b_src, rstd_row, b2_row, b_rows, dst, b_dst,
                 dst_b16, b_dst_b16, scope_sb, tag, b16_tc_bufs=None):
        rb16 = scope_sb(f"lnr{tag}", [1, 2 * T], BF16)
        b_rb16 = B(f"lnr{tag}")
        p.op("act", lambda e: e.activation(rb16[0:1, 0:T], rstd_row, AF.Copy),
             reads=[b_rows], writes=[b_rb16])
        p.op("act", lambda e: e.activation(rb16[0:1, T:2 * T], b2_row, AF.Copy),
             reads=[b_rows], writes=[b_rb16])
        lc = [scope_sb(f"lnc{tag}{i}", [128, 512], F32) for i in range(2)]
        b_lc = [B(f"lnc{tag}{i}") for i in range(2)]
        ot = scope_sb(f"lno{tag}", [128, 512], F32)
        b_ot = B(f"lno{tag}")
        ob = scope_sb(f"lnob{tag}", [128, 512], BF16)
        b_ob = B(f"lnob{tag}")
        for tc in range(8):
            t0 = tc * 512
            p.op("pe", lambda e, t0=t0: e.matmul(
                pb[2][:, :], ones_row[0:1, :], rb16[0:1, t0:t0 + 512],
                start=True, stop=True),
                reads=[b_ones_row, b_rb16], writes=[b_pb[2]])
            p.op("pe", lambda e, t0=t0: e.matmul(
                pb[3][:, :], ones_row[0:1, :], rb16[0:1, T + t0:T + t0 + 512],
                start=True, stop=True),
                reads=[b_ones_row, b_rb16], writes=[b_pb[3]])
            for m in range(H):
                s = m % 2
                p.dma("act", out=lc[s][:], in_=src_f32[m * 128:(m + 1) * 128, t0:t0 + 512],
                      reads=[b_src], writes=[b_lc[s]], key=f"lnc{tag}{s}")
                p.op("dve", lambda e, s=s: e.tensor_mul(ot[:], lc[s][:], pb[2][:, :]),
                     reads=[b_lc[s], b_pb[2]], writes=[b_ot])
                p.op("dve", lambda e: e.tensor_sub(ot[:], ot[:], pb[3][:, :]),
                     reads=[b_ot, b_pb[3]], writes=[b_ot])
                p.dma("sp", out=dst[m * 128:(m + 1) * 128, t0:t0 + 512],
                      in_=ot[:], reads=[b_ot], writes=[b_dst], key=f"lnst{tag}")
                if dst_b16 is not None:
                    p.op("act", lambda e: e.activation(ob[:], ot[:], AF.Copy),
                         reads=[b_ot], writes=[b_ob])
                    wtgt = b16_tc_bufs[tc] if b16_tc_bufs is not None else b_dst_b16
                    p.dma("pool", out=dst_b16[m * 128:(m + 1) * 128, t0:t0 + 512],
                          in_=ob[:], reads=[b_ob], writes=[wtgt],
                          key=f"lnstb{tag}")

    es3b = ExitStack()
    sb3b = lambda n, s, d=F32: es3b.enter_context(nc.sbuf_tensor(n, s, d))
    rowbuf2 = sb3b("rowbuf2", [1, 3 * T], F32)
    b_rowbuf2 = B("rowbuf2")
    rstd1, b21 = ln_stats(r1b_s, b_r1b, rowbuf2, b_rowbuf2, sb3b, "1")
    ln_apply(r1_s, b_r1, rstd1, b21, b_rowbuf2, x1_s, b_x1, x1b_s, None,
             sb3b, "1", b16_tc_bufs=b_x1btc)
    es3b.close()
    p.barrier()

    es4 = ExitStack()
    sb4 = lambda n, s, d=F32: es4.enter_context(nc.sbuf_tensor(n, s, d))
    rw_sb = sb4("rw_sb", [128, FC * E], BF16)
    b_rw = B("rw_sb")
    p.dma("pool", out=rw_sb[:].rearrange("p (fc e) -> p fc e", e=E),
          in_=rwT.rearrange("(fc p) e -> p fc e", p=128),
          writes=[b_rw])
    x4b = [sb4(f"x4b{i}", [128, 512], BF16) for i in range(2)]
    b_x4b = [B(f"x4b{i}") for i in range(2)]
    s_f = sb4("s_f", [16, T], F32); b_s_f = B("s_f")
    id16 = sb4("id16", [16, 16], F32); b_id16 = B("id16")
    id16i = sb4("id16i", [16, 16], F32); b_id16i = B("id16i")
    p.op("pool", lambda e: e.iota(id16i[:], pattern=[[1, 16]], base=0,
                                  channel_multiplier=-1,
                                  allow_small_or_imprecise_dtypes=True),
         writes=[b_id16i])
    p.op("dve", lambda e: e.tensor_single_scalar(id16[:], id16i[:], 0.0, ALU.is_equal),
         reads=[b_id16i], writes=[b_id16])
    wT_sb = sb4("wT_sb", [16, T], BF16); b_wT = B("wT_sb")
    s_tm = sb4("s_tm", [128, E], F32); b_s_tm = B("s_tm")
    h16r = sb4("h16r", [128, E], F32); b_h16r = B("h16r")
    t8r = sb4("t8r", [128, 8], F32); b_t8r = B("t8r")
    t8c = sb4("t8c", [128, 8], F32); b_t8c = B("t8c")
    mask16r = sb4("mask16r", [128, E], F32); b_mask16r = B("mask16r")
    wur = sb4("wur", [128, E], F32); b_wur = B("wur")
    wsum = sb4("wsum", [128, 2], F32); b_wsum = B("wsum")
    w_tmr = sb4("w_tmr", [128, E], F32); b_w_tmr = B("w_tmr")
    for tc in range(8):
        t0 = tc * 512
        xs = tc % 2
        for fc in range(FC):
            p.dma("act", out=x4b[xs][:], in_=x1b_s[fc * 128:(fc + 1) * 128, t0:t0 + 512],
                  reads=[b_x1btc[tc]], writes=[b_x4b[xs]], key=f"x4b{xs}")
            p.op("pe", lambda e, fc=fc, xs=xs: e.matmul(
                pb[2][0:16, :], rw_sb[:, fc * E:(fc + 1) * E], x4b[xs][:],
                start=(fc == 0), stop=(fc == FC - 1)),
                reads=[b_rw, b_x4b[xs]], writes=[b_pb[2]])
            xs = (xs + 1) % 2
        p.op("act", lambda e, t0=t0: e.activation(
            s_f[0:16, t0:t0 + 512], pb[2][0:16, :], AF.Sigmoid),
            reads=[b_pb[2]], writes=[b_s_f])
    for tt in range(NT):
        t0 = tt * 128
        p.op("pe", lambda e, t0=t0: e.transpose(
            pb[3][:, 0:16], s_f[0:16, t0:t0 + 128], id16[:, :]),
            reads=[b_s_f, b_id16], writes=[b_pb[3]])
        p.op("dve", lambda e: e.tensor_copy(s_tm[:], pb[3][:, 0:16]),
             reads=[b_pb[3]], writes=[b_s_tm])
        p.op("dve", lambda e: e.tensor_add(h16r[:], s_tm[:], eb_b[:]),
             reads=[b_s_tm, b_eb_b], writes=[b_h16r])
        p.op("dve", lambda e: e.max(t8r[:], h16r[:]),
             reads=[b_h16r], writes=[b_t8r])
        p.op("act", lambda e: e.copy(t8c[:], t8r[:]),
             reads=[b_t8r], writes=[b_t8c])
        p.op("dve", lambda e: e.tensor_single_scalar(
            mask16r[:], h16r[:], t8c[:, 3:4], ALU.is_ge),
            reads=[b_h16r, b_t8c], writes=[b_mask16r])
        p.op("dve", lambda e: e.tensor_mul(wur[:], mask16r[:], s_tm[:]),
             reads=[b_mask16r, b_s_tm], writes=[b_wur])
        p.op("dve", lambda e: e.reduce_sum(wsum[:, 0:1], wur[:], axis=AX.X),
             reads=[b_wur], writes=[b_wsum])
        p.op("dve", lambda e: e.tensor_single_scalar(
            wsum[:, 1:2], wsum[:, 0:1], 1e-20, ALU.add),
            reads=[b_wsum], writes=[b_wsum])
        p.op("dve", lambda e: e.reciprocal(wsum[:, 0:1], wsum[:, 1:2]),
             reads=[b_wsum], writes=[b_wsum])
        p.op("dve", lambda e: e.tensor_scalar_mul(w_tmr[:], wur[:], wsum[:, 0:1]),
             reads=[b_wur, b_wsum], writes=[b_w_tmr])
        p.op("pe", lambda e: e.transpose(pb[3][0:16, 0:128], w_tmr[:, 0:16], identf[:]),
             reads=[b_w_tmr, b_identf], writes=[b_pb[3]])
        p.op("act", lambda e, t0=t0: e.activation(
            wT_sb[0:16, t0:t0 + 128], pb[3][0:16, 0:128], AF.Copy),
            reads=[b_pb[3]], writes=[b_wT])

    gupb = [sb4("gupb0", [128, FC * 2 * I], BF16)]
    b_gupb = [B("gupb0")]
    wrow = [sb4(f"wrow{i}", [1, 2048], BF16) for i in range(2)]
    b_wrow = [B(f"wrow{i}") for i in range(2)]
    x1w = [sb4(f"x1w{c}", [128, 2048], BF16) for c in range(FC)]
    b_x1w = [B(f"x1w{c}") for c in range(FC)]
    sg_sb = sb4("sg_sb", [128, IC * 2048], BF16); b_sg = B("sg_sb")
    wbc_sb = sb4("wbc_sb", [128, 2048], BF16); b_wbc = B("wbc_sb")
    mid_sb = sb4("mid_sb", [128, 2048], BF16); b_mid_sb = B("mid_sb")

    for tcg in range(2):
        tg0 = tcg * 2048
        for c in range(FC):
            p.dma("act", out=x1w[c][:], in_=x1b_s[c * 128:(c + 1) * 128, tg0:tg0 + 2048],
                  reads=[b_x1btc[tcg * 4], b_x1btc[tcg * 4 + 1], b_x1btc[tcg * 4 + 2],
                         b_x1btc[tcg * 4 + 3]],
                  writes=[b_x1w[c]], key=f"x1w{c % 2}")
        for e_i in range(17):
            s = 0
            if e_i == 16:
                p.dma("pool", out=gupb[s][:].rearrange("p (fc g) -> p fc g", g=2 * I),
                      in_=sgu.rearrange("fc p g -> p fc g"),
                      writes=[b_gupb[s]], key=f"gupb{s}")
            if e_i < 16:
                p.dma("pool", out=gupb[s][:].rearrange("p (fc g) -> p fc g", g=2 * I),
                      in_=gup[e_i].rearrange("fc p g -> p fc g"),
                      writes=[b_gupb[s]], key=f"gupb{s}")
                p.dma("pool", out=wrow[e_i % 2][:], in_=wT_sb[e_i:e_i + 1, tg0:tg0 + 2048],
                      reads=[b_wT], writes=[b_wrow[e_i % 2]], key=f"wrow{e_i % 2}")
                for tq in range(4):
                    p.op("pe", lambda e, e_i=e_i, tq=tq: e.matmul(
                        pb[4 + tq][:, :], ones_row[0:1, :],
                        wrow[e_i % 2][0:1, tq * 512:(tq + 1) * 512],
                        start=True, stop=True),
                        reads=[b_ones_row, b_wrow[e_i % 2]], writes=[b_pb[4 + tq]])
                    p.op("act", lambda e, tq=tq: e.activation(
                        wbc_sb[:, tq * 512:(tq + 1) * 512], pb[4 + tq][:, :], AF.Copy),
                        reads=[b_pb[4 + tq]], writes=[b_wbc])
            for ic in range(IC):
                bs = (ic % 2) * 4
                bwr = [b_pb[bs + j] for j in range(4)]
                def gmm(e, s=s, ic=ic, bs=bs):
                    last = None
                    for fc in range(FC):
                        lw = gupb[s][:, fc * 2 * I + ic * 128:fc * 2 * I + (ic + 1) * 128]
                        for tq in range(4):
                            last = e.matmul(pb[bs + tq][:, :], lw,
                                            x1w[fc][:, tq * 512:(tq + 1) * 512],
                                            start=(fc == 0), stop=(fc == FC - 1))
                    return last
                p.op("pe", gmm, reads=[b_gupb[s]] + b_x1w, writes=bwr)
                for tq in range(4):
                    p.op("act", lambda e, ic=ic, bs=bs, tq=tq: e.activation(
                        sg_sb[:, ic * 2048 + tq * 512:ic * 2048 + (tq + 1) * 512],
                        pb[bs + tq][:, :], AF.Silu),
                        reads=[b_pb[bs + tq]], writes=[b_sg])
            if e_i < 16:
                for ic in range(IC):
                    p.op("dve", lambda e, ic=ic: e.tensor_mul(
                        sg_sb[:, ic * 2048:(ic + 1) * 2048],
                        sg_sb[:, ic * 2048:(ic + 1) * 2048], wbc_sb[:]),
                        reads=[b_sg, b_wbc], writes=[b_sg])
            for ic in range(IC):
                bs = (ic % 2) * 4
                bwr = [b_pb[bs + j] for j in range(4)]
                def umm(e, s=s, ic=ic, bs=bs):
                    last = None
                    for fc in range(FC):
                        lw = gupb[s][:, fc * 2 * I + I + ic * 128:fc * 2 * I + I + (ic + 1) * 128]
                        for tq in range(4):
                            last = e.matmul(pb[bs + tq][:, :], lw,
                                            x1w[fc][:, tq * 512:(tq + 1) * 512],
                                            start=(fc == 0), stop=(fc == FC - 1))
                    return last
                p.op("pe", umm, reads=[b_gupb[s]] + b_x1w, writes=bwr)
                for tq in range(4):
                    p.op("dve", lambda e, ic=ic, bs=bs, tq=tq: e.tensor_mul(
                        mid_sb[:, tq * 512:(tq + 1) * 512],
                        sg_sb[:, ic * 2048 + tq * 512:ic * 2048 + (tq + 1) * 512],
                        pb[bs + tq][:, :]),
                        reads=[b_sg, b_pb[bs + tq]], writes=[b_mid_sb])
                p.dma("sp", out=mid_s[e_i, ic * 128:(ic + 1) * 128, tg0:tg0 + 2048],
                      in_=mid_sb[:], reads=[b_mid_sb], writes=[b_mid], key="st_mid")
    es4.close()
    p.barrier()

    es4b = ExitStack()
    sb4b = lambda n, s, d=F32: es4b.enter_context(nc.sbuf_tensor(n, s, d))
    dcast = sb4b("dcast", [128, 102 * 128], BF16)
    b_dcast = B("dcast")
    for m in range(H):
        p.dma("pool", out=dcast[:], in_=dallm[m].rearrange("p kc q -> p (kc q)"),
              writes=[b_dcast], key="dcast_ld")
        p.dma("sp", out=dbf_s[m], in_=dcast[:],
              reads=[b_dcast], writes=[b_dbf], key="dcast_st")
    es4b.close()
    p.barrier()

    es4c = ExitStack()
    sb4c = lambda n, s, d=F32: es4c.enter_context(nc.sbuf_tensor(n, s, d))
    midwA = sb4c("midwA", [128, 48 * 512], BF16); b_midwA = B("midwA")
    midwB = sb4c("midwB", [128, 54 * 512], BF16); b_midwB = B("midwB")
    dwb = [sb4c(f"dwb{i}", [128, 102 * 128], BF16) for i in range(2)]
    b_dwb = [B(f"dwb{i}") for i in range(2)]
    x1fb = [sb4c(f"x1fb{i}", [128, 512], F32) for i in range(2)]
    b_x1fb = [B(f"x1fb{i}") for i in range(2)]
    r2c = [sb4c(f"r2c{i}", [128, 512], F32) for i in range(2)]
    b_r2c = [B(f"r2c{i}") for i in range(2)]
    r2bc = [sb4c(f"r2bc{i}", [128, 512], BF16) for i in range(2)]
    b_r2bc = [B(f"r2bc{i}") for i in range(2)]

    for tc in range(8):
        t0 = tc * 512
        for e_i in range(17):
            half = 0 if e_i < 8 else 1
            dst = midwA if half == 0 else midwB
            bdst = b_midwA if half == 0 else b_midwB
            base = e_i * IC if half == 0 else (e_i - 8) * IC
            p.dma("sp", out=dst[:, base * 512:(base + IC) * 512]
                  .rearrange("p (ic t) -> p ic t", t=512),
                  in_=mid_s[e_i, :, t0:t0 + 512].rearrange("(ic p) t -> p ic t", p=128),
                  reads=[b_mid], writes=[bdst], key="midw" + ("A" if half == 0 else "B"))
        for m in range(H):
            s = m % 2
            p.dma("sp", out=dwb[s][:], in_=dbf_s[m],
                  reads=[b_dbf], writes=[b_dwb[s]], key=f"dwb{s}")
            bank = m % 4
            def dmmA(e, s=s, bank=bank):
                last = None
                for eic in range(48):
                    last = e.matmul(pb[bank][:, :],
                                    dwb[s][:, eic * 128:(eic + 1) * 128],
                                    midwA[:, eic * 512:(eic + 1) * 512],
                                    start=(eic == 0), stop=False)
                return last
            def dmmB(e, s=s, bank=bank):
                last = None
                for eic in range(54):
                    last = e.matmul(pb[bank][:, :],
                                    dwb[s][:, (48 + eic) * 128:(48 + eic + 1) * 128],
                                    midwB[:, eic * 512:(eic + 1) * 512],
                                    start=False, stop=(eic == 53))
                return last
            p.op("pe", dmmA, reads=[b_dwb[s], b_midwA], writes=[b_pb[bank]])
            p.op("pe", dmmB, reads=[b_dwb[s], b_midwB], writes=[b_pb[bank]])
            xs = m % 2
            p.dma("pool", out=x1fb[xs][:], in_=x1_s[m * 128:(m + 1) * 128, t0:t0 + 512],
                  reads=[b_x1], writes=[b_x1fb[xs]], key=f"x1fb{xs}")
            p.op("dve", lambda e, bank=bank, xs=xs: e.tensor_add(
                r2c[xs][:], pb[bank][:, :], x1fb[xs][:]),
                reads=[b_pb[bank], b_x1fb[xs]], writes=[b_r2c[xs]])
            p.dma("pool", out=r2_s[m * 128:(m + 1) * 128, t0:t0 + 512],
                  in_=r2c[xs][:], reads=[b_r2c[xs]], writes=[b_r2], key=f"st_r2{xs}")
            p.op("act", lambda e, xs=xs: e.activation(r2bc[xs][:], r2c[xs][:], AF.Copy),
                 reads=[b_r2c[xs]], writes=[b_r2bc[xs]])
            p.dma("pool", out=r2b_s[m * 128:(m + 1) * 128, t0:t0 + 512],
                  in_=r2bc[xs][:], reads=[b_r2bc[xs]], writes=[b_r2b], key=f"st_r2b{xs}")
    es4c.close()
    p.barrier()

    es5 = ExitStack()
    sb5 = lambda n, s, d=F32: es5.enter_context(nc.sbuf_tensor(n, s, d))
    rowbuf3 = sb5("rowbuf3", [1, 3 * T], F32)
    b_rowbuf3 = B("rowbuf3")
    rstd2, b22 = ln_stats(r2b_s, b_r2b, rowbuf3, b_rowbuf3, sb5, "2")
    ln_apply(r2_s, b_r2, rstd2, b22, b_rowbuf3, out, b_out, None, None,
             sb5, "2")
    es5.close()
    p.barrier()
    out_names = ["out"]
    if "scratch" in dev_taps:
        out_names += ["qpb_s", "kpb_s", "vtm_s", "avtm_s", "x1_s", "mid_s"]
    if "rows" in dev_taps:
        rows_o = DT("rows_o", [1, 2 * T], F32, kind="ExternalOutput").ap()
        b_rows_o = B("rows_o")
        p.dma("sp", out=rows_o[:, :], in_=rowbuf[:], reads=[b_rowbuf],
              writes=[b_rows_o], key="st_rows")
        out_names.append("rows_o")
    p.finish("pool")
    p.emit()
    return nc, es, out_names


def _host_layout(inputs):
    x = inputs["L0_attn_x"]
    wq = np.asarray(inputs["L0_attn_wq"])
    wk = np.asarray(inputs["L0_attn_wk"])
    wv = np.asarray(inputs["L0_attn_wv"])
    wo = np.asarray(inputs["L0_attn_wo"])
    gup = np.asarray(inputs["L0_ffn_gate_up_proj"])
    down = np.asarray(inputs["L0_ffn_down_proj"])
    sgw = np.asarray(inputs["L0_ffn_shared_gate_w"])
    suw = np.asarray(inputs["L0_ffn_shared_up_w"])
    sdw = np.asarray(inputs["L0_ffn_shared_down_w"])

    def mblock(w):
        wT = np.ascontiguousarray(w.T)
        return np.ascontiguousarray(
            wT.reshape(FC, 128, H, 128).transpose(2, 0, 1, 3))

    wqm = mblock(wq); wkm = mblock(wk); wom = mblock(wo)
    wvT = np.ascontiguousarray(wv.T)
    wvdc = np.ascontiguousarray(
        wvT.reshape(FC, 128, 4, 512).transpose(2, 0, 1, 3))

    def cols(v):
        return np.ascontiguousarray(v.reshape(FC, 128).T)

    gupT = np.ascontiguousarray(
        gup.transpose(0, 2, 1).reshape(E, FC, 128, 2 * I))
    sguT = np.concatenate([sgw.T, suw.T], axis=1)
    sguT = np.ascontiguousarray(sguT.reshape(FC, 128, 2 * I))

    dT = np.concatenate([down[e].T for e in range(E)] + [sdw.T], axis=0)
    dallm = np.ascontiguousarray(
        dT.reshape(102, 128, H, 128).transpose(2, 1, 0, 3))

    base = {
        "wqm": wqm, "wkm": wkm, "wvdc": wvdc, "wom": wom,
        "bqr": np.asarray(inputs["L0_attn_bq"])[None, :],
        "bkr": np.asarray(inputs["L0_attn_bk"])[None, :],
        "boc": cols(np.asarray(inputs["L0_attn_bo"])),
        "gqc": cols(np.asarray(inputs["L0_attn_gq"])),
        "gkc": cols(np.asarray(inputs["L0_attn_gk"])),
        "bvr": np.asarray(inputs["L0_attn_bv"])[None, :],
        "wmix": np.asarray(inputs["L0_attn_w_mix"]),
        "rwT": np.ascontiguousarray(np.asarray(inputs["L0_ffn_router_w"]).T),
        "ebr": np.asarray(inputs["L0_ffn_e_bias"])[None, :],
        "gup": gupT, "sgu": sguT, "dallm": dallm,
    }
    in_maps = []
    for b in range(8):
        m = dict(base)
        m["xT"] = np.ascontiguousarray(np.asarray(x[b]).T)
        in_maps.append(m)
    return in_maps


def kernel(**inputs):
    in_maps = _host_layout(inputs)
    nc, es, out_names = build_program()
    try:
        res = run_bass_kernel_spmd(nc, in_maps, core_ids=list(range(8)))
    finally:
        es.close()
    outs = [np.ascontiguousarray(np.asarray(r["out"], np.float32).T)
            for r in res.results]
    return np.stack(outs, axis=0)


# version 13
# speedup vs baseline: 1.2845x; 1.1075x over previous
import math
from collections import defaultdict
from contextlib import ExitStack

import numpy as np
import concourse.bass as bass
import concourse.mybir as mybir
from concourse.bass_utils import run_bass_kernel_spmd

F32 = mybir.dt.float32
BF16 = mybir.dt.bfloat16
AF = mybir.ActivationFunctionType
ALU = mybir.AluOpType
AX = mybir.AxisListType

T = 4096
DM = 2048
H = 16
D = 128
FC = DM // 128
E = 16
I = 768
IC = I // 128
TOPK = 4
SCALE = D ** -0.5
EPS_RMS = 1e-6
EPS_LN = 1e-5
SEQ_SHIFTS = [0, 1, -1, 3, -3, 7, -7, 20, -20, 53, -53, 141, -141, 380, -380, 1024]
CH_SHIFTS = [0, 1, 2, 4, 8]
SEGS = [-3, -2, -1, 0, 1, 2, 3, 8]
NSEG = len(SEGS)
SEG_DIAGS = []
for _d in SEGS:
    SEG_DIAGS.append(sorted(s - _d * 128 for s in SEQ_SHIFTS
                            if -128 < s - _d * 128 < 128))
NT = T // 128


class Buf:
    __slots__ = ("name", "writer", "readers")

    def __init__(self, name):
        self.name = name
        self.writer = None
        self.readers = []


class Prog:
    ENGINES = ("pe", "act", "dve", "pool", "sp")

    def __init__(self, nc, es):
        self.nc = nc
        self.es = es
        self.sem = {}
        self.count = defaultdict(int)
        self.hwm = defaultdict(int)
        self.queue = {e: [] for e in self.ENGINES}
        self.nops = 0
        for e in self.ENGINES:
            self.sem[e] = es.enter_context(nc.semaphore(f"sem_{e}"))

    def buf(self, name):
        return Buf(name)

    def _dma_sem(self, key):
        skey = "dma_" + key
        if skey not in self.sem:
            self.sem[skey] = self.es.enter_context(
                self.nc.semaphore(f"sem_dma_{key}"))
        return skey

    def _wait(self, waiter_eng, src_stream, val):
        if val <= 0:
            return
        if src_stream.startswith("dma_"):
            val = self.count[src_stream]
        k = (waiter_eng, src_stream)
        if self.hwm[k] >= val:
            return
        self.hwm[k] = val
        sem = self.sem[src_stream]
        tgt = val * (16 if src_stream.startswith("dma_") else 1)
        self.queue[waiter_eng].append(lambda e: e.wait_ge(sem, tgt))

    def _pre(self, eng, reads, writes):
        need = {}
        bar = getattr(self, "barrier_counts", None)
        if bar:
            for s, c in bar.items():
                if c > need.get(s, 0):
                    need[s] = c
        for b in reads:
            if b.writer is not None:
                s, c = b.writer
                if c > need.get(s, 0):
                    need[s] = c
        for b in writes:
            if b.writer is not None:
                s, c = b.writer
                if c > need.get(s, 0):
                    need[s] = c
            for (s, c) in b.readers:
                if c > need.get(s, 0):
                    need[s] = c
        for s, c in need.items():
            self._wait(eng, s, c)

    def barrier(self):
        self.barrier_counts = dict(self.count)

    def _post(self, stream, reads, writes):
        self.count[stream] += 1
        c = self.count[stream]
        for b in writes:
            b.writer = (stream, c)
            b.readers = []
        for b in reads:
            b.readers.append((stream, c))
        return c

    def op(self, eng, fn, reads=(), writes=()):
        self._pre(eng, reads, writes)
        sem = self.sem[eng]
        self.queue[eng].append(lambda e: fn(e).then_inc(sem, 1))
        self.nops += 1
        return self._post(eng, reads, writes)

    def dma(self, queue, out=None, in_=None, reads=(), writes=(), key=None, **kw):
        if key is None:
            key = writes[0].name if writes else (reads[0].name if reads else queue)
        stream = self._dma_sem(key)
        self._pre(queue, reads, writes)
        sem = self.sem[stream]
        o, i = out, in_
        self.queue[queue].append(
            lambda e: e.dma_start(out=o, in_=i, **kw).then_inc(sem, 16))
        self.nops += 1
        return self._post(stream, reads, writes)

    def dma_transpose(self, queue, out, in_, reads=(), writes=(), key=None):
        if key is None:
            key = writes[0].name
        stream = self._dma_sem(key)
        self._pre(queue, reads, writes)
        sem = self.sem[stream]
        self.queue[queue].append(
            lambda e: e.dma_start(out=out, in_=in_, transpose=True).then_inc(sem, 16))
        self.nops += 1
        return self._post(stream, reads, writes)

    def finish(self, eng="pool"):
        for s, c in list(self.count.items()):
            if s != eng and c > 0:
                self._wait(eng, s, c)

    def emit(self):
        with self.nc.Block() as block:
            for eng_name, starter in (("pe", block.tensor), ("act", block.scalar),
                                      ("dve", block.vector), ("pool", block.gpsimd),
                                      ("sp", block.sync)):
                q = self.queue[eng_name]
                if q:
                    def body(e, q=q):
                        for fn in q:
                            fn(e)
                    starter(body)


def circ_runs(a, ln, period=T):
    a %= period
    out = []
    while ln > 0:
        take = min(ln, period - a)
        out.append((a, take))
        ln -= take
        a = 0
    return out


def build_program(dev_taps=()):
    es = ExitStack()
    nc = bass.Bass("TRN2", target_bir_lowering=False, debug=False)
    p = Prog(nc, es)

    kind_scr = "ExternalOutput" if "scratch" in dev_taps else "Internal"

    DT = nc.dram_tensor
    xT = DT("xT", [DM, T], F32, kind="ExternalInput").ap()
    wqm = DT("wqm", [H, FC, 128, 128], F32, kind="ExternalInput").ap()
    wkm = DT("wkm", [H, FC, 128, 128], F32, kind="ExternalInput").ap()
    wvdc = DT("wvdc", [4, FC, 128, 512], F32, kind="ExternalInput").ap()
    wom = DT("wom", [H, FC, 128, 128], F32, kind="ExternalInput").ap()
    bqr = DT("bqr", [1, DM], F32, kind="ExternalInput").ap()
    bkr = DT("bkr", [1, DM], F32, kind="ExternalInput").ap()
    boc = DT("boc", [128, FC], F32, kind="ExternalInput").ap()
    gqc = DT("gqc", [128, FC], F32, kind="ExternalInput").ap()
    gkc = DT("gkc", [128, FC], F32, kind="ExternalInput").ap()
    bvr = DT("bvr", [1, DM], F32, kind="ExternalInput").ap()
    wmix = DT("wmix", [1, 5], F32, kind="ExternalInput").ap()
    rwT = DT("rwT", [DM, E], F32, kind="ExternalInput").ap()
    ebr = DT("ebr", [1, E], F32, kind="ExternalInput").ap()
    gup = DT("gup", [E, FC, 128, 2 * I], F32, kind="ExternalInput").ap()
    sgu = DT("sgu", [FC, 128, 2 * I], F32, kind="ExternalInput").ap()
    dallm = DT("dallm", [H, 128, 102, 128], F32, kind="ExternalInput").ap()

    out = DT("out", [DM, T], F32, kind="ExternalOutput").ap()

    qpb_s = DT("qpb_s", [DM, T], BF16, kind=kind_scr).ap()
    kpb_s = DT("kpb_s", [DM, T], BF16, kind=kind_scr).ap()
    vtm_s = DT("vtm_s", [H, T, 128], BF16, kind=kind_scr).ap()
    avtm_s = DT("avtm_s", [T, DM], BF16, kind=kind_scr).ap()
    r1_s = DT("r1_s", [DM, T], F32, kind="Internal").ap()
    r1b_s = DT("r1b_s", [DM, T], BF16, kind="Internal").ap()
    x1_s = DT("x1_s", [DM, T], F32, kind=kind_scr).ap()
    x1b_s = DT("x1b_s", [DM, T], BF16, kind="Internal").ap()
    mid_s = DT("mid_s", [17, I, T], BF16, kind=kind_scr).ap()
    dbf_s = DT("dbf_s", [H, 128, 102 * 128], BF16, kind="Internal").ap()
    r2_s = DT("r2_s", [DM, T], F32, kind="Internal").ap()
    r2b_s = DT("r2b_s", [DM, T], BF16, kind="Internal").ap()

    sb = lambda n, s, d=F32: es.enter_context(nc.sbuf_tensor(n, s, d))
    pb = [es.enter_context(nc.psum_tensor(f"pb{i}", [128, 512], F32))
          for i in range(8)]
    b_pb = [p.buf(f"pb{i}") for i in range(8)]

    B = p.buf
    ones_row = sb("ones_row", [1, 128], BF16)
    ones_r512 = sb("ones_r512", [1, 512], BF16)
    ones_col = sb("ones_col", [128, 1], BF16)
    ident = sb("ident", [128, 128], BF16)
    identf = sb("identf", [128, 128], F32)
    stencil = sb("stencil", [128, 128], BF16)
    amask = sb("amask", [128, NSEG * 128], BF16)
    bq_r = sb("bq_r", [1, DM], BF16)
    bk_r = sb("bk_r", [1, DM], BF16)
    bo_c = sb("bo_c", [128, FC], F32)
    gq_c = sb("gq_c", [128, FC], F32)
    gk_c = sb("gk_c", [128, FC], F32)
    wmix_r = sb("wmix_r", [1, 8], F32)
    wmix_c = sb("wmix_c", [128, 8], F32)
    eb_r = sb("eb_r", [1, E], BF16)
    eb_b = sb("eb_b", [128, E], F32)
    b_ones_row, b_ones_col, b_ident = B("ones_row"), B("ones_col"), B("ident")
    b_identf = B("identf")
    b_ones_r512 = B("ones_r512")
    b_stencil = B("stencil")
    b_amask = B("amask")
    b_bq_r, b_bk_r, b_bo_c, b_gq_c, b_gk_c = B("bq_r"), B("bk_r"), B("bo_c"), B("gq_c"), B("gk_c")
    b_wmix_r, b_wmix_c, b_eb_r, b_eb_b = B("wmix_r"), B("wmix_c"), B("eb_r"), B("eb_b")

    b_qpb = B("qpb_s"); b_kpb = B("kpb_s"); b_vtm = B("vtm_s")
    b_avtm = B("avtm_s"); b_r1 = B("r1_s"); b_r1b = B("r1b_s")
    b_x1 = B("x1_s"); b_mid = B("mid_s")
    b_x1btc = [B(f"x1b_s{i}") for i in range(8)]
    b_dbf = B("dbf_s"); b_r2 = B("r2_s"); b_r2b = B("r2b_s"); b_out = B("out")

    constcols = sb("constcols", [128, 8], F32)
    b_constcols = B("constcols")
    _ccn = [0]
    def register_const(v):
        i = _ccn[0]
        _ccn[0] += 1
        p.op("dve", lambda e, i=i, v=v: e.memset(constcols[:, i:i + 1], float(v)),
             writes=[b_constcols])
        nc.const_aps.register(F32, float(v), constcols[:, i:i + 1])
    for _v in (EPS_RMS, EPS_LN, 1e-20):
        register_const(_v)

    rstd_rb = sb("rstd_rb", [1, 2 * T], BF16)
    b_rstd_rb = B("rstd_rb")
    ROW_Q, ROW_K = 0, 1
    es0 = ExitStack()
    sb0 = lambda n, s, d=F32: es0.enter_context(nc.sbuf_tensor(n, s, d))
    iotaD = sb0("iotaD", [128, 128], F32)
    stenf = sb0("stenf", [128, 128], F32)
    amaskf = sb0("amaskf", [128, NSEG * 128], F32)
    b_iotaD, b_stenf, b_amaskf = B("iotaD"), B("stenf"), B("amaskf")

    p.op("dve", lambda e: e.memset(ones_row[:], 1.0), writes=[b_ones_row])
    p.op("dve", lambda e: e.memset(ones_r512[:], 1.0), writes=[b_ones_r512])
    p.op("dve", lambda e: e.memset(ones_col[:], 1.0), writes=[b_ones_col])
    p.op("pool", lambda e: e.iota(iotaD[:], pattern=[[1, 128]], base=0,
                                  channel_multiplier=-1,
                                  allow_small_or_imprecise_dtypes=True),
         writes=[b_iotaD])
    p.op("dve", lambda e: e.tensor_single_scalar(identf[:], iotaD[:], 0.0, ALU.is_equal),
         reads=[b_iotaD], writes=[b_identf])
    p.op("dve", lambda e: e.tensor_copy(ident[:], identf[:]),
         reads=[b_identf], writes=[b_ident])
    for name, srcap, dst, bdst in (("boc", boc, bo_c, b_bo_c), ("gqc", gqc, gq_c, b_gq_c),
                                   ("gkc", gkc, gk_c, b_gk_c)):
        p.dma("sp", out=dst[:], in_=srcap[:, :], writes=[bdst])
    p.dma("pool", out=bq_r[:], in_=bqr[:, :], writes=[b_bq_r])
    p.dma("pool", out=bk_r[:], in_=bkr[:, :], writes=[b_bk_r])

    p.dma("sp", out=wmix_r[0:1, 0:5], in_=wmix[:, :], writes=[b_wmix_r])
    p.dma("pool", out=eb_r[:], in_=ebr[:, :], writes=[b_eb_r])

    wmix_rb = sb0("wmix_rb", [1, 8], BF16)
    b_wmix_rb = B("wmix_rb")
    p.op("dve", lambda e: e.memset(wmix_rb[:], 0.0), writes=[b_wmix_rb])
    p.op("dve", lambda e: e.tensor_copy(wmix_rb[0:1, 0:5], wmix_r[0:1, 0:5]),
         reads=[b_wmix_r], writes=[b_wmix_rb])
    p.op("pe", lambda e: e.matmul(pb[6][:, 0:8], ones_row[0:1, :], wmix_rb[0:1, :],
                                  start=True, stop=True),
         reads=[b_ones_row, b_wmix_rb], writes=[b_pb[6]])
    p.op("dve", lambda e: e.tensor_copy(wmix_c[:], pb[6][:, 0:8]),
         reads=[b_pb[6]], writes=[b_wmix_c])
    p.op("pe", lambda e: e.matmul(pb[6][:, 0:E], ones_row[0:1, :], eb_r[0:1, :],
                                  start=True, stop=True),
         reads=[b_ones_row, b_eb_r], writes=[b_pb[6]])
    p.op("dve", lambda e: e.tensor_copy(eb_b[:], pb[6][:, 0:E]),
         reads=[b_pb[6]], writes=[b_eb_b])

    p.op("dve", lambda e: e.memset(stenf[:], 0.0), writes=[b_stenf])
    mtmp = sb0("mtmp", [128, 128], F32)
    mtmp2 = sb0("mtmp2", [128, 128], F32)
    b_mtmp, b_mtmp2 = B("mtmp"), B("mtmp2")
    for ci, c in enumerate(CH_SHIFTS):
        p.op("dve", lambda e, c=c: e.tensor_single_scalar(mtmp[:], iotaD[:], float(c), ALU.is_equal),
             reads=[b_iotaD], writes=[b_mtmp])
        p.op("dve", lambda e, c=c: e.tensor_single_scalar(mtmp2[:], iotaD[:], float(c - 128), ALU.is_equal),
             reads=[b_iotaD], writes=[b_mtmp2])
        p.op("dve", lambda e: e.tensor_add(mtmp[:], mtmp[:], mtmp2[:]),
             reads=[b_mtmp, b_mtmp2], writes=[b_mtmp])
        p.op("dve", lambda e, ci=ci: e.scalar_tensor_tensor(
            stenf[:], mtmp[:], wmix_c[:, ci:ci + 1], stenf[:], ALU.mult, ALU.add),
            reads=[b_mtmp, b_wmix_c, b_stenf], writes=[b_stenf])
    p.op("act", lambda e: e.activation(stencil[:], stenf[:], AF.Copy, scale=SCALE),
         reads=[b_stenf], writes=[b_stencil])

    p.op("dve", lambda e: e.memset(amaskf[:], 0.0), writes=[b_amaskf])
    for si in range(NSEG):
        for dlt in SEG_DIAGS[si]:
            p.op("dve", lambda e, si=si, dlt=dlt: e.tensor_single_scalar(
                mtmp[:], iotaD[:], float(dlt), ALU.is_equal),
                reads=[b_iotaD], writes=[b_mtmp])
            p.op("dve", lambda e, si=si: e.tensor_max(
                amaskf[:, si * 128:(si + 1) * 128],
                amaskf[:, si * 128:(si + 1) * 128], mtmp[:]),
                reads=[b_amaskf, b_mtmp], writes=[b_amaskf])
    p.op("dve", lambda e: e.tensor_copy(amask[:], amaskf[:]),
         reads=[b_amaskf], writes=[b_amask])

    es0.close()
    p.barrier()

    es1 = ExitStack()
    sb1 = lambda n, s, d=F32: es1.enter_context(nc.sbuf_tensor(n, s, d))
    rowbuf = sb1("rowbuf", [1, T], F32)
    b_rowbuf = B("rowbuf")
    xh = [sb1(f"xh{c}", [128, T], BF16) for c in range(FC)]
    b_xh = [B(f"xh{c}") for c in range(FC)]
    def row(i):
        return rowbuf[0:1, 0:T]
    def rowslc(i, a, ln):
        return rowbuf[0:1, a:a + ln]
    for c in range(FC):
        p.dma("pool", out=xh[c][:], in_=xT[c * 128:(c + 1) * 128, :],
              writes=[b_xh[c]], key=f"xh{c % 2}")

    wqb = [sb1(f"wqb{i}", [128, FC * 128], BF16) for i in range(2)]
    b_wqb = [B(f"wqb{i}") for i in range(2)]
    wvb = [sb1("wvb0", [128, FC * 512], BF16)]
    b_wvb = [B("wvb0")]
    sqv = [sb1(f"sqv{i}", [128, 512], BF16) for i in range(2)]
    b_sqv = [B(f"sqv{i}") for i in range(2)]
    pj_sb = [sb1(f"pj_sb{i}", [128, 512], BF16) for i in range(4)]
    b_pj = [B(f"pj_sb{i}") for i in range(4)]
    bv_r = sb1("bv_r", [1, DM], BF16)
    b_bv_r = B("bv_r")
    p.dma("pool", out=bv_r[:], in_=bvr[:, :], writes=[b_bv_r])

    def qk_sweep(which):
        wsrc = wqm if which == "q" else wkm
        bias_r = bq_r if which == "q" else bk_r
        b_bias = b_bq_r if which == "q" else b_bk_r
        dst = qpb_s if which == "q" else kpb_s
        b_dst = b_qpb if which == "q" else b_kpb
        rowi = ROW_Q if which == "q" else ROW_K
        TCB = [0, 1, 2, 7]
        for g in range(2):
            for m in range(H):
                s = m % 2
                p.dma("pool", out=wqb[s][:].rearrange("p (fc q) -> p fc q", q=128),
                      in_=wsrc[m].rearrange("fc p q -> p fc q"),
                      writes=[b_wqb[s]], key=f"wqk{s}")
                tg0 = g * 2048
                for hb in range(2):
                    tcis = (hb * 2, hb * 2 + 1)
                    bwr = [b_pb[TCB[t]] for t in tcis]
                    def proj(e, m=m, s=s, tg0=tg0, tcis=tcis):
                        last = None
                        for fc in range(FC):
                            lw = wqb[s][:, fc * 128:(fc + 1) * 128]
                            for tci in tcis:
                                last = e.matmul(pb[TCB[tci]][:, :], lw,
                                                xh[fc][:, tg0 + tci * 512:tg0 + (tci + 1) * 512],
                                                start=(fc == 0), stop=False)
                        for tci in tcis:
                            last = e.matmul(pb[TCB[tci]][:, :],
                                            bias_r[0:1, m * 128:(m + 1) * 128],
                                            ones_r512[0:1, :],
                                            start=False, stop=True)
                        return last
                    p.op("pe", proj, reads=[b_wqb[s], b_bias, b_ones_r512] + b_xh,
                         writes=bwr)
                for tci in range(4):
                    t0 = g * 2048 + tci * 512
                    bank = TCB[tci]
                    p.op("act", lambda e, bank=bank, tci=tci: e.activation(
                        pj_sb[tci][:], pb[bank][:, :], AF.Copy),
                        reads=[b_pb[bank]], writes=[b_pj[tci]])
                    p.op("act", lambda e, bank=bank, tci=tci: e.activation(
                        sqv[tci % 2][:], pb[bank][:, :], AF.Square),
                        reads=[b_pb[bank]], writes=[b_sqv[tci % 2]])
                    p.dma("sp", out=dst[m * 128:(m + 1) * 128, t0:t0 + 512],
                          in_=pj_sb[tci][:], reads=[b_pj[tci]], writes=[b_dst],
                          key=f"st_{which}{tci}")
                    rb = 3 + tci
                    p.op("pe", lambda e, rb=rb, m=m, tci=tci: e.matmul(
                        pb[rb][0:1, :], ones_col[:, 0:1], sqv[tci % 2][:],
                        start=(m == 0), stop=(m == H - 1)),
                        reads=[b_ones_col, b_sqv[tci % 2]], writes=[b_pb[rb]])
            for tci in range(4):
                t0 = g * 2048 + tci * 512
                rb = 3 + tci
                p.op("act", lambda e, rb=rb, t0=t0, rowi=rowi: e.activation(
                    rowslc(rowi, t0, 512), pb[rb][0:1, :], AF.Copy),
                    reads=[b_pb[rb]], writes=[b_rowbuf])
        p.op("act", lambda e: e.activation(
            row(0), row(0), AF.Ln, scale=1.0 / DM, bias=EPS_RMS),
            reads=[b_rowbuf, b_constcols], writes=[b_rowbuf])
        p.op("act", lambda e: e.activation(
            row(0), row(0), AF.Exp, scale=-0.5),
            reads=[b_rowbuf], writes=[b_rowbuf])
        p.op("act", lambda e, rowi=rowi: e.activation(
            rstd_rb[0:1, rowi * T:(rowi + 1) * T], row(0), AF.Copy),
            reads=[b_rowbuf], writes=[b_rstd_rb])

    def v_sweep():
        for g in range(2):
            for dc in range(4):
                s = 0
                p.dma("pool", out=wvb[s][:].rearrange("p (fc q) -> p fc q", q=512),
                      in_=wvdc[dc].rearrange("fc p q -> p fc q"),
                      writes=[b_wvb[s]], key=f"wv{s}")
                for tt in range(16):
                    t0 = g * 2048 + tt * 128
                    bank = [0, 1, 2, 7][tt % 4]
                    def vproj(e, s=s, t0=t0, dc=dc, bank=bank):
                        last = None
                        for fc in range(FC):
                            last = e.matmul(pb[bank][:, :],
                                            xh[fc][:, t0:t0 + 128],
                                            wvb[s][:, fc * 512:(fc + 1) * 512],
                                            start=(fc == 0), stop=False)
                        return e.matmul(pb[bank][:, :], ones_row[0:1, :],
                                        bv_r[0:1, dc * 512:(dc + 1) * 512],
                                        start=False, stop=True)
                    p.op("pe", vproj,
                         reads=[b_wvb[s], b_ones_row, b_bv_r] + b_xh,
                         writes=[b_pb[bank]])
                    st = tt % 4
                    p.op("act", lambda e, bank=bank, st=st: e.activation(
                        pj_sb[st][:], pb[bank][:, :], AF.Copy),
                        reads=[b_pb[bank]], writes=[b_pj[st]])
                    for hh in range(4):
                        h = dc * 4 + hh
                        p.dma("sp",
                              out=vtm_s[h, t0:t0 + 128, :],
                              in_=pj_sb[st][:, hh * 128:(hh + 1) * 128],
                              reads=[b_pj[st]], writes=[b_vtm], key=f"st_v{st}")

    qk_sweep("q")
    qk_sweep("k")
    v_sweep()



    es1.close()
    p.barrier()
    es2 = ExitStack()
    sb2 = lambda n, s, d=F32: es2.enter_context(nc.sbuf_tensor(n, s, d))

    qpbh = [sb2(f"qpbh{i}", [128, T], BF16) for i in range(2)]
    kpbh = [sb2(f"kpbh{i}", [128, T], BF16) for i in range(2)]
    vtmh = [sb2(f"vtmh{i}", [128, NT * 132], BF16) for i in range(2)]
    b_qpbh = [B(f"qpbh{i}") for i in range(2)]
    b_kpbh = [B(f"kpbh{i}") for i in range(2)]
    b_vtmh = [B(f"vtmh{i}") for i in range(2)]
    for i in range(2):
        p.op("dve", lambda e, i=i: e.memset(vtmh[i][:], 1.0), writes=[b_vtmh[i]])
    qn = sb2("qn", [128, T], BF16); b_qn = B("qn")
    kn = sb2("kn", [128, T], BF16); b_kn = B("kn")
    qm = sb2("qm", [128, T], BF16); b_qm = B("qm")
    emr = [sb2(f"emr{i}", [128, NSEG * 128], BF16) for i in range(2)]
    b_emr = [B(f"emr{i}") for i in range(2)]
    e_all = sb2("e_all", [128, NT * NSEG * 128], BF16)
    b_e_allg = [B(f"e_all{gidx}") for gidx in range(4)]
    av_sb = sb2("av_sb", [128, 2048], BF16); b_av_sb = B("av_sb")
    chain = sb2("chain", [128, 1], F32); b_chain = B("chain")
    brow = sb2("brow", [1, 128], F32); b_brow = B("brow")
    bval = sb2("bval", [1, 1], F32); b_bval = B("bval")
    bvalb = sb2("bvalb", [1, 1], BF16); b_bvalb = B("bvalb")
    negb = sb2("negb", [128, 1], F32); b_negb = B("negb")
    zr16 = sb2("zr16", [128, 16], F32); b_zr16 = B("zr16")
    mcol = sb2("mcol", [128, 1], F32); b_mcol = B("mcol")

    for h in range(H):
        s = h % 2
        p.dma("act", out=qpbh[s][:], in_=qpb_s[h * 128:(h + 1) * 128, :],
              reads=[b_qpb], writes=[b_qpbh[s]])
        p.dma("act", out=kpbh[s][:], in_=kpb_s[h * 128:(h + 1) * 128, :],
              reads=[b_kpb], writes=[b_kpbh[s]])
        p.dma("act", out=vtmh[s][:].rearrange("p (tt q) -> p tt q", q=132)[:, :, 0:128],
              in_=vtm_s[h].rearrange("(tt p) q -> p tt q", p=128),
              reads=[b_vtm], writes=[b_vtmh[s]])
        for which in ("q", "k"):
            srcb = qpbh[s] if which == "q" else kpbh[s]
            b_srcb = b_qpbh[s] if which == "q" else b_kpbh[s]
            dstb = qn if which == "q" else kn
            b_dstb = b_qn if which == "q" else b_kn
            g_c = gq_c if which == "q" else gk_c
            b_g_c = b_gq_c if which == "q" else b_gk_c
            ri = ROW_Q if which == "q" else ROW_K
            for tc in range(8):
                t0 = tc * 512
                p.op("pe", lambda e, ri=ri, t0=t0: e.matmul(
                    pb[6][:, :], ones_row[0:1, :],
                    rstd_rb[0:1, ri * T + t0:ri * T + t0 + 512],
                    start=True, stop=True),
                    reads=[b_ones_row, b_rstd_rb], writes=[b_pb[6]])
                p.op("dve", lambda e, t0=t0, srcb=srcb, dstb=dstb, g_c=g_c, h=h:
                     e.scalar_tensor_tensor(
                         dstb[:, t0:t0 + 512], srcb[:, t0:t0 + 512],
                         g_c[:, h:h + 1], pb[6][:, :], ALU.mult, ALU.mult),
                     reads=[b_srcb, b_g_c, b_pb[6]], writes=[b_dstb])
        for tc in range(8):
            t0 = tc * 512
            p.op("pe", lambda e, t0=t0: e.matmul(
                pb[6][:, :], stencil[:, :], qn[:, t0:t0 + 512],
                start=True, stop=True),
                reads=[b_stencil, b_qn], writes=[b_pb[6]])
            p.op("act", lambda e, t0=t0: e.activation(
                qm[:, t0:t0 + 512], pb[6][:, :], AF.Copy),
                reads=[b_pb[6]], writes=[b_qm])
        p.op("dve", lambda e: e.memset(chain[:], -1e30), writes=[b_chain])
        for sc in (0, 8, 16, 24):
            p.op("pe", lambda e, sc=sc: e.matmul(
                pb[6][:, 0:128], kn[:, sc * 128:sc * 128 + 128],
                qm[:, sc * 128:sc * 128 + 128], start=True, stop=True),
                reads=[b_kn, b_qm], writes=[b_pb[6]])
            p.op("dve", lambda e: e.reduce_max(mcol[:, 0:1], pb[6][:, 0:128], axis=AX.X),
                 reads=[b_pb[6]], writes=[b_mcol])
            p.op("dve", lambda e: e.tensor_max(chain[:], chain[:], mcol[:]),
                 reads=[b_chain, b_mcol], writes=[b_chain])
        p.op("pe", lambda e: e.transpose(pb[6][0:1, 0:128], chain[:, 0:1], identf[:]),
             reads=[b_chain, b_identf], writes=[b_pb[6]])
        p.op("act", lambda e: e.activation(brow[:], pb[6][0:1, 0:128], AF.Copy),
             reads=[b_pb[6]], writes=[b_brow])
        p.op("dve", lambda e: e.reduce_max(bval[:, 0:1], brow[:], axis=AX.X),
             reads=[b_brow], writes=[b_bval])
        p.op("dve", lambda e: e.tensor_copy(bvalb[:], bval[:]),
             reads=[b_bval], writes=[b_bvalb])
        p.op("pe", lambda e: e.matmul(pb[6][:, 0:1], ones_row[0:1, :], bvalb[0:1, :],
                                      start=True, stop=True),
             reads=[b_ones_row, b_bvalb], writes=[b_pb[6]])
        p.op("act", lambda e: e.activation(negb[:], pb[6][:, 0:1], AF.Copy,
                                           scale=-1.0, bias=-5.0),
             reads=[b_pb[6], b_constcols], writes=[b_negb])

        for kc in range(NT):
            sbank = (kc % 3) * 2
            def smm(e, kc=kc, sbank=sbank):
                last = None
                pieces = []
                for (a2, l2) in circ_runs(((kc - 3) % NT) * 128, 896):
                    pieces.append((a2, l2))
                off = 0
                for (a2, l2) in pieces:
                    while l2 > 0:
                        bank = sbank + (1 if off >= 512 else 0)
                        boff = off - 512 if off >= 512 else off
                        take = min(l2, 512 - boff)
                        last = e.matmul(pb[bank][:, boff:boff + take],
                                        kn[:, kc * 128:kc * 128 + 128],
                                        qm[:, a2:a2 + take],
                                        start=True, stop=True)
                        off += take
                        a2 += take
                        l2 -= take
                q8 = ((kc + 8) % NT) * 128
                last = e.matmul(pb[sbank + 1][:, 384:512],
                                kn[:, kc * 128:kc * 128 + 128],
                                qm[:, q8:q8 + 128], start=True, stop=True)
                return last
            p.op("pe", smm, reads=[b_kn, b_qm],
                 writes=[b_pb[sbank], b_pb[sbank + 1]])
            kp = kc % 2
            for hf in range(2):
                p.op("act", lambda e, hf=hf, sbank=sbank, kp=kp: e.activation(
                    emr[kp][:, hf * 512:(hf + 1) * 512], pb[sbank + hf][:, :],
                    AF.Exp, bias=negb[:, 0:1]),
                    reads=[b_pb[sbank + hf], b_negb], writes=[b_emr[kp]])
            p.op("dve" if kp == 0 else "pool", lambda e, kc=kc, kp=kp: e.tensor_mul(
                e_all[:, kc * 1024:(kc + 1) * 1024], emr[kp][:], amask[:]),
                reads=[b_emr[kp], b_amask], writes=[b_e_allg[kc // 8]])
        for tt in range(NT):
            avbank = tt % 2
            def avz(e, tt=tt, avbank=avbank, s=s):
                last = None
                for j, dseg in enumerate(SEGS):
                    kc = (tt - dseg) % NT
                    lw = e_all[:, (kc * 8 + j) * 128:(kc * 8 + j) * 128 + 128]
                    last = e.matmul(pb[avbank][:, 0:129], lw,
                                    vtmh[s][:, kc * 132:kc * 132 + 129],
                                    start=(j == 0), stop=(j == NSEG - 1))
                return last
            gset = sorted({((tt - dseg) % NT) // 8 for dseg in SEGS})
            p.op("pe", avz, reads=[b_e_allg[gidx] for gidx in gset] + [b_vtmh[s]],
                 writes=[b_pb[avbank]])
            p.op("dve", lambda e, avbank=avbank: e.reciprocal(
                zr16[:, 0:1], pb[avbank][:, 128:129]),
                reads=[b_pb[avbank]], writes=[b_zr16])
            p.op("act", lambda e, tt=tt, avbank=avbank: e.activation(
                av_sb[:, (tt % 16) * 128:(tt % 16 + 1) * 128],
                pb[avbank][:, 0:128], AF.Copy, scale=zr16[:, 0:1]),
                reads=[b_pb[avbank], b_zr16], writes=[b_av_sb])
            p.dma("sp", out=avtm_s[tt * 128:(tt + 1) * 128, h * 128:(h + 1) * 128],
                  in_=av_sb[:, (tt % 16) * 128:(tt % 16 + 1) * 128],
                  reads=[b_av_sb], writes=[b_avtm], key="st_av")

    es2.close()
    p.barrier()

    es3 = ExitStack()
    sb3 = lambda n, s, d=F32: es3.enter_context(nc.sbuf_tensor(n, s, d))
    avf = [sb3(f"avf{dc}", [128, T], BF16) for dc in range(FC)]
    b_avf = [B(f"avf{dc}") for dc in range(FC)]
    for dc in range(FC):
        p.dma_transpose("sp", out=avf[dc][:],
                        in_=avtm_s[:, dc * 128:(dc + 1) * 128],
                        reads=[b_avtm], writes=[b_avf[dc]], key=f"avf{dc % 2}")
    wob = [sb3(f"wob{i}", [128, FC * 128], BF16) for i in range(2)]
    b_wob = [B(f"wob{i}") for i in range(2)]
    xfb = [sb3(f"xfb{i}", [128, 512], F32) for i in range(2)]
    b_xfb = [B(f"xfb{i}") for i in range(2)]
    r1c = sb3("r1c", [128, 512], F32); b_r1c = B("r1c")
    r1bc = sb3("r1bc", [128, 512], BF16); b_r1bc = B("r1bc")

    for m in range(H):
        s = m % 2
        p.dma("pool", out=wob[s][:].rearrange("p (fc q) -> p fc q", q=128),
              in_=wom[m].rearrange("fc p q -> p fc q"),
              writes=[b_wob[s]], key=f"wob{s}")
        for tg in range(2):
            tg0 = tg * 2048
            bwr = [b_pb[j] for j in range(4)]
            def womm(e, s=s, tg0=tg0):
                last = None
                for dc in range(FC):
                    lw = wob[s][:, dc * 128:(dc + 1) * 128]
                    for tq in range(4):
                        last = e.matmul(pb[tq][:, :], lw,
                                        avf[dc][:, tg0 + tq * 512:tg0 + (tq + 1) * 512],
                                        start=(dc == 0), stop=(dc == FC - 1))
                return last
            p.op("pe", womm, reads=[b_wob[s]] + b_avf, writes=bwr)
            for tq in range(4):
                t0 = tg0 + tq * 512
                xs = tq % 2
                p.dma("act", out=xfb[xs][:], in_=xT[m * 128:(m + 1) * 128, t0:t0 + 512],
                      writes=[b_xfb[xs]], key=f"xfb{xs}")
                p.op("dve", lambda e, m=m, tq=tq, xs=xs: e.scalar_tensor_tensor(
                    r1c[:], pb[tq][:, :], bo_c[:, m:m + 1], xfb[xs][:],
                    ALU.add, ALU.add),
                    reads=[b_pb[tq], b_bo_c, b_xfb[xs]], writes=[b_r1c])
                p.dma("sp", out=r1_s[m * 128:(m + 1) * 128, t0:t0 + 512],
                      in_=r1c[:], reads=[b_r1c], writes=[b_r1], key="st_r1")
                p.op("act", lambda e: e.activation(r1bc[:], r1c[:], AF.Copy),
                     reads=[b_r1c], writes=[b_r1bc])
                p.dma("pool", out=r1b_s[m * 128:(m + 1) * 128, t0:t0 + 512],
                      in_=r1bc[:], reads=[b_r1bc], writes=[b_r1b], key="st_r1b")
    es3.close()
    p.barrier()

    def ln_stats(src_b16, b_src, rowbuf2, b_rowbuf2, scope_sb, tag):
        lb = [scope_sb(f"lnb{tag}{i}", [128, 512], BF16) for i in range(2)]
        b_lb = [B(f"lnb{tag}{i}") for i in range(2)]
        sqb = scope_sb(f"sqb{tag}", [128, 512], BF16)
        b_sqb = B(f"sqb{tag}")
        for tc in range(8):
            t0 = tc * 512
            for m in range(H):
                s = m % 2
                p.dma("act", out=lb[s][:], in_=src_b16[m * 128:(m + 1) * 128, t0:t0 + 512],
                      reads=[b_src], writes=[b_lb[s]], key=f"lnb{tag}{s}")
                p.op("act", lambda e, s=s: e.activation(sqb[:], lb[s][:], AF.Square),
                     reads=[b_lb[s]], writes=[b_sqb])
                p.op("pe", lambda e, s=s, m=m: e.matmul(
                    pb[2][0:1, :], ones_col[:, 0:1], lb[s][:],
                    start=(m == 0), stop=(m == H - 1)),
                    reads=[b_ones_col, b_lb[s]], writes=[b_pb[2]])
                p.op("pe", lambda e, m=m: e.matmul(
                    pb[3][0:1, :], ones_col[:, 0:1], sqb[:],
                    start=(m == 0), stop=(m == H - 1)),
                    reads=[b_ones_col, b_sqb], writes=[b_pb[3]])
            p.op("act", lambda e, t0=t0: e.activation(
                rowbuf2[0:1, t0:t0 + 512], pb[2][0:1, :], AF.Copy, scale=1.0 / DM),
                reads=[b_pb[2]], writes=[b_rowbuf2])
            p.op("act", lambda e, t0=t0: e.activation(
                rowbuf2[0:1, T + t0:T + t0 + 512], pb[3][0:1, :], AF.Copy, scale=1.0 / DM),
                reads=[b_pb[3]], writes=[b_rowbuf2])
        MU = rowbuf2[0:1, 0:T]
        MSQ = rowbuf2[0:1, T:2 * T]
        MU2 = rowbuf2[0:1, 2 * T:3 * T]
        p.op("act", lambda e: e.activation(MU2, MU, AF.Square),
             reads=[b_rowbuf2], writes=[b_rowbuf2])
        p.op("dve", lambda e: e.tensor_sub(MSQ, MSQ, MU2),
             reads=[b_rowbuf2], writes=[b_rowbuf2])
        p.op("act", lambda e: e.activation(MSQ, MSQ, AF.Ln, bias=EPS_LN),
             reads=[b_rowbuf2, b_constcols], writes=[b_rowbuf2])
        p.op("act", lambda e: e.activation(MSQ, MSQ, AF.Exp, scale=-0.5),
             reads=[b_rowbuf2], writes=[b_rowbuf2])
        p.op("dve", lambda e: e.tensor_mul(MU2, MU, MSQ),
             reads=[b_rowbuf2], writes=[b_rowbuf2])
        return MSQ, MU2

    def ln_apply(src_f32, b_src, rstd_row, b2_row, b_rows, dst, b_dst,
                 dst_b16, b_dst_b16, scope_sb, tag, b16_tc_bufs=None):
        rb16 = scope_sb(f"lnr{tag}", [1, 2 * T], BF16)
        b_rb16 = B(f"lnr{tag}")
        p.op("act", lambda e: e.activation(rb16[0:1, 0:T], rstd_row, AF.Copy),
             reads=[b_rows], writes=[b_rb16])
        p.op("act", lambda e: e.activation(rb16[0:1, T:2 * T], b2_row, AF.Copy),
             reads=[b_rows], writes=[b_rb16])
        lc = [scope_sb(f"lnc{tag}{i}", [128, 512], F32) for i in range(2)]
        b_lc = [B(f"lnc{tag}{i}") for i in range(2)]
        ot = scope_sb(f"lno{tag}", [128, 512], F32)
        b_ot = B(f"lno{tag}")
        ob = scope_sb(f"lnob{tag}", [128, 512], BF16)
        b_ob = B(f"lnob{tag}")
        for tc in range(8):
            t0 = tc * 512
            p.op("pe", lambda e, t0=t0: e.matmul(
                pb[2][:, :], ones_row[0:1, :], rb16[0:1, t0:t0 + 512],
                start=True, stop=True),
                reads=[b_ones_row, b_rb16], writes=[b_pb[2]])
            p.op("pe", lambda e, t0=t0: e.matmul(
                pb[3][:, :], ones_row[0:1, :], rb16[0:1, T + t0:T + t0 + 512],
                start=True, stop=True),
                reads=[b_ones_row, b_rb16], writes=[b_pb[3]])
            for m in range(H):
                s = m % 2
                p.dma("act", out=lc[s][:], in_=src_f32[m * 128:(m + 1) * 128, t0:t0 + 512],
                      reads=[b_src], writes=[b_lc[s]], key=f"lnc{tag}{s}")
                p.op("dve", lambda e, s=s: e.tensor_mul(ot[:], lc[s][:], pb[2][:, :]),
                     reads=[b_lc[s], b_pb[2]], writes=[b_ot])
                p.op("dve", lambda e: e.tensor_sub(ot[:], ot[:], pb[3][:, :]),
                     reads=[b_ot, b_pb[3]], writes=[b_ot])
                p.dma("sp", out=dst[m * 128:(m + 1) * 128, t0:t0 + 512],
                      in_=ot[:], reads=[b_ot], writes=[b_dst], key=f"lnst{tag}")
                if dst_b16 is not None:
                    p.op("act", lambda e: e.activation(ob[:], ot[:], AF.Copy),
                         reads=[b_ot], writes=[b_ob])
                    wtgt = b16_tc_bufs[tc] if b16_tc_bufs is not None else b_dst_b16
                    p.dma("pool", out=dst_b16[m * 128:(m + 1) * 128, t0:t0 + 512],
                          in_=ob[:], reads=[b_ob], writes=[wtgt],
                          key=f"lnstb{tag}")

    es3b = ExitStack()
    sb3b = lambda n, s, d=F32: es3b.enter_context(nc.sbuf_tensor(n, s, d))
    rowbuf2 = sb3b("rowbuf2", [1, 3 * T], F32)
    b_rowbuf2 = B("rowbuf2")
    rstd1, b21 = ln_stats(r1b_s, b_r1b, rowbuf2, b_rowbuf2, sb3b, "1")
    ln_apply(r1_s, b_r1, rstd1, b21, b_rowbuf2, x1_s, b_x1, x1b_s, None,
             sb3b, "1", b16_tc_bufs=b_x1btc)
    es3b.close()
    p.barrier()

    es4 = ExitStack()
    sb4 = lambda n, s, d=F32: es4.enter_context(nc.sbuf_tensor(n, s, d))
    rw_sb = sb4("rw_sb", [128, FC * E], BF16)
    b_rw = B("rw_sb")
    p.dma("pool", out=rw_sb[:].rearrange("p (fc e) -> p fc e", e=E),
          in_=rwT.rearrange("(fc p) e -> p fc e", p=128),
          writes=[b_rw])
    x4b = [sb4(f"x4b{i}", [128, 512], BF16) for i in range(2)]
    b_x4b = [B(f"x4b{i}") for i in range(2)]
    s_f = sb4("s_f", [16, T], F32); b_s_f = B("s_f")
    id16 = sb4("id16", [16, 16], F32); b_id16 = B("id16")
    id16i = sb4("id16i", [16, 16], F32); b_id16i = B("id16i")
    p.op("pool", lambda e: e.iota(id16i[:], pattern=[[1, 16]], base=0,
                                  channel_multiplier=-1,
                                  allow_small_or_imprecise_dtypes=True),
         writes=[b_id16i])
    p.op("dve", lambda e: e.tensor_single_scalar(id16[:], id16i[:], 0.0, ALU.is_equal),
         reads=[b_id16i], writes=[b_id16])
    wT_sb = sb4("wT_sb", [16, T], BF16); b_wT = B("wT_sb")
    s_tm = sb4("s_tm", [128, E], F32); b_s_tm = B("s_tm")
    h16r = sb4("h16r", [128, E], F32); b_h16r = B("h16r")
    t8r = sb4("t8r", [128, 8], F32); b_t8r = B("t8r")
    t8c = sb4("t8c", [128, 8], F32); b_t8c = B("t8c")
    mask16r = sb4("mask16r", [128, E], F32); b_mask16r = B("mask16r")
    wur = sb4("wur", [128, E], F32); b_wur = B("wur")
    wsum = sb4("wsum", [128, 2], F32); b_wsum = B("wsum")
    w_tmr = sb4("w_tmr", [128, E], F32); b_w_tmr = B("w_tmr")
    for tc in range(8):
        t0 = tc * 512
        xs = tc % 2
        for fc in range(FC):
            p.dma("act", out=x4b[xs][:], in_=x1b_s[fc * 128:(fc + 1) * 128, t0:t0 + 512],
                  reads=[b_x1btc[tc]], writes=[b_x4b[xs]], key=f"x4b{xs}")
            p.op("pe", lambda e, fc=fc, xs=xs: e.matmul(
                pb[2][0:16, :], rw_sb[:, fc * E:(fc + 1) * E], x4b[xs][:],
                start=(fc == 0), stop=(fc == FC - 1)),
                reads=[b_rw, b_x4b[xs]], writes=[b_pb[2]])
            xs = (xs + 1) % 2
        p.op("act", lambda e, t0=t0: e.activation(
            s_f[0:16, t0:t0 + 512], pb[2][0:16, :], AF.Sigmoid),
            reads=[b_pb[2]], writes=[b_s_f])
    for tt in range(NT):
        t0 = tt * 128
        p.op("pe", lambda e, t0=t0: e.transpose(
            pb[3][:, 0:16], s_f[0:16, t0:t0 + 128], id16[:, :]),
            reads=[b_s_f, b_id16], writes=[b_pb[3]])
        p.op("dve", lambda e: e.tensor_copy(s_tm[:], pb[3][:, 0:16]),
             reads=[b_pb[3]], writes=[b_s_tm])
        p.op("dve", lambda e: e.tensor_add(h16r[:], s_tm[:], eb_b[:]),
             reads=[b_s_tm, b_eb_b], writes=[b_h16r])
        p.op("dve", lambda e: e.max(t8r[:], h16r[:]),
             reads=[b_h16r], writes=[b_t8r])
        p.op("act", lambda e: e.copy(t8c[:], t8r[:]),
             reads=[b_t8r], writes=[b_t8c])
        p.op("dve", lambda e: e.tensor_single_scalar(
            mask16r[:], h16r[:], t8c[:, 3:4], ALU.is_ge),
            reads=[b_h16r, b_t8c], writes=[b_mask16r])
        p.op("dve", lambda e: e.tensor_mul(wur[:], mask16r[:], s_tm[:]),
             reads=[b_mask16r, b_s_tm], writes=[b_wur])
        p.op("dve", lambda e: e.reduce_sum(wsum[:, 0:1], wur[:], axis=AX.X),
             reads=[b_wur], writes=[b_wsum])
        p.op("dve", lambda e: e.tensor_single_scalar(
            wsum[:, 1:2], wsum[:, 0:1], 1e-20, ALU.add),
            reads=[b_wsum], writes=[b_wsum])
        p.op("dve", lambda e: e.reciprocal(wsum[:, 0:1], wsum[:, 1:2]),
             reads=[b_wsum], writes=[b_wsum])
        p.op("dve", lambda e: e.tensor_scalar_mul(w_tmr[:], wur[:], wsum[:, 0:1]),
             reads=[b_wur, b_wsum], writes=[b_w_tmr])
        p.op("pe", lambda e: e.transpose(pb[3][0:16, 0:128], w_tmr[:, 0:16], identf[:]),
             reads=[b_w_tmr, b_identf], writes=[b_pb[3]])
        p.op("act", lambda e, t0=t0: e.activation(
            wT_sb[0:16, t0:t0 + 128], pb[3][0:16, 0:128], AF.Copy),
            reads=[b_pb[3]], writes=[b_wT])

    gub = sb4("gub0", [128, FC * I], BF16)
    upb = sb4("upb0", [128, FC * I], BF16)
    b_gub, b_upb = B("gub0"), B("upb0")
    dchalf = sb4("dchalf", [128, 6 * 128], BF16)
    b_dchalf = B("dchalf")
    dck = [0]
    wrow = sb4("wrow0", [1, 2048], BF16)
    b_wrow = B("wrow0")
    x1w = [sb4(f"x1w{c}", [128, 2048], BF16) for c in range(FC)]
    b_x1w = [B(f"x1w{c}") for c in range(FC)]
    sg_sb = sb4("sg_sb", [128, IC * 2048], BF16); b_sg = B("sg_sb")
    wbc_sb = sb4("wbc_sb", [128, 2048], BF16); b_wbc = B("wbc_sb")
    mid_sb = sb4("mid_sb", [128, 2048], BF16); b_mid_sb = B("mid_sb")

    for tcg in range(2):
        tg0 = tcg * 2048
        for c in range(FC):
            p.dma("act", out=x1w[c][:], in_=x1b_s[c * 128:(c + 1) * 128, tg0:tg0 + 2048],
                  reads=[b_x1btc[tcg * 4], b_x1btc[tcg * 4 + 1], b_x1btc[tcg * 4 + 2],
                         b_x1btc[tcg * 4 + 3]],
                  writes=[b_x1w[c]], key=f"x1w{c % 2}")
        for e_i in range(17):
            for _ in range(8):
                if dck[0] >= H * 17:
                    break
                m, j = dck[0] // 17, dck[0] % 17
                dck[0] += 1
                p.dma("pool", out=dchalf[:],
                      in_=dallm[m][:, j * 6:j * 6 + 6, :]
                      .rearrange("p kc q -> p (kc q)"),
                      writes=[b_dchalf], key="dcast_ld")
                p.dma("pool", out=dbf_s[m][:, j * 768:j * 768 + 768],
                      in_=dchalf[:], reads=[b_dchalf], writes=[b_dbf],
                      key="dcast_st")
            src = sgu if e_i == 16 else gup[e_i]
            p.dma("pool", out=gub[:].rearrange("p (fc g) -> p fc g", g=I),
                  in_=src[:, :, 0:I].rearrange("fc p g -> p fc g"),
                  writes=[b_gub], key="gub")
            if e_i < 16:
                p.dma("pool", out=wrow[:], in_=wT_sb[e_i:e_i + 1, tg0:tg0 + 2048],
                      reads=[b_wT], writes=[b_wrow], key="wrow0")
                for tq in range(4):
                    p.op("pe", lambda e, tq=tq: e.matmul(
                        pb[4 + tq][:, :], ones_row[0:1, :],
                        wrow[0:1, tq * 512:(tq + 1) * 512],
                        start=True, stop=True),
                        reads=[b_ones_row, b_wrow], writes=[b_pb[4 + tq]])
                    p.op("act", lambda e, tq=tq: e.activation(
                        wbc_sb[:, tq * 512:(tq + 1) * 512], pb[4 + tq][:, :], AF.Copy),
                        reads=[b_pb[4 + tq]], writes=[b_wbc])
            p.dma("pool", out=upb[:].rearrange("p (fc g) -> p fc g", g=I),
                  in_=src[:, :, I:2 * I].rearrange("fc p g -> p fc g"),
                  writes=[b_upb], key="upb")
            for ic in range(IC):
                bs = (ic % 2) * 4
                bwr = [b_pb[bs + j] for j in range(4)]
                def gmm(e, ic=ic, bs=bs):
                    last = None
                    for fc in range(FC):
                        lw = gub[:, fc * I + ic * 128:fc * I + (ic + 1) * 128]
                        for tq in range(4):
                            last = e.matmul(pb[bs + tq][:, :], lw,
                                            x1w[fc][:, tq * 512:(tq + 1) * 512],
                                            start=(fc == 0), stop=(fc == FC - 1))
                    return last
                p.op("pe", gmm, reads=[b_gub] + b_x1w, writes=bwr)
                for tq in range(4):
                    p.op("act", lambda e, ic=ic, bs=bs, tq=tq: e.activation(
                        sg_sb[:, ic * 2048 + tq * 512:ic * 2048 + (tq + 1) * 512],
                        pb[bs + tq][:, :], AF.Silu),
                        reads=[b_pb[bs + tq]], writes=[b_sg])
            if e_i < 16:
                for ic in range(IC):
                    p.op("dve", lambda e, ic=ic: e.tensor_mul(
                        sg_sb[:, ic * 2048:(ic + 1) * 2048],
                        sg_sb[:, ic * 2048:(ic + 1) * 2048], wbc_sb[:]),
                        reads=[b_sg, b_wbc], writes=[b_sg])
            for ic in range(IC):
                bs = (ic % 2) * 4
                bwr = [b_pb[bs + j] for j in range(4)]
                def umm(e, ic=ic, bs=bs):
                    last = None
                    for fc in range(FC):
                        lw = upb[:, fc * I + ic * 128:fc * I + (ic + 1) * 128]
                        for tq in range(4):
                            last = e.matmul(pb[bs + tq][:, :], lw,
                                            x1w[fc][:, tq * 512:(tq + 1) * 512],
                                            start=(fc == 0), stop=(fc == FC - 1))
                    return last
                p.op("pe", umm, reads=[b_upb] + b_x1w, writes=bwr)
                for tq in range(4):
                    p.op("dve", lambda e, ic=ic, bs=bs, tq=tq: e.tensor_mul(
                        mid_sb[:, tq * 512:(tq + 1) * 512],
                        sg_sb[:, ic * 2048 + tq * 512:ic * 2048 + (tq + 1) * 512],
                        pb[bs + tq][:, :]),
                        reads=[b_sg, b_pb[bs + tq]], writes=[b_mid_sb])
                p.dma("sp", out=mid_s[e_i, ic * 128:(ic + 1) * 128, tg0:tg0 + 2048],
                      in_=mid_sb[:], reads=[b_mid_sb], writes=[b_mid], key="st_mid")
    es4.close()
    p.barrier()

    es4c = ExitStack()
    sb4c = lambda n, s, d=F32: es4c.enter_context(nc.sbuf_tensor(n, s, d))
    midwA = sb4c("midwA", [128, 48 * 512], BF16); b_midwA = B("midwA")
    midwB = sb4c("midwB", [128, 54 * 512], BF16); b_midwB = B("midwB")
    dwb = [sb4c(f"dwb{i}", [128, 102 * 128], BF16) for i in range(2)]
    b_dwb = [B(f"dwb{i}") for i in range(2)]
    x1fb = [sb4c(f"x1fb{i}", [128, 512], F32) for i in range(2)]
    b_x1fb = [B(f"x1fb{i}") for i in range(2)]
    r2c = [sb4c(f"r2c{i}", [128, 512], F32) for i in range(2)]
    b_r2c = [B(f"r2c{i}") for i in range(2)]
    r2bc = [sb4c(f"r2bc{i}", [128, 512], BF16) for i in range(2)]
    b_r2bc = [B(f"r2bc{i}") for i in range(2)]

    for tc in range(8):
        t0 = tc * 512
        for e_i in range(17):
            half = 0 if e_i < 8 else 1
            dst = midwA if half == 0 else midwB
            bdst = b_midwA if half == 0 else b_midwB
            base = e_i * IC if half == 0 else (e_i - 8) * IC
            p.dma("sp", out=dst[:, base * 512:(base + IC) * 512]
                  .rearrange("p (ic t) -> p ic t", t=512),
                  in_=mid_s[e_i, :, t0:t0 + 512].rearrange("(ic p) t -> p ic t", p=128),
                  reads=[b_mid], writes=[bdst], key="midw" + ("A" if half == 0 else "B"))
        for m in range(H):
            s = m % 2
            p.dma("sp", out=dwb[s][:], in_=dbf_s[m],
                  reads=[b_dbf], writes=[b_dwb[s]], key=f"dwb{s}")
            bank = m % 4
            def dmmA(e, s=s, bank=bank):
                last = None
                for eic in range(48):
                    last = e.matmul(pb[bank][:, :],
                                    dwb[s][:, eic * 128:(eic + 1) * 128],
                                    midwA[:, eic * 512:(eic + 1) * 512],
                                    start=(eic == 0), stop=False)
                return last
            def dmmB(e, s=s, bank=bank):
                last = None
                for eic in range(54):
                    last = e.matmul(pb[bank][:, :],
                                    dwb[s][:, (48 + eic) * 128:(48 + eic + 1) * 128],
                                    midwB[:, eic * 512:(eic + 1) * 512],
                                    start=False, stop=(eic == 53))
                return last
            p.op("pe", dmmA, reads=[b_dwb[s], b_midwA], writes=[b_pb[bank]])
            p.op("pe", dmmB, reads=[b_dwb[s], b_midwB], writes=[b_pb[bank]])
            xs = m % 2
            p.dma("pool", out=x1fb[xs][:], in_=x1_s[m * 128:(m + 1) * 128, t0:t0 + 512],
                  reads=[b_x1], writes=[b_x1fb[xs]], key=f"x1fb{xs}")
            p.op("dve", lambda e, bank=bank, xs=xs: e.tensor_add(
                r2c[xs][:], pb[bank][:, :], x1fb[xs][:]),
                reads=[b_pb[bank], b_x1fb[xs]], writes=[b_r2c[xs]])
            p.dma("pool", out=r2_s[m * 128:(m + 1) * 128, t0:t0 + 512],
                  in_=r2c[xs][:], reads=[b_r2c[xs]], writes=[b_r2], key=f"st_r2{xs}")
            p.op("act", lambda e, xs=xs: e.activation(r2bc[xs][:], r2c[xs][:], AF.Copy),
                 reads=[b_r2c[xs]], writes=[b_r2bc[xs]])
            p.dma("pool", out=r2b_s[m * 128:(m + 1) * 128, t0:t0 + 512],
                  in_=r2bc[xs][:], reads=[b_r2bc[xs]], writes=[b_r2b], key=f"st_r2b{xs}")
    es4c.close()
    p.barrier()

    es5 = ExitStack()
    sb5 = lambda n, s, d=F32: es5.enter_context(nc.sbuf_tensor(n, s, d))
    rowbuf3 = sb5("rowbuf3", [1, 3 * T], F32)
    b_rowbuf3 = B("rowbuf3")
    rstd2, b22 = ln_stats(r2b_s, b_r2b, rowbuf3, b_rowbuf3, sb5, "2")
    ln_apply(r2_s, b_r2, rstd2, b22, b_rowbuf3, out, b_out, None, None,
             sb5, "2")
    es5.close()
    p.barrier()
    out_names = ["out"]
    if "scratch" in dev_taps:
        out_names += ["qpb_s", "kpb_s", "vtm_s", "avtm_s", "x1_s", "mid_s"]
    if "rows" in dev_taps:
        rows_o = DT("rows_o", [1, 2 * T], F32, kind="ExternalOutput").ap()
        b_rows_o = B("rows_o")
        p.dma("sp", out=rows_o[:, :], in_=rowbuf[:], reads=[b_rowbuf],
              writes=[b_rows_o], key="st_rows")
        out_names.append("rows_o")
    p.finish("pool")
    p.emit()
    return nc, es, out_names


def _host_layout(inputs):
    x = inputs["L0_attn_x"]
    wq = np.asarray(inputs["L0_attn_wq"])
    wk = np.asarray(inputs["L0_attn_wk"])
    wv = np.asarray(inputs["L0_attn_wv"])
    wo = np.asarray(inputs["L0_attn_wo"])
    gup = np.asarray(inputs["L0_ffn_gate_up_proj"])
    down = np.asarray(inputs["L0_ffn_down_proj"])
    sgw = np.asarray(inputs["L0_ffn_shared_gate_w"])
    suw = np.asarray(inputs["L0_ffn_shared_up_w"])
    sdw = np.asarray(inputs["L0_ffn_shared_down_w"])

    def mblock(w):
        wT = np.ascontiguousarray(w.T)
        return np.ascontiguousarray(
            wT.reshape(FC, 128, H, 128).transpose(2, 0, 1, 3))

    wqm = mblock(wq); wkm = mblock(wk); wom = mblock(wo)
    wvT = np.ascontiguousarray(wv.T)
    wvdc = np.ascontiguousarray(
        wvT.reshape(FC, 128, 4, 512).transpose(2, 0, 1, 3))

    def cols(v):
        return np.ascontiguousarray(v.reshape(FC, 128).T)

    gupT = np.ascontiguousarray(
        gup.transpose(0, 2, 1).reshape(E, FC, 128, 2 * I))
    sguT = np.concatenate([sgw.T, suw.T], axis=1)
    sguT = np.ascontiguousarray(sguT.reshape(FC, 128, 2 * I))

    dT = np.concatenate([down[e].T for e in range(E)] + [sdw.T], axis=0)
    dallm = np.ascontiguousarray(
        dT.reshape(102, 128, H, 128).transpose(2, 1, 0, 3))

    base = {
        "wqm": wqm, "wkm": wkm, "wvdc": wvdc, "wom": wom,
        "bqr": np.asarray(inputs["L0_attn_bq"])[None, :],
        "bkr": np.asarray(inputs["L0_attn_bk"])[None, :],
        "boc": cols(np.asarray(inputs["L0_attn_bo"])),
        "gqc": cols(np.asarray(inputs["L0_attn_gq"])),
        "gkc": cols(np.asarray(inputs["L0_attn_gk"])),
        "bvr": np.asarray(inputs["L0_attn_bv"])[None, :],
        "wmix": np.asarray(inputs["L0_attn_w_mix"]),
        "rwT": np.ascontiguousarray(np.asarray(inputs["L0_ffn_router_w"]).T),
        "ebr": np.asarray(inputs["L0_ffn_e_bias"])[None, :],
        "gup": gupT, "sgu": sguT, "dallm": dallm,
    }
    in_maps = []
    for b in range(8):
        m = dict(base)
        m["xT"] = np.ascontiguousarray(np.asarray(x[b]).T)
        in_maps.append(m)
    return in_maps


def kernel(**inputs):
    in_maps = _host_layout(inputs)
    nc, es, out_names = build_program()
    try:
        res = run_bass_kernel_spmd(nc, in_maps, core_ids=list(range(8)))
    finally:
        es.close()
    outs = [np.ascontiguousarray(np.asarray(r["out"], np.float32).T)
            for r in res.results]
    return np.stack(outs, axis=0)
